# version 18
# baseline (speedup 1.0000x reference)
import numpy as np
import ml_dtypes
from contextlib import ExitStack

import concourse.bass as bass
import concourse.mybir as mybir
from concourse.bass_utils import run_bass_kernel_spmd

F32 = mybir.dt.float32
BF16 = mybir.dt.bfloat16
AF = mybir.ActivationFunctionType
ALU = mybir.AluOpType
AX = mybir.AxisListType

D = 1024
DH = 4096
EPS_LN = 1e-5
NCORES = 8
ENGS = ("sync", "gpsimd", "vector", "scalar", "tensor")
SELF = {"vector": "sv", "scalar": "ss", "gpsimd": "sp"}

COMPUTE = {
    "tensor_copy", "tensor_tensor", "tensor_scalar", "tensor_single_scalar",
    "scalar_tensor_tensor", "tensor_reduce", "reduce_sum", "reduce_max",
    "reciprocal", "activation", "memset", "memzero", "iota", "select",
    "bn_stats", "bn_aggr", "tensor_add", "tensor_sub", "tensor_mul",
    "tensor_max", "tensor_scalar_add", "tensor_scalar_sub",
    "tensor_scalar_mul", "tensor_scalar_max", "tensor_scalar_min",
    "square", "sqrt", "mul", "add", "copy", "sign",
    "tensor_relu", "tensor_tensor_scan", "affine_select", "stream_shuffle",
    "reciprocal_approx_fast", "reciprocal_approx_accurate",
}


class _Proxy:

    def __init__(self, eng, self_sem):
        self._e = eng
        self._sem = self_sem
        self.n = 0
        self._last = None

    def _pre(self):
        if self._last is not None:
            self._e.wait_ge(*self._last)

    def wait_ge(self, sem, val):
        return self._e.wait_ge(sem, val)

    def nop(self):
        return self._e.nop()

    def dma_start(self, *a, **k):
        self._pre()
        return self._e.dma_start(*a, **k)

    def dma_start_transpose(self, *a, **k):
        self._pre()
        return self._e.dma_start_transpose(*a, **k)

    def collective_compute(self, *a, **k):
        self._pre()
        return self._e.collective_compute(*a, **k)

    def __getattr__(self, name):
        attr = getattr(self._e, name)
        if name in COMPUTE and callable(attr) and self._sem is not None:
            def wrapped(*a, **k):
                self._pre()
                instr = attr(*a, **k)
                self.n += 1
                instr.then_inc(self._sem, 1)
                self._last = (self._sem, self.n)
                return instr
            return wrapped
        return attr


class Prog:

    def __init__(self, nc):
        self.nc = nc
        self.ops = {e: [] for e in ENGS}
        self.sem_names = ["sv", "ss", "sp"]
        self.cnt = {"sv": 0, "ss": 0, "sp": 0}

    def sem(self, *names):
        for n in names:
            self.sem_names.append(n)
            self.cnt[n] = 0

    def op(self, eng, fn, waits=(), inc=None, k=1, n=None, no_pre=False):
        if eng in SELF:
            assert n is not None, "declare instruction count"
            self.cnt[SELF[eng]] += n
        if inc is not None:
            self.cnt[inc] += k
        self.ops[eng].append((fn, tuple(waits), inc, k, no_pre))

    def n(self, name):
        return self.cnt[name]

    def barrier(self):
        snap = [(s, v) for s, v in self.cnt.items() if v > 0]
        for eng in ENGS:
            self.ops[eng].append((lambda e, S: None, tuple(snap), None, 1, False))

    def emit(self, es):
        nc = self.nc
        sems = {nm: es.enter_context(nc.semaphore(nm)) for nm in self.sem_names}
        block = es.enter_context(nc.Block())

        def body(eng_obj, eng_name):
            proxy = _Proxy(eng_obj, sems.get(SELF.get(eng_name)))
            for fn, waits, inc, k, no_pre in self.ops[eng_name]:
                for s, v in waits:
                    if v > 0:
                        eng_obj.wait_ge(sems[s], v)
                if no_pre:
                    proxy._last = None
                instr = fn(proxy, sems)
                if inc is not None:
                    assert instr is not None
                    instr.then_inc(sems[inc], k)
            if eng_name in SELF:
                assert proxy.n == self.cnt[SELF[eng_name]], \
                    f"{eng_name}: emitted {proxy.n} != scheduled {self.cnt[SELF[eng_name]]}"

        block.sync(lambda e: body(e, "sync"))
        block.gpsimd(lambda e: body(e, "gpsimd"))
        block.vector(lambda e: body(e, "vector"))
        block.scalar(lambda e: body(e, "scalar"))
        block.tensor(lambda e: body(e, "tensor"))
        return sems


def sb_ap(t, offset, dims):
    if isinstance(t, View):
        offset = offset + t.base
        t = t.t
    free = t.shape[1]
    return bass.AP(t, offset, [[free, 128]] + [list(d) for d in dims])


def g_ap(t, part_base, part_count, col_off, dims):
    if isinstance(t, View):
        base, tt = t.base, t.t
    else:
        base, tt = 0, t
    W = tt.shape[1]
    return bass.AP(tt, part_base * W + base + col_off,
                   [[W, part_count]] + [list(d) for d in dims])


class Arena:

    def __init__(self, nc, bf16_cols, f32_cols):
        self.t = {}
        self.t[BF16] = nc.alloc_sbuf_tensor("arena_bf", [128, bf16_cols], BF16)
        self.t[F32] = nc.alloc_sbuf_tensor("arena_f32", [128, f32_cols], F32)
        self.off = {BF16: 0, F32: 0}
        self.peak = {BF16: 0, F32: 0}

    def sb(self, name, cols, dt):
        o = (self.off[dt] + 15) // 16 * 16
        v = View(self.t[dt], o, cols)
        self.off[dt] = o + cols
        self.peak[dt] = max(self.peak[dt], self.off[dt])
        assert self.off[dt] <= self.t[dt].shape[1], \
            f"arena {dt} overflow at {name}: {self.off[dt]} > {self.t[dt].shape[1]}"
        return v

    def reset(self):
        self.off = {BF16: 0, F32: 0}


class View:

    def __init__(self, t, base, width):
        self.t = t
        self.base = base
        self.shape = (t.shape[0], width)

    def __getitem__(self, idx):
        ps, cs = idx
        a = self.base + (cs.start if cs.start is not None else 0)
        b = self.base + (cs.stop if cs.stop is not None else self.shape[1])
        return self.t[ps, a:b]

    def ap(self):
        return self.t[:, self.base:self.base + self.shape[1]]



def emit_ln_stats(p, src_ap_fn, st, cols, waits_v, waits_s, sq_out_fn):
    c0, c1, c2, c3, c4, c5, c6 = range(cols, cols + 7)

    def f_red(v, S):
        return v.reduce_sum(out=st[:, c0:c0 + 1], in_=src_ap_fn(), axis=AX.X)
    p.op("vector", f_red, waits=waits_v, n=1)

    def f_sq(s, S):
        return s.activation(out=sq_out_fn(), in_=src_ap_fn(), func=AF.Square,
                            accum_out=st[:, c1:c1 + 1])
    p.op("scalar", f_sq, waits=tuple(waits_s) + (("sv", p.n("sv")),), n=1)
    n_sq = p.n("ss")

    def f_fin(v, S):
        v.tensor_scalar(out=st[:, c2:c2 + 2], in0=st[:, c0:c0 + 2],
                        scalar1=1.0 / D, scalar2=None, op0=ALU.mult)
        v.tensor_tensor(out=st[:, c4:c4 + 1], in0=st[:, c2:c2 + 1],
                        in1=st[:, c2:c2 + 1], op=ALU.mult)
        v.tensor_tensor(out=st[:, c4:c4 + 1], in0=st[:, c3:c3 + 1],
                        in1=st[:, c4:c4 + 1], op=ALU.subtract)
        return v.tensor_scalar(out=st[:, c4:c4 + 1], in0=st[:, c4:c4 + 1],
                               scalar1=EPS_LN, scalar2=None, op0=ALU.add)
    p.op("vector", f_fin, waits=(("ss", n_sq),), n=4)
    n_var = p.n("sv")

    def f_sqrt(s, S):
        return s.activation(out=st[:, c5:c5 + 1], in_=st[:, c4:c4 + 1],
                            func=AF.Sqrt)
    p.op("scalar", f_sqrt, waits=(("sv", n_var),), n=1)
    n_rst = p.n("ss")

    def f_recip(v, S):
        v.reciprocal(out=st[:, c5:c5 + 1], in_=st[:, c5:c5 + 1])
        return v.scalar_tensor_tensor(out=st[:, c6:c6 + 1], in0=st[:, c2:c2 + 1],
                                      scalar=-1.0, in1=st[:, c5:c5 + 1],
                                      op0=ALU.mult, op1=ALU.mult)
    p.op("vector", f_recip, waits=(("ss", n_rst),), n=2)
    return (lambda: st[:, c5:c5 + 1]), (lambda: st[:, c6:c6 + 1])



def emit_ffn(p, nc, ar, cio, pf, T=2048, TC=512):
    nH = T // 2
    ntc = nH // TC
    ntile = nH // 128
    n_oc = DH // 128
    n_ic = D // 128
    n_oc3 = D // 128
    sub = TC // 128

    xin = cio["xin"]
    w1s, w2s, h1s, h2s, w3s = cio["w1s"], cio["w2s"], cio["h1s"], cio["h2s"], cio["w3s"]
    sc12, sch, sc3 = cio["sc12"], cio["sch"], cio["sc3"]
    xout = cio["xout"]
    idb_sb, idf_sb = cio["idb"], cio["idf"]
    P, TPS = cio["P"], cio["TPS"]

    xin_t = ar.sb("xin_t", 2 * D, F32)
    u_tm = ar.sb("u_tm", 2 * D, BF16)
    u = ar.sb("u", n_ic * nH, BF16)
    big_a = ar.sb("big_a", n_oc * nH, BF16)
    big_b = ar.sb("big_b", max(n_oc * nH, n_oc3 * n_oc * 128), BF16)
    wab = ar.sb("wab", 2 * 2 * n_ic * 128, BF16)
    wh = ar.sb("wh", 2 * n_oc * 128, BF16)
    hsil = ar.sb("hsil", 2 * TC, BF16)
    TCE = min(TC, 256)
    sub_e = TCE // 128
    ntc_e = nH // TCE
    ysc = ar.sb("ysc", n_oc3 * TCE, F32)
    y_tm = ar.sb("y_tm", D, F32)
    xo_t = ar.sb("xo_t", 2 * D, F32)
    st = ar.sb("st", 16, F32)
    sc12_sb = ar.sb("sc12_sb", 2 * n_oc, F32)
    sch_sb = ar.sb("sch_sb", 2 * n_oc, F32)
    sc3_sb = ar.sb("sc3_sb", n_oc3, F32)

    names = ["aux", "xa0", "xa1", "wa0", "wa1", "wh0", "wh1", "w3l",
             "tr", "mma", "mmb", "mmc", "mme", "tre", "so0", "so1"]
    g = {nm: pf + nm for nm in names}
    p.sem(*[g[nm] for nm in names])

    class _G:
        def __getitem__(self, k):
            return g[k]
    aux, xa0, xa1 = g["aux"], g["xa0"], g["xa1"]

    for dst, srcap in [(sc12_sb, sc12), (sch_sb, sch), (sc3_sb, sc3)]:
        def f_aux(e, S, dst=dst, srcap=srcap):
            return e.dma_start(out=dst.ap(), in_=srcap)
        p.op("sync", f_aux, inc=g["aux"], k=16)
    AUXN = ("common_aux", cio["aux_n"])
    SCLN = (g["aux"], p.n(g["aux"]))

    xin_free = [None, None]
    tps_free = [0, 0]
    silu_done = [0, 0]
    h_done = [0, 0]
    gelu_done = [0, 0, 0, 0]
    ye_done = [0, 0]
    yev_free = 0
    wa_free = [None, None]
    wh_free = [None, None]
    mme_total_prev = 0

    for half in range(2):
        tok0 = half * nH

        for i in range(ntile):
            slot = i % 2
            tok = tok0 + i * 128
            xa = g[f"xa{slot}"]
            gi = half * ntile + i

            def f_ld(e, S, tok=tok, slot=slot):
                return e.dma_start(out=xin_t[:, slot * D:(slot + 1) * D],
                                   in_=xin[tok:tok + 128, :])
            w = (xin_free[slot],) if xin_free[slot] else ()
            p.op("sync", f_ld, waits=w, inc=xa, k=16)
            nx = p.n(xa)

            src = (lambda slot=slot: xin_t[:, slot * D:(slot + 1) * D])
            sqdump = (lambda slot=slot: u_tm[:, slot * D:(slot + 1) * D])
            sq_waits = [(xa, nx)]
            if tps_free[slot]:
                sq_waits.append((g["tr"], p.n(g["tr"])))
            r_ap, mr_ap = emit_ln_stats(p, src, st, 0, ((xa, nx),),
                                        sq_waits, sqdump)

            def f_apply(v, S, slot=slot, r=r_ap, mr=mr_ap):
                return v.tensor_scalar(out=u_tm[:, slot * D:(slot + 1) * D],
                                       in0=xin_t[:, slot * D:(slot + 1) * D],
                                       scalar1=r(), scalar2=mr(),
                                       op0=ALU.mult, op1=ALU.add)
            p.op("vector", f_apply, n=1)
            n_ua = p.n("sv")
            xin_free[slot] = ("sv", n_ua)

            tr_waits = [("sv", n_ua), AUXN, SCLN]
            if tps_free[slot]:
                tr_waits.append(("sv", tps_free[slot]))

            def f_tr(t, S, slot=slot):
                last = None
                for j in range(n_ic):
                    last = t.transpose(
                        sb_ap(TPS[slot], j * 128, [[1, 128]]),
                        u_tm[:, slot * D + j * 128: slot * D + (j + 1) * 128],
                        idb_sb.ap())
                return last
            p.op("tensor", f_tr, waits=tr_waits, inc=g["tr"])

            uev_waits = [(g["tr"], p.n(g["tr"]))]
            if half == 1 and i == 0:
                uev_waits.append((g["mmb"], n_oc * ntc))

            def f_uev(v, S, slot=slot, i=i):
                return v.tensor_copy(
                    out=sb_ap(u, i * 128, [[nH, n_ic], [1, 128]]),
                    in_=sb_ap(TPS[slot], 0, [[128, n_ic], [1, 128]]))
            p.op("vector", f_uev, waits=uev_waits, n=1)
            tps_free[slot] = p.n("sv")
        uev_done = p.n("sv")

        for oc in range(n_oc):
            gslot = (half * n_oc + oc) % 2
            wl_waits = [wa_free[gslot]] if wa_free[gslot] else []

            def f_wab1(e, S, oc=oc, slot=gslot):
                c = slot * 2 * n_ic * 128
                return e.dma_start(out=wab[:, c:c + n_ic * 128], in_=w1s[oc])

            def f_wab2(e, S, oc=oc, slot=gslot):
                c = slot * 2 * n_ic * 128
                return e.dma_start(out=wab[:, c + n_ic * 128:c + 2 * n_ic * 128],
                                   in_=w2s[oc])
            wa = g[f"wa{gslot}"]
            p.op("gpsimd", f_wab1, waits=wl_waits, inc=wa, k=16, n=0)
            p.op("gpsimd", f_wab2, inc=wa, k=16, n=0)
            wl_wait = (wa, p.n(wa))

            for tc in range(ntc):
                idx = (half * n_oc + oc) * ntc + tc
                mm_waits = [wl_wait, ("sv", uev_done)]
                if silu_done[idx % 2]:
                    mm_waits.append(("ss", silu_done[idx % 2]))
                if h_done[idx % 2]:
                    mm_waits.append(("sv", h_done[idx % 2]))
                pa, pb = P[idx % 2], P[2 + idx % 2]

                def f_mma(t, S, oc=oc, tc=tc, slot=gslot, pa=pa):
                    c = slot * 2 * n_ic * 128
                    last = None
                    for ic in range(n_ic):
                        last = t.matmul(pa[:, 0:TC],
                                        wab[:, c + ic * 128:c + (ic + 1) * 128],
                                        u[:, ic * nH + tc * TC: ic * nH + (tc + 1) * TC],
                                        start=(ic == 0), stop=(ic == n_ic - 1))
                    return last
                p.op("tensor", f_mma, waits=mm_waits, inc=g["mma"])

                def f_mmb(t, S, oc=oc, tc=tc, slot=gslot, pb=pb):
                    c = slot * 2 * n_ic * 128 + n_ic * 128
                    last = None
                    for ic in range(n_ic):
                        last = t.matmul(pb[:, 0:TC],
                                        wab[:, c + ic * 128:c + (ic + 1) * 128],
                                        u[:, ic * nH + tc * TC: ic * nH + (tc + 1) * TC],
                                        start=(ic == 0), stop=(ic == n_ic - 1))
                    return last
                p.op("tensor", f_mmb, inc=g["mmb"])
                if tc == ntc - 1:
                    wa_free[gslot] = (g["mmb"], p.n(g["mmb"]))

                sil_waits = [(g["mma"], p.n(g["mma"])), SCLN]
                if h_done[idx % 2]:
                    sil_waits.append(("sv", h_done[idx % 2]))

                def f_sil(s, S, oc=oc, idx=idx, pa=pa):
                    return s.activation(out=hsil[:, (idx % 2) * TC:(idx % 2 + 1) * TC],
                                        in_=pa[:, 0:TC], func=AF.Silu,
                                        scale=sc12_sb[:, oc:oc + 1])
                p.op("scalar", f_sil, waits=sil_waits, n=1)
                silu_done[idx % 2] = p.n("ss")

                hw_waits = [(g["mmb"], p.n(g["mmb"])), ("ss", silu_done[idx % 2])]
                if half == 1 and oc == 0 and tc == 0:
                    hw_waits.append((g["mme"], mme_total_prev))

                def f_h(v, S, oc=oc, tc=tc, idx=idx, pb=pb):
                    return v.scalar_tensor_tensor(
                        out=big_a[:, oc * nH + tc * TC: oc * nH + (tc + 1) * TC],
                        in0=pb[:, 0:TC], scalar=sc12_sb[:, n_oc + oc:n_oc + oc + 1],
                        in1=hsil[:, (idx % 2) * TC:(idx % 2 + 1) * TC],
                        op0=ALU.mult, op1=ALU.mult)
                p.op("vector", f_h, waits=hw_waits, n=1)
                h_done[idx % 2] = p.n("sv")
        h_all = p.n("sv")

        for stage in range(2):
            wsrc = h1s if stage == 0 else h2s
            rbuf = big_a if stage == 0 else big_b
            obuf = big_b if stage == 0 else big_a
            scol0 = 0 if stage == 0 else n_oc
            ready = ("sv", h_all) if stage == 0 else ("ss", p.n("ss"))
            for oc in range(n_oc):
                gslot = ((half * 2 + stage) * n_oc + oc) % 2
                wl_waits = [wh_free[gslot]] if wh_free[gslot] else []

                def f_wh(e, S, oc=oc, slot=gslot, wsrc=wsrc):
                    c = slot * n_oc * 128
                    return e.dma_start(out=wh[:, c:c + n_oc * 128], in_=wsrc[oc])
                whs = g[f"wh{gslot}"]
                p.op("gpsimd", f_wh, waits=wl_waits, inc=whs, k=16, n=0)
                wl_wait = (whs, p.n(whs))

                for tc in range(ntc):
                    pslot = p.n(g["mmc"]) % 2
                    mm_waits = [wl_wait, ready]
                    if gelu_done[pslot]:
                        mm_waits.append(("ss", gelu_done[pslot]))
                    pc = P[pslot]

                    def f_mmc(t, S, oc=oc, tc=tc, slot=gslot, pc=pc, rbuf=rbuf):
                        c = slot * n_oc * 128
                        last = None
                        for ic in range(n_oc):
                            last = t.matmul(
                                pc[:, 0:TC], wh[:, c + ic * 128:c + (ic + 1) * 128],
                                rbuf[:, ic * nH + tc * TC: ic * nH + (tc + 1) * TC],
                                start=(ic == 0), stop=(ic == n_oc - 1))
                        return last
                    p.op("tensor", f_mmc, waits=mm_waits, inc=g["mmc"])
                    if tc == ntc - 1:
                        wh_free[gslot] = (g["mmc"], p.n(g["mmc"]))

                    g_waits = [(g["mmc"], p.n(g["mmc"]))]
                    if half == 1 and stage == 0 and oc == 0 and tc == 0:
                        g_waits.append((g["mme"], mme_total_prev))

                    def f_g(s, S, oc=oc, tc=tc, pc=pc, obuf=obuf, scol=scol0 + oc):
                        return s.activation(
                            out=obuf[:, oc * nH + tc * TC: oc * nH + (tc + 1) * TC],
                            in_=pc[:, 0:TC], func=AF.Gelu,
                            scale=sch_sb[:, scol:scol + 1])
                    p.op("scalar", f_g, waits=g_waits, n=1)
                    gelu_done[pslot] = p.n("ss")
        d_mm_done = p.n(g["mmc"])
        g2_done = p.n("ss")

        for oc3 in range(n_oc3):
            def f_w3(e, S, oc=oc3):
                return e.dma_start(out=big_b[:, oc * n_oc * 128:(oc + 1) * n_oc * 128],
                                   in_=w3s[oc])
            p.op("gpsimd", f_w3,
                 waits=(((g["mmc"], d_mm_done),) if oc3 == 0 else ()),
                 inc=g["w3l"], k=16, n=0)
        w3_wait = (g["w3l"], p.n(g["w3l"]))

        for tc in range(ntc_e):
            tre_before = p.n(g["tre"])
            for oc in range(n_oc3):
                eslot = p.n(g["mme"]) % 2
                mm_waits = [w3_wait, ("ss", g2_done)]
                if ye_done[eslot]:
                    mm_waits.append(("sv", ye_done[eslot]))
                pe = P[eslot]

                def f_mme(t, S, oc=oc, tc=tc, pe=pe):
                    last = None
                    for ic in range(n_oc):
                        last = t.matmul(
                            pe[:, 0:TCE], big_b[:, oc * n_oc * 128 + ic * 128:
                                                oc * n_oc * 128 + (ic + 1) * 128],
                            big_a[:, ic * nH + tc * TCE: ic * nH + (tc + 1) * TCE],
                            start=(ic == 0), stop=(ic == n_oc - 1))
                    return last
                p.op("tensor", f_mme, waits=mm_waits, inc=g["mme"])

                ye_waits = [(g["mme"], p.n(g["mme"]))]
                if tre_before > 0:
                    ye_waits.append((g["tre"], tre_before))

                def f_ye(v, S, oc=oc, pe=pe):
                    return v.tensor_scalar(out=ysc[:, oc * TCE:(oc + 1) * TCE],
                                           in0=pe[:, 0:TCE], scalar1=sc3_sb[:, oc:oc + 1],
                                           scalar2=None, op0=ALU.mult)
                p.op("vector", f_ye, waits=ye_waits, n=1)
                ye_done[eslot] = p.n("sv")
            ye_all = p.n("sv")

            for t4 in range(sub_e):
                i_glob = half * ntile + tc * sub_e + t4
                tok = tok0 + (tc * sub_e + t4) * 128

                tre_waits = [("sv", ye_all)]
                if yev_free:
                    tre_waits.append(("sv", yev_free))

                def f_tre(t, S, t4=t4):
                    last = None
                    for oc in range(n_oc3):
                        last = t.transpose(
                            sb_ap(P[4 + oc // 4], (oc % 4) * 128, [[1, 128]]),
                            ysc[:, oc * TCE + t4 * 128: oc * TCE + (t4 + 1) * 128],
                            idf_sb.ap())
                    return last
                p.op("tensor", f_tre, waits=tre_waits, inc=g["tre"])

                def f_yev(v, S):
                    v.tensor_copy(out=y_tm[:, 0:512], in_=P[4].ap())
                    return v.tensor_copy(out=y_tm[:, 512:1024], in_=P[5].ap())
                p.op("vector", f_yev, waits=((g["tre"], p.n(g["tre"])),), n=2)
                yev_free = p.n("sv")
                n_yev = p.n("sv")

                slot = i_glob % 2
                xa = g[f"xa{slot}"]

                def f_xr(e, S, tok=tok, slot=slot):
                    return e.dma_start(out=xin_t[:, slot * D:(slot + 1) * D],
                                       in_=xin[tok:tok + 128, :])
                w = (xin_free[slot],) if xin_free[slot] else ()
                p.op("sync", f_xr, waits=w, inc=xa, k=16)
                nx = p.n(xa)

                r_ap, mr_ap = emit_ln_stats(p, lambda: y_tm.ap(), st, 8,
                                            (("sv", n_yev),), (("sv", n_yev),),
                                            lambda: u_tm[:, 0:D])

                so = g[f"so{slot}"]

                def f_res(v, S, slot=slot, r=r_ap, mr=mr_ap):
                    v.tensor_scalar(out=xo_t[:, slot * D:(slot + 1) * D],
                                    in0=y_tm.ap(), scalar1=r(), scalar2=mr(),
                                    op0=ALU.mult, op1=ALU.add)
                    return v.tensor_tensor(out=xo_t[:, slot * D:(slot + 1) * D],
                                           in0=xo_t[:, slot * D:(slot + 1) * D],
                                           in1=xin_t[:, slot * D:(slot + 1) * D],
                                           op=ALU.add)
                res_waits = [(xa, nx)]
                if p.n(so) > 0:
                    res_waits.append((so, p.n(so)))
                p.op("vector", f_res, waits=res_waits, n=2)
                xin_free[slot] = ("sv", p.n("sv"))

                def f_sto(e, S, tok=tok, slot=slot):
                    return e.dma_start(out=xout[tok:tok + 128, :],
                                       in_=xo_t[:, slot * D:(slot + 1) * D])
                p.op("sync", f_sto, waits=(("sv", p.n("sv")),), inc=so, k=16)
        mme_total_prev = p.n(g["mme"])

    p.op("sync", lambda e, S: e.nop(),
         waits=((g["so0"], p.n(g["so0"])), (g["so1"], p.n(g["so1"]))))





def build_ffn(T=2048, TC=512):
    nc = bass.Bass("TRN2", target_bir_lowering=False)
    n_oc, n_ic, n_oc3 = DH // 128, D // 128, D // 128
    cio = {
        "xin": nc.dram_tensor("xin", [T, D], F32, kind="ExternalInput").ap(),
        "w1s": nc.dram_tensor("w1s", [n_oc, 128, n_ic * 128], F32, kind="ExternalInput").ap(),
        "w2s": nc.dram_tensor("w2s", [n_oc, 128, n_ic * 128], F32, kind="ExternalInput").ap(),
        "h1s": nc.dram_tensor("h1s", [n_oc, 128, n_oc * 128], F32, kind="ExternalInput").ap(),
        "h2s": nc.dram_tensor("h2s", [n_oc, 128, n_oc * 128], F32, kind="ExternalInput").ap(),
        "w3s": nc.dram_tensor("w3s", [n_oc3, 128, n_oc * 128], F32, kind="ExternalInput").ap(),
        "sc12": nc.dram_tensor("sc12", [128, 2 * n_oc], F32, kind="ExternalInput").ap(),
        "sch": nc.dram_tensor("sch", [128, 2 * n_oc], F32, kind="ExternalInput").ap(),
        "sc3": nc.dram_tensor("sc3", [128, n_oc3], F32, kind="ExternalInput").ap(),
        "xout": nc.dram_tensor("xout", [T, D], F32, kind="ExternalOutput").ap(),
    }
    id_bf = nc.inline_tensor(np.eye(128, dtype=ml_dtypes.bfloat16), name="id_bf")
    id_f32 = nc.inline_tensor(np.eye(128, dtype=np.float32), name="id_f32")
    idb_sb = nc.alloc_sbuf_tensor("idb_sb", [128, 128], BF16)
    idf_sb = nc.alloc_sbuf_tensor("idf_sb", [128, 128], F32)
    nH = T // 2
    bfc = (2 * D) + (n_ic * nH) + 2 * max(DH // 128 * nH, 8 * 32 * 128) + \
          (2 * 2 * (D // 128) * 128) + (2 * 32 * 128) + (2 * TC) + 64
    f32c = 2 * D + 8 * min(TC, 256) + D + 2 * D + 16 + 64 + 64 + 8 + 64
    ar = Arena(nc, bfc, f32c)
    Pb = [nc.alloc_psum_tensor(f"P{i}", [128, 512], F32) for i in range(6)]
    TPSb = [nc.alloc_psum_tensor(f"Tp{i}", [128, 1024], BF16) for i in range(2)]
    cio["P"], cio["TPS"] = Pb, TPSb
    cio["idb"], cio["idf"] = idb_sb, idf_sb
    p = Prog(nc)
    p.sem("common_aux")

    def f_id1(e, S):
        return e.dma_start(out=idb_sb.ap(), in_=id_bf.ap())
    p.op("sync", f_id1, inc="common_aux", k=16)

    def f_id2(e, S):
        return e.dma_start(out=idf_sb.ap(), in_=id_f32.ap())
    p.op("sync", f_id2, inc="common_aux", k=16)
    cio["aux_n"] = p.n("common_aux")
    emit_ffn(p, nc, ar, cio, "F_", T=T, TC=TC)
    with ExitStack() as es:
        p.emit(es)
    return nc

def swz_lhsT(w):
    O, I = w.shape
    a = np.ascontiguousarray(
        w.reshape(O // 128, 128, I // 128, 128).transpose(0, 3, 2, 1))
    return a.reshape(O // 128, 128, I)


def ffn_host_inputs(w1_w, w1_s, w2_w, w2_s, h1_w, h1_s, h2_w, h2_s, w3_w, w3_s):
    d = {
        "w1s": swz_lhsT(np.asarray(w1_w)),
        "w2s": swz_lhsT(np.asarray(w2_w)),
        "h1s": swz_lhsT(np.asarray(h1_w)),
        "h2s": swz_lhsT(np.asarray(h2_w)),
        "w3s": swz_lhsT(np.asarray(w3_w)),
        "sc12": np.ascontiguousarray(
            np.concatenate([np.asarray(w1_s).reshape(32, 128).T,
                            np.asarray(w2_s).reshape(32, 128).T], axis=1)),
        "sch": np.ascontiguousarray(
            np.concatenate([np.asarray(h1_s).reshape(32, 128).T,
                            np.asarray(h2_s).reshape(32, 128).T], axis=1)),
        "sc3": np.ascontiguousarray(np.asarray(w3_s).reshape(8, 128).T),
    }
    return {k: np.ascontiguousarray(v, dtype=np.float32) for k, v in d.items()}





NH1 = 8
DH1 = 16
DPHI = 1 + DH1 + DH1 * DH1
EPS_L1 = 1e-5


def _emit_rms_tile(p, xin, xin_t, st, slot, tok, xa, xin_free, z_tm,
                   sq_extra=()):

    def f_ld(e, S, tok=tok, slot=slot):
        return e.dma_start(out=xin_t[:, slot * D:(slot + 1) * D],
                           in_=xin[tok:tok + 128, :])
    w = (xin_free[slot],) if xin_free[slot] else ()
    p.op("sync", f_ld, waits=w, inc=xa, k=16)
    nx = p.n(xa)

    def f_sq(s, S, slot=slot):
        return s.activation(out=z_tm[:, slot * D:(slot + 1) * D],
                            in_=xin_t[:, slot * D:(slot + 1) * D],
                            func=AF.Square, accum_out=st[:, 0:1])
    p.op("scalar", f_sq,
         waits=((xa, nx), ("sv", p.n("sv"))) + tuple(sq_extra), n=1)

    def f_nrm(s, S):
        return s.activation(out=st[:, 1:2], in_=st[:, 0:1], func=AF.Sqrt)
    p.op("scalar", f_nrm, waits=(("sv", p.n("sv")),), n=1)
    n_rst = p.n("ss")

    def f_r(v, S):
        v.tensor_scalar(out=st[:, 1:2], in0=st[:, 1:2], scalar1=1e-12,
                        scalar2=None, op0=ALU.max)
        v.reciprocal(out=st[:, 1:2], in_=st[:, 1:2])
        return v.tensor_scalar(out=st[:, 2:3], in0=st[:, 1:2],
                               scalar1=float(D) ** 0.5, scalar2=None,
                               op0=ALU.mult)
    p.op("vector", f_r, waits=(("ss", n_rst),), n=3)

    def f_z(v, S, slot=slot):
        return v.tensor_scalar(out=z_tm[:, slot * D:(slot + 1) * D],
                               in0=xin_t[:, slot * D:(slot + 1) * D],
                               scalar1=st[:, 2:3], scalar2=None, op0=ALU.mult)
    p.op("vector", f_z, n=1)
    xin_free[slot] = ("sv", p.n("sv"))


def _emit_phi(p, src_sb, src_col0, khalf, phi, waits_v):

    def f_phi(v, S):
        src = lambda dims, off=0: g_ap(src_sb, 0, 128, src_col0 + off, dims)
        v.tensor_scalar(out=khalf.ap(),
                        in0=src([[1, NH1 * DH1]]),
                        scalar1=0.5 ** 0.5, scalar2=None, op0=ALU.mult)
        v.memset(sb_ap(phi, 0, [[DPHI, NH1], [1, 1]]), 1.0)
        v.tensor_copy(out=sb_ap(phi, 1, [[DPHI, NH1], [1, DH1]]),
                      in_=src([[DH1, NH1], [1, DH1]]))
        return v.tensor_tensor(
            out=sb_ap(phi, 1 + DH1, [[DPHI, NH1], [DH1, DH1], [1, DH1]]),
            in0=src([[DH1, NH1], [0, DH1], [1, DH1]]),
            in1=sb_ap(khalf, 0, [[DH1, NH1], [1, DH1], [0, DH1]]),
            op=ALU.mult)
    p.op("vector", f_phi, waits=waits_v, n=4)


def emit_l1a(p, nc, ar, cio, pf, T=2048):
    ntile = T // 128
    n_ic = D // 128

    xin = cio["xin"]
    kvw = cio["kvw"]
    gam = cio["gam"]
    kvp = cio["kvp"]
    idb_sb = cio["idb"]
    Pg, TPSg = cio["P"], cio["TPS"]
    P = Pg[0:4]
    TPS = TPSg

    xin_t = ar.sb("xin_t", 2 * D, F32)
    z_tm = ar.sb("z_tm", 2 * D, BF16)
    z_fm = ar.sb("z_fm", n_ic * T, BF16)
    kvw_sb = ar.sb("kvw_sb", 2 * n_ic * 128, BF16)
    gam_sb = ar.sb("gam_sb", n_ic, F32)
    kv_fm = ar.sb("kv_fm", 2 * T, BF16)
    ktm = ar.sb("ktm", 256, BF16)
    khalf = ar.sb("khalf", 128, BF16)
    phi = ar.sb("phi", NH1 * DPHI, BF16)
    vaug = ar.sb("vaug", NH1 * 17, BF16)
    kvacc = ar.sb("kvacc", NH1 * DPHI, F32)
    st = ar.sb("st", 8, F32)

    names = ["aux", "gld", "xa0", "xa1", "tr", "mkv", "mmk", "sto"]
    g = {nm: pf + nm for nm in names}
    p.sem(*[g[nm] for nm in names])
    AUXC = ("common_aux", cio["aux_n"])

    def f_gam(e, S):
        return e.dma_start(out=gam_sb.ap(), in_=gam)
    p.op("sync", f_gam, inc=g["aux"], k=16)

    def f_wl(e, S):
        return e.dma_start(out=kvw_sb.ap(),
                           in_=kvw.rearrange("a p c -> p a c"))
    p.op("gpsimd", f_wl, inc=g["gld"], k=16, n=0)

    def f_gs(v, S):
        last = None
        for oc in range(2):
            for ic in range(n_ic):
                c = oc * n_ic * 128 + ic * 128
                last = v.tensor_scalar(out=kvw_sb[:, c:c + 128],
                                       in0=kvw_sb[:, c:c + 128],
                                       scalar1=gam_sb[:, ic:ic + 1],
                                       scalar2=None, op0=ALU.mult)
        return last
    p.op("vector", f_gs, waits=((g["gld"], 16), (g["aux"], 16), AUXC), n=2 * n_ic)
    gs_done = p.n("sv")

    xin_free = [None, None]
    tps_free = [0, 0]

    for i in range(ntile):
        slot = i % 2
        xa = g[f"xa{slot}"]
        sq_extra = [(g["tr"], p.n(g["tr"]))] if tps_free[slot] else []
        _emit_rms_tile(p, xin, xin_t, st, slot, i * 128, xa, xin_free, z_tm,
                       sq_extra)
        n_z = p.n("sv")

        tr_waits = [("sv", n_z), (g["aux"], 16), AUXC]
        if tps_free[slot]:
            tr_waits.append(("sv", tps_free[slot]))

        def f_tr(t, S, slot=slot):
            last = None
            for j in range(n_ic):
                last = t.transpose(
                    sb_ap(TPS[slot], j * 128, [[1, 128]]),
                    z_tm[:, slot * D + j * 128: slot * D + (j + 1) * 128],
                    idb_sb.ap())
            return last
        p.op("tensor", f_tr, waits=tr_waits, inc=g["tr"])

        def f_zev(v, S, slot=slot, i=i):
            return v.tensor_copy(
                out=sb_ap(z_fm, i * 128, [[T, n_ic], [1, 128]]),
                in_=sb_ap(TPS[slot], 0, [[128, n_ic], [1, 128]]))
        p.op("vector", f_zev, waits=((g["tr"], p.n(g["tr"])),), n=1)
        tps_free[slot] = p.n("sv")
    z_done = p.n("sv")

    ev_free = [0, 0]
    for oc in range(2):
        for tc in range(T // 512):
            idx = oc * (T // 512) + tc
            ps = P[idx % 2]
            mm_waits = [("sv", z_done)]
            if ev_free[idx % 2]:
                mm_waits.append(("sv", ev_free[idx % 2]))

            def f_mm(t, S, oc=oc, tc=tc, ps=ps):
                last = None
                for ic in range(n_ic):
                    last = t.matmul(ps.ap(),
                                    kvw_sb[:, oc * n_ic * 128 + ic * 128:
                                           oc * n_ic * 128 + (ic + 1) * 128],
                                    z_fm[:, ic * T + tc * 512: ic * T + (tc + 1) * 512],
                                    start=(ic == 0), stop=(ic == n_ic - 1))
                return last
            p.op("tensor", f_mm, waits=mm_waits, inc=g["mkv"])

            def f_ev(v, S, oc=oc, tc=tc, ps=ps):
                return v.tensor_copy(out=kv_fm[:, oc * T + tc * 512:
                                               oc * T + (tc + 1) * 512],
                                     in_=ps.ap())
            p.op("vector", f_ev, waits=((g["mkv"], p.n(g["mkv"])),), n=1)
            ev_free[idx % 2] = p.n("sv")
    kv_done = p.n("sv")

    def f_z0(v, S):
        return v.memset(g_ap(kvacc, 0, 17, 0, [[1, NH1 * DPHI]]), 0.0)
    p.op("vector", f_z0, n=1)

    pk_free = [0, 0]
    for i in range(ntile):
        slot = i % 2
        tr_waits = [("sv", kv_done), (g["tr"], p.n(g["tr"]))]
        if tps_free[slot]:
            tr_waits.append(("sv", tps_free[slot]))

        def f_trk(t, S, slot=slot, i=i):
            t.transpose(sb_ap(TPS[slot], 0, [[1, 128]]),
                        kv_fm[:, i * 128:(i + 1) * 128], idb_sb.ap())
            return t.transpose(sb_ap(TPS[slot], 128, [[1, 128]]),
                               kv_fm[:, T + i * 128: T + (i + 1) * 128],
                               idb_sb.ap())
        p.op("tensor", f_trk, waits=tr_waits, inc=g["tr"])

        def f_kev(v, S, slot=slot):
            return v.tensor_copy(out=ktm.ap(), in_=TPS[slot].ap()[:, 0:256])
        p.op("vector", f_kev, waits=((g["tr"], p.n(g["tr"])),), n=1)
        tps_free[slot] = p.n("sv")

        _emit_phi(p, ktm, 0, khalf, phi, ())

        def f_va(v, S):
            v.tensor_copy(out=sb_ap(vaug, 0, [[17, NH1], [1, DH1]]),
                          in_=sb_ap(ktm, 128, [[DH1, NH1], [1, DH1]]))
            return v.memset(sb_ap(vaug, DH1, [[17, NH1], [1, 1]]), 1.0)
        p.op("vector", f_va, n=2)
        n_feat = p.n("sv")

        for h in range(NH1):
            pslot = 2 + (p.n(g["mmk"]) % 2)
            mm_waits = [("sv", n_feat)]
            if pk_free[pslot - 2]:
                mm_waits.append(("sv", pk_free[pslot - 2]))

            def f_mmk(t, S, h=h, pslot=pslot):
                return t.matmul(P[pslot][0:17, 0:DPHI],
                                vaug[:, h * 17:(h + 1) * 17],
                                phi[:, h * DPHI:(h + 1) * DPHI],
                                start=True, stop=True)
            p.op("tensor", f_mmk, waits=mm_waits, inc=g["mmk"])

            def f_acc(v, S, h=h, pslot=pslot):
                return v.tensor_tensor(
                    out=g_ap(kvacc, 0, 17, h * DPHI, [[1, DPHI]]),
                    in0=P[pslot][0:17, 0:DPHI],
                    in1=g_ap(kvacc, 0, 17, h * DPHI, [[1, DPHI]]),
                    op=ALU.add)
            p.op("vector", f_acc, waits=((g["mmk"], p.n(g["mmk"])),), n=1)
            pk_free[pslot - 2] = p.n("sv")

    def f_out(e, S):
        return e.dma_start(out=kvp,
                           in_=g_ap(kvacc, 0, 17, 0, [[1, NH1 * DPHI]]))
    p.op("sync", f_out, waits=(("sv", p.n("sv")),), inc=g["sto"], k=16)
    p.op("sync", lambda e, S: e.nop(), waits=((g["sto"], p.n(g["sto"])),))


def build_l1a(T=2048):
    nc = bass.Bass("TRN2", target_bir_lowering=False)
    n_ic = D // 128
    cio = {
        "xin": nc.dram_tensor("xin", [T, D], F32, kind="ExternalInput").ap(),
        "kvw": nc.dram_tensor("kvw", [2, 128, n_ic * 128], F32, kind="ExternalInput").ap(),
        "gam": nc.dram_tensor("gam", [128, n_ic], F32, kind="ExternalInput").ap(),
        "kvp": nc.dram_tensor("kvp", [17, NH1 * DPHI], F32, kind="ExternalOutput").ap(),
    }
    id_bf = nc.inline_tensor(np.eye(128, dtype=ml_dtypes.bfloat16), name="id_bf")
    idb_sb = nc.alloc_sbuf_tensor("idb_sb", [128, 128], BF16)
    cio["idb"] = idb_sb
    cio["P"] = [nc.alloc_psum_tensor(f"P{i}", [128, 512], F32) for i in range(4)]
    cio["TPS"] = [nc.alloc_psum_tensor(f"Tp{i}", [128, 1024], BF16) for i in range(2)]
    ar = Arena(nc, n_ic * T + 2 * T + 16000, 3 * D + 4096)
    p = Prog(nc)
    p.sem("common_aux")

    def f_id1(e, S):
        return e.dma_start(out=idb_sb.ap(), in_=id_bf.ap())
    p.op("sync", f_id1, inc="common_aux", k=16)
    cio["aux_n"] = p.n("common_aux")
    emit_l1a(p, nc, ar, cio, "A_", T=T)
    with ExitStack() as es:
        p.emit(es)
    return nc




def emit_l1b(p, nc, ar, cio, pf, T=2048):
    ntile = T // 128
    n_ic = D // 128
    DI = NH1 * DH1

    xin = cio["xin"]
    qw = cio["qw"]
    gam = cio["gam"]
    wout = cio["wout"]
    xout = cio["xout"]
    idb_sb = cio["idb"]
    Pg, TPS = cio["P"], cio["TPS"]
    P = Pg[0:4]
    PH = cio["PH"]

    xin_t = ar.sb("xin_t", 2 * D, F32)
    z_tm = ar.sb("z_tm", 2 * D, BF16)
    z_fm = ar.sb("z_fm", n_ic * T, BF16)
    qw_sb = ar.sb("qw_sb", n_ic * 128, BF16)
    gam_sb = ar.sb("gam_sb", n_ic, F32)
    wo_sb = ar.sb("wo_sb", D, BF16)
    q_fm = ar.sb("q_fm", T, BF16)
    qtm = ar.sb("qtm", 128, BF16)
    khalf = ar.sb("khalf", 128, BF16)
    phi = ar.sb("phi", NH1 * 384, BF16)
    phf = ar.sb("phf", 2 * 384, BF16)
    kvg = ar.sb("kvg", NH1 * DPHI, F32)
    kvtmp = None if "cc_out" in cio else ar.sb("kvtmp", NH1 * DPHI, F32)
    kvgb = ar.sb("kvgb", NH1 * DPHI, BF16)
    kvT = ar.sb("kvT", NH1 * 3 * 17, BF16)
    comb = ar.sb("comb", 128, BF16)
    combf = ar.sb("combf", 128, BF16)
    y_tm = ar.sb("y_tm", D, F32)
    xo_t = ar.sb("xo_t", 2 * D, F32)
    sqd = ar.sb("sqd", D, BF16)
    st = ar.sb("st", 16, F32)

    names = ["aux", "gld", "kl0", "kl1", "xa0", "xa1",
             "tr", "trf", "mq", "mmn", "mmy", "so0", "so1"]
    g = {nm: pf + nm for nm in names}
    p.sem(*[g[nm] for nm in names])
    AUXC = ("common_aux", cio["aux_n"])

    def f_gam(e, S):
        return e.dma_start(out=gam_sb.ap(), in_=gam)
    p.op("sync", f_gam, inc=g["aux"], k=16)

    def f_wl(e, S):
        return e.dma_start(out=qw_sb.ap(), in_=qw)
    p.op("gpsimd", f_wl, inc=g["gld"], k=16, n=0)

    def f_wo(e, S):
        return e.dma_start(out=wo_sb.ap(), in_=wout)
    p.op("gpsimd", f_wo, inc=g["gld"], k=16, n=0)

    if "cc_out" in cio:
        def f_k0(e, S):
            return e.dma_start(out=g_ap(kvg, 0, 17, 0, [[1, NH1 * DPHI]]),
                               in_=cio["cc_out"])
        p.op("sync", f_k0, waits=(("ccs", cio["ccs_n"]),), inc=g["kl0"], k=16)

        def f_cast(v, S):
            return v.tensor_copy(
                out=g_ap(kvgb, 0, 17, 0, [[1, NH1 * DPHI]]),
                in_=g_ap(kvg, 0, 17, 0, [[1, NH1 * DPHI]]))
        p.op("vector", f_cast, waits=((g["kl0"], 16),), n=1)
        cast_done = p.n("sv")
    else:
        kvp4 = cio["kvp4"]

        def f_k0(e, S):
            return e.dma_start(out=g_ap(kvg, 0, 17, 0, [[1, NH1 * DPHI]]),
                               in_=kvp4[0])
        p.op("sync", f_k0, inc=g["kl0"], k=16)
        add_done = 0
        for j in range(1, 4):
            def f_kj(e, S, j=j):
                return e.dma_start(
                    out=g_ap(kvtmp, 0, 17, 0, [[1, NH1 * DPHI]]),
                    in_=kvp4[j])
            w = [("sv", add_done)] if add_done else []
            p.op("sync", f_kj, waits=w, inc=g["kl1"], k=16)

            def f_add(v, S):
                return v.tensor_tensor(
                    out=g_ap(kvg, 0, 17, 0, [[1, NH1 * DPHI]]),
                    in0=g_ap(kvtmp, 0, 17, 0, [[1, NH1 * DPHI]]),
                    in1=g_ap(kvg, 0, 17, 0, [[1, NH1 * DPHI]]),
                    op=ALU.add)
            p.op("vector", f_add,
                 waits=((g["kl0"], 16), (g["kl1"], p.n(g["kl1"]))), n=1)
            add_done = p.n("sv")

        def f_cast(v, S):
            return v.tensor_copy(
                out=g_ap(kvgb, 0, 17, 0, [[1, NH1 * DPHI]]),
                in_=g_ap(kvg, 0, 17, 0, [[1, NH1 * DPHI]]))
        p.op("vector", f_cast, n=1)
        cast_done = p.n("sv")

    def f_kvz(v, S_):
        return v.memset(kvT.ap(), 0.0)
    p.op("vector", f_kvz, n=1)
    for h in range(NH1):
        def f_trk(t, S, h=h):
            last = None
            for c in range(3):
                dsz = min(128, DPHI - c * 128)
                last = t.transpose(
                    g_ap(PH[h % 2], 0, dsz, c * 32, [[1, 17]]),
                    g_ap(kvgb, 0, 17, h * DPHI + c * 128, [[1, dsz]]),
                    idb_sb[0:17, 0:17])
            return last
        trk_waits = [("sv", cast_done), (g["aux"], 16), AUXC]
        if h >= 2:
            trk_waits.append(("sv", kvt_ev[h % 2]))
        p.op("tensor", f_trk, waits=trk_waits, inc=g["tr"])

        def f_tev(v, S, h=h):
            last = None
            for c in range(3):
                dsz = min(128, DPHI - c * 128)
                last = v.tensor_copy(
                    out=g_ap(kvT, 0, dsz, h * 51 + c * 17, [[1, 17]]),
                    in_=g_ap(PH[h % 2], 0, dsz, c * 32, [[1, 17]]))
            return last
        p.op("vector", f_tev, waits=((g["tr"], p.n(g["tr"])),), n=3)
        if h == 0:
            kvt_ev = [0, 0]
        kvt_ev[h % 2] = p.n("sv")
    kvt_done = p.n("sv")
    tps_init = [kvt_ev[0], kvt_ev[1]]

    def f_gs(v, S):
        last = None
        for ic in range(n_ic):
            last = v.tensor_scalar(out=qw_sb[:, ic * 128:(ic + 1) * 128],
                                   in0=qw_sb[:, ic * 128:(ic + 1) * 128],
                                   scalar1=gam_sb[:, ic:ic + 1],
                                   scalar2=float(DH1) ** -0.5, op0=ALU.mult,
                                   op1=ALU.mult)
        return last
    p.op("vector", f_gs, waits=((g["gld"], p.n(g["gld"])), (g["aux"], 16), AUXC), n=n_ic)
    gs_done = p.n("sv")

    xin_free = [None, None]
    tps_free = [tps_init[0], tps_init[1]]

    for i in range(ntile):
        slot = i % 2
        xa = g[f"xa{slot}"]
        sq_extra = [(g["tr"], p.n(g["tr"]))] if tps_free[slot] else []
        _emit_rms_tile(p, xin, xin_t, st, slot, i * 128, xa, xin_free, z_tm,
                       sq_extra)
        n_z = p.n("sv")
        tr_waits = [("sv", n_z), (g["aux"], 16), AUXC]
        if tps_free[slot]:
            tr_waits.append(("sv", tps_free[slot]))

        def f_tr(t, S, slot=slot):
            last = None
            for j in range(n_ic):
                last = t.transpose(
                    sb_ap(TPS[slot], j * 128, [[1, 128]]),
                    z_tm[:, slot * D + j * 128: slot * D + (j + 1) * 128],
                    idb_sb.ap())
            return last
        p.op("tensor", f_tr, waits=tr_waits, inc=g["tr"])

        def f_zev(v, S, slot=slot, i=i):
            return v.tensor_copy(
                out=sb_ap(z_fm, i * 128, [[T, n_ic], [1, 128]]),
                in_=sb_ap(TPS[slot], 0, [[128, n_ic], [1, 128]]))
        p.op("vector", f_zev, waits=((g["tr"], p.n(g["tr"])),), n=1)
        tps_free[slot] = p.n("sv")
    z_done = p.n("sv")

    ev_free = [0, 0]
    for tc in range(T // 512):
        ps = P[tc % 2]
        mm_waits = [("sv", z_done), ("sv", gs_done)]
        if ev_free[tc % 2]:
            mm_waits.append(("sv", ev_free[tc % 2]))

        def f_mm(t, S, tc=tc, ps=ps):
            last = None
            for ic in range(n_ic):
                last = t.matmul(ps.ap(),
                                qw_sb[:, ic * 128:(ic + 1) * 128],
                                z_fm[:, ic * T + tc * 512: ic * T + (tc + 1) * 512],
                                start=(ic == 0), stop=(ic == n_ic - 1))
            return last
        p.op("tensor", f_mm, waits=mm_waits, inc=g["mq"])

        def f_ev(v, S, tc=tc, ps=ps):
            return v.tensor_copy(out=q_fm[:, tc * 512:(tc + 1) * 512], in_=ps.ap())
        p.op("vector", f_ev, waits=((g["mq"], p.n(g["mq"])),), n=1)
        ev_free[tc % 2] = p.n("sv")
    q_done = p.n("sv")

    phf_free = [0, 0]
    pn_free = [0, 0]
    py_free = 0
    yev_free = 0
    prev_cf = 0
    for i in range(ntile):
        slot = i % 2
        tok = i * 128

        tr_waits = [("sv", q_done), (g["tr"], p.n(g["tr"]))]
        if tps_free[slot]:
            tr_waits.append(("sv", tps_free[slot]))
        if prev_cf:
            tr_waits.append(("sv", prev_cf))

        def f_trq(t, S, slot=slot, i=i):
            return t.transpose(sb_ap(TPS[slot], 0, [[1, 128]]),
                               q_fm[:, i * 128:(i + 1) * 128], idb_sb.ap())
        p.op("tensor", f_trq, waits=tr_waits, inc=g["tr"])

        def f_qev(v, S, slot=slot):
            return v.tensor_copy(out=qtm.ap(), in_=TPS[slot].ap()[:, 0:128])
        p.op("vector", f_qev, waits=((g["tr"], p.n(g["tr"])),), n=1)
        tps_free[slot] = p.n("sv")

        _emit_phi(p, qtm, 0, khalf, phi, ())
        n_phi = p.n("sv")

        for h in range(NH1):
            trf_waits = [("sv", n_phi)]
            if phf_free[h % 2]:
                trf_waits.append(("sv", phf_free[h % 2]))

            def f_trf(t, S, h=h):
                last = None
                for c in range(3):
                    dsz = min(128, DPHI - c * 128)
                    last = t.transpose(
                        g_ap(PH[h % 2], 0, dsz, c * 128, [[1, 128]]),
                        phi[:, h * DPHI + c * 128: h * DPHI + c * 128 + dsz],
                        idb_sb.ap())
                return last
            p.op("tensor", f_trf, waits=trf_waits, inc=g["trf"])

            def f_fev(v, S, h=h):
                last = None
                for c in range(3):
                    dsz = min(128, DPHI - c * 128)
                    last = v.tensor_copy(
                        out=g_ap(phf, 0, dsz, c * 128, [[1, 128]]),
                        in_=g_ap(PH[h % 2], 0, dsz, c * 128, [[1, 128]]))
                return last
            p.op("vector", f_fev, waits=((g["trf"], p.n(g["trf"])),), n=3)
            phf_free[h % 2] = p.n("sv")

            pn = P[2 + (h % 2)]
            mmn_waits = [("sv", phf_free[h % 2]), ("sv", kvt_done)]
            if pn_free[h % 2]:
                mmn_waits.append(("sv", pn_free[h % 2]))

            def f_mmn(t, S, h=h, pn=pn):
                last = None
                for c in range(3):
                    dsz = min(128, DPHI - c * 128)
                    last = t.matmul(
                        pn[:, 0:17],
                        phf[0:dsz, c * 128:c * 128 + 128],
                        kvT[0:dsz, (h * 3 + c) * 17:(h * 3 + c + 1) * 17],
                        start=(c == 0), stop=(c == 2))
                return last
            p.op("tensor", f_mmn, waits=mmn_waits, inc=g["mmn"])

            def f_nrm(v, S, h=h, pn=pn):
                v.tensor_scalar(out=st[:, 4:5], in0=pn[:, 16:17],
                                scalar1=EPS_L1, scalar2=None, op0=ALU.max)
                v.reciprocal(out=st[:, 4:5], in_=st[:, 4:5])
                return v.tensor_scalar(out=comb[:, h * DH1:(h + 1) * DH1],
                                       in0=pn[:, 0:DH1], scalar1=st[:, 4:5],
                                       scalar2=None, op0=ALU.mult)
            p.op("vector", f_nrm, waits=((g["mmn"], p.n(g["mmn"])),), n=3)
            pn_free[h % 2] = p.n("sv")
        comb_done = p.n("sv")

        def f_trc(t, S, slot=slot):
            return t.transpose(sb_ap(TPS[slot], 256, [[1, 128]]),
                               comb.ap(), idb_sb.ap())
        p.op("tensor", f_trc, waits=(("sv", comb_done),), inc=g["tr"])

        def f_cev(v, S, slot=slot):
            return v.tensor_copy(out=combf.ap(), in_=TPS[slot].ap()[:, 256:384])
        p.op("vector", f_cev, waits=((g["tr"], p.n(g["tr"])),), n=1)
        cf_done = p.n("sv")
        prev_cf = cf_done

        mmy_waits = [("sv", cf_done), (g["gld"], p.n(g["gld"]))]
        if py_free:
            mmy_waits.append(("sv", py_free))

        def f_mmy(t, S):
            t.matmul(P[2][:, 0:512], combf.ap(), wo_sb[:, 0:512],
                     start=True, stop=True)
            return t.matmul(P[3][:, 0:512], combf.ap(), wo_sb[:, 512:1024],
                            start=True, stop=True)
        p.op("tensor", f_mmy, waits=mmy_waits, inc=g["mmy"])

        def f_yev(v, S):
            v.tensor_copy(out=y_tm[:, 0:512], in_=P[2].ap())
            return v.tensor_copy(out=y_tm[:, 512:1024], in_=P[3].ap())
        p.op("vector", f_yev, waits=((g["mmy"], p.n(g["mmy"])),), n=2)
        py_free = p.n("sv")
        pn_free = [p.n("sv"), p.n("sv")]
        n_yev = p.n("sv")

        xa = g[f"xa{slot}"]

        def f_xr(e, S, tok=tok, slot=slot):
            return e.dma_start(out=xin_t[:, slot * D:(slot + 1) * D],
                               in_=xin[tok:tok + 128, :])
        w = (xin_free[slot],) if xin_free[slot] else ()
        p.op("sync", f_xr, waits=w, inc=xa, k=16)
        nx = p.n(xa)

        r_ap, mr_ap = emit_ln_stats(p, lambda: y_tm.ap(), st, 8,
                                    (("sv", n_yev),), (("sv", n_yev),),
                                    lambda: sqd.ap())

        so = g[f"so{slot}"]

        def f_res(v, S, slot=slot, r=r_ap, mr=mr_ap):
            v.tensor_scalar(out=xo_t[:, slot * D:(slot + 1) * D],
                            in0=y_tm.ap(), scalar1=r(), scalar2=mr(),
                            op0=ALU.mult, op1=ALU.add)
            return v.tensor_tensor(out=xo_t[:, slot * D:(slot + 1) * D],
                                   in0=xo_t[:, slot * D:(slot + 1) * D],
                                   in1=xin_t[:, slot * D:(slot + 1) * D],
                                   op=ALU.add)
        res_waits = [(xa, nx)]
        if p.n(so) > 0:
            res_waits.append((so, p.n(so)))
        p.op("vector", f_res, waits=res_waits, n=2)
        xin_free[slot] = ("sv", p.n("sv"))

        def f_sto(e, S, tok=tok, slot=slot):
            return e.dma_start(out=xout[tok:tok + 128, :],
                               in_=xo_t[:, slot * D:(slot + 1) * D])
        p.op("sync", f_sto, waits=(("sv", p.n("sv")),), inc=so, k=16)

    p.op("sync", lambda e, S: e.nop(),
         waits=((g["so0"], p.n(g["so0"])), (g["so1"], p.n(g["so1"]))))


def build_l1b(T=2048):
    nc = bass.Bass("TRN2", target_bir_lowering=False)
    n_ic = D // 128
    cio = {
        "xin": nc.dram_tensor("xin", [T, D], F32, kind="ExternalInput").ap(),
        "qw": nc.dram_tensor("qw", [128, n_ic * 128], F32, kind="ExternalInput").ap(),
        "gam": nc.dram_tensor("gam", [128, n_ic], F32, kind="ExternalInput").ap(),
        "wout": nc.dram_tensor("wout", [128, D], F32, kind="ExternalInput").ap(),
        "kvp4": nc.dram_tensor("kvp4", [4, 17, NH1 * DPHI], F32, kind="ExternalInput").ap(),
        "xout": nc.dram_tensor("xout", [T, D], F32, kind="ExternalOutput").ap(),
    }
    id_bf = nc.inline_tensor(np.eye(128, dtype=ml_dtypes.bfloat16), name="id_bf")
    idb_sb = nc.alloc_sbuf_tensor("idb_sb", [128, 128], BF16)
    cio["idb"] = idb_sb
    cio["P"] = [nc.alloc_psum_tensor(f"P{i}", [128, 512], F32) for i in range(4)]
    cio["TPS"] = [nc.alloc_psum_tensor(f"Tp{i}", [128, 1024], BF16) for i in range(2)]
    cio["PH"] = [nc.alloc_psum_tensor(f"Ph{i}", [128, 512], BF16) for i in range(2)]
    ar = Arena(nc, n_ic * T + T + 16000, 4 * D + 3 * NH1 * DPHI + 4096)
    p = Prog(nc)
    p.sem("common_aux")

    def f_id1(e, S):
        return e.dma_start(out=idb_sb.ap(), in_=id_bf.ap())
    p.op("sync", f_id1, inc="common_aux", k=16)
    cio["aux_n"] = p.n("common_aux")
    emit_l1b(p, nc, ar, cio, "B_", T=T)
    with ExitStack() as es:
        p.emit(es)
    return nc


def l1_host_inputs(gamma, w_q, w_kv, w_out):
    w_kv = np.asarray(w_kv)
    w_q = np.asarray(w_q)
    return {
        "kvw": swz_lhsT(np.ascontiguousarray(w_kv.T)),
        "qw": swz_lhsT(np.ascontiguousarray(w_q.T))[0],
        "gam": np.ascontiguousarray(np.asarray(gamma).reshape(8, 128).T,
                                    dtype=np.float32),
        "wout": np.ascontiguousarray(np.asarray(w_out), dtype=np.float32),
    }





NH0 = 16
DH0 = 64
SCALE0 = DH0 ** -0.5


def emit_l0(p, nc, ar, cio, pf, S=8192, TQ=2048):
    n_ic = D // 128
    nkt = S // 128
    nkb = S // 512
    nqc = TQ // 512
    npair = NH0 // 2

    xfull, xq = cio["xfull"], cio["xq"]
    wq, wl, wk, wv, wo = cio["wq"], cio["wl"], cio["wk"], cio["wv"], cio["wo"]
    bq, bl, bk, bv, bo = cio["bq"], cio["bl"], cio["bk"], cio["bv"], cio["bo"]
    xout = cio["xout"]
    idb_sb, on_sb = cio["idb"], cio["onb"]
    P, SS, TPS = cio["P2"], cio["SS"], cio["TPS"]

    xf_t = ar.sb("xf_t", 2 * D, F32)
    u_tm = ar.sb("u_tm", 2 * D, BF16)
    latent = ar.sb("latent", 2 * S, BF16)
    q_fm = ar.sb("q_fm", n_ic * TQ, BF16)
    k_sb = ar.sb("k_sb", S, BF16)
    v_sb = ar.sb("v_sb", nkt * 130, BF16)
    ao_fm = ar.sb("ao_fm", n_ic * TQ, BF16)
    if n_ic * TQ >= n_ic * D + 2 * D + n_ic * 512 + 2 * D:
        _b = ao_fm.base
        wq_ov = View(ao_fm.t, _b, n_ic * D); _b += n_ic * D
        wl_ov = View(ao_fm.t, _b, 2 * D); _b += 2 * D
        u0qs = View(ao_fm.t, _b, n_ic * 512); _b += n_ic * 512
        u0s = View(ao_fm.t, _b, 2 * D)
    else:
        wq_ov = ar.sb("wq_sb", n_ic * D, BF16)
        wl_ov = ar.sb("wl_sb", 2 * D, BF16)
        u0qs = ar.sb("u0qs", n_ic * 512, BF16)
        u0s = ar.sb("u0s", 2 * D, BF16)
    pexp = ar.sb("pexp", 8 * 1024, BF16)
    rr_sb = ar.sb("rr_sb", 1024, BF16)
    rrf_sb = ar.sb("rrf_sb", 1024, F32)
    rb_sb = ar.sb("rb_sb", 512, F32)
    rbr_sb = ar.sb("rbr_sb", 512, F32)
    aon_sb = ar.sb("aon_sb", 512, BF16)
    wq_sb = wq_ov
    wl_sb = wl_ov
    wk_sb = ar.sb("wk_sb", n_ic * 256, BF16)
    wv_sb = ar.sb("wv_sb", 2 * D, BF16)
    wo_late = 2 * S >= n_ic * D
    if wo_late:
        wo_sb = View(latent.t, latent.base, n_ic * D)
    else:
        wo_sb = ar.sb("wo_sb", n_ic * D, BF16)
    bq_sb = ar.sb("bq_sb", D, BF16)
    bl_sb = ar.sb("bl_sb", 256, BF16)
    bk_sb = ar.sb("bk_sb", D, BF16)
    bv_sb = ar.sb("bv_sb", D, BF16)
    bo_sb = ar.sb("bo_sb", D, BF16)
    y_tm = ar.sb("y_tm", D, F32)
    xo_t = ar.sb("xo_t", 2 * D, F32)
    st = ar.sb("st", 16, F32)

    names = ["aux", "wld", "xf0", "xf1", "xf2", "xf3", "xq0", "xq1",
             "tr", "mlat", "mq", "mk", "mv", "msc", "mav", "mbc", "my",
             "so0", "so1"]
    g = {nm: pf + nm for nm in names}
    p.sem(*[g[nm] for nm in names])
    AUXW = ("common_aux", cio["aux_n"])

    w_list = [(wq_sb, wq.rearrange("a p c -> p a c")),
              (wl_sb, wl.rearrange("a p c -> p a c")),
              (wk_sb, wk.rearrange("a p c -> p a c")),
              (wv_sb, wv.rearrange("a p c -> p a c")),
              (bq_sb, bq), (bl_sb, bl), (bk_sb, bk),
              (bv_sb, bv), (bo_sb, bo)]
    if not wo_late:
        w_list.append((wo_sb, wo.rearrange("a p c -> p a c")))
    for dst, srcap in w_list:
        def f_w(e, S_, dst=dst, srcap=srcap):
            o = dst.ap()
            if srcap.shape[0] == 1:
                o = dst[0:1, :]
            return e.dma_start(out=o, in_=srcap)
        p.op("gpsimd", f_w, inc=g["wld"], k=16, n=0)
    WLDN = p.n(g["wld"])

    xf4 = View(xf_t.t, xf_t.base, 4 * D)
    xf_free = [None, None, None, None]
    tps_free = [0, 0]
    lat_ev = [0, 0]
    for kt in range(nkt):
        slot = kt % 2
        ls = kt % 4
        xf = g[f"xf{ls}"]

        def f_ld(e, S_, kt=kt, ls=ls):
            return e.dma_start(out=xf4[:, ls * D:(ls + 1) * D],
                               in_=xfull[kt * 128:(kt + 1) * 128, :])
        w = (xf_free[ls],) if xf_free[ls] else ()
        p.op("sync", f_ld, waits=w, inc=xf, k=16)
        nx = p.n(xf)

        src = (lambda ls=ls: xf4[:, ls * D:(ls + 1) * D])
        sq_waits = [(xf, nx)]
        if tps_free[slot]:
            sq_waits.append((g["tr"], p.n(g["tr"])))
        r_ap, mr_ap = emit_ln_stats(
            p, src, st, 0, ((xf, nx),), sq_waits,
            lambda slot=slot: u_tm[:, slot * D:(slot + 1) * D])

        def f_ap(v, S_, slot=slot, ls=ls, r=r_ap, mr=mr_ap):
            return v.tensor_scalar(out=u_tm[:, slot * D:(slot + 1) * D],
                                   in0=xf4[:, ls * D:(ls + 1) * D],
                                   scalar1=r(), scalar2=mr(),
                                   op0=ALU.mult, op1=ALU.add)
        p.op("vector", f_ap, n=1)
        n_ua = p.n("sv")
        xf_free[ls] = ("sv", n_ua)

        tr_waits = [("sv", n_ua), AUXW]
        if tps_free[slot]:
            tr_waits.append(("sv", tps_free[slot]))

        def f_tr(t, S_, slot=slot):
            last = None
            for j in range(n_ic):
                last = t.transpose(
                    sb_ap(TPS[slot], j * 128, [[1, 128]]),
                    u_tm[:, slot * D + j * 128: slot * D + (j + 1) * 128],
                    idb_sb.ap())
            return last
        p.op("tensor", f_tr, waits=tr_waits, inc=g["tr"])

        def f_uev(v, S_, slot=slot):
            return v.tensor_copy(out=u0s[:, slot * D:(slot + 1) * D],
                                 in_=TPS[slot].ap()[:, 0:D])
        p.op("vector", f_uev, waits=((g["tr"], p.n(g["tr"])),), n=1)
        tps_free[slot] = p.n("sv")
        n_uev = p.n("sv")

        sg = SS[kt % 2]
        mm_waits = [("sv", n_uev), (g["wld"], WLDN)]
        if lat_ev[kt % 2]:
            mm_waits.append(("sv", lat_ev[kt % 2]))

        def f_mlat(t, S_, slot=slot, sg=sg):
            last = None
            for lc in range(2):
                co = lc * 512
                t.matmul(sg[:, co:co + 128], bl_sb[0:1, lc * 128:(lc + 1) * 128],
                         on_sb[0:1, 0:128], start=True, stop=False)
                for ic in range(n_ic):
                    last = t.matmul(
                        sg[:, co:co + 128],
                        wl_sb[:, lc * D + ic * 128: lc * D + (ic + 1) * 128],
                        u0s[:, slot * D + ic * 128: slot * D + (ic + 1) * 128],
                        start=False, stop=(ic == n_ic - 1))
            return last
        p.op("tensor", f_mlat, waits=mm_waits, inc=g["mlat"])

        def f_lev(v, S_, kt=kt, sg=sg):
            v.tensor_copy(out=latent[:, kt * 128:(kt + 1) * 128],
                          in_=sg[:, 0:128])
            return v.tensor_copy(out=latent[:, S + kt * 128: S + (kt + 1) * 128],
                                 in_=sg[:, 512:640])
        p.op("vector", f_lev, waits=((g["mlat"], p.n(g["mlat"])),), n=2)
        lat_ev[kt % 2] = p.n("sv")
    lat_done = p.n("sv")
    mlat_done = p.n(g["mlat"])

    xq_free = [None, None]
    qtps_free = [0, 0]
    q_ev = [0, 0]
    for qc in range(nqc):
        uq_start = p.n("sv")
        for t4 in range(4):
            i = qc * 4 + t4
            slot = i % 2
            xqs = g[f"xq{slot}"]

            def f_ld(e, S_, i=i, slot=slot):
                return e.dma_start(out=xf_t[:, slot * D:(slot + 1) * D],
                                   in_=xq[i * 128:(i + 1) * 128, :])
            w = [xq_free[slot]] if xq_free[slot] else []
            if i < 2:
                w.append(xf_free[slot])
            p.op("sync", f_ld, waits=w, inc=xqs, k=16)
            nx = p.n(xqs)

            src = (lambda slot=slot: xf_t[:, slot * D:(slot + 1) * D])
            sq_waits = [(xqs, nx)]
            if qtps_free[slot] or tps_free[slot]:
                sq_waits.append((g["tr"], p.n(g["tr"])))
            r_ap, mr_ap = emit_ln_stats(
                p, src, st, 0, ((xqs, nx),), sq_waits,
                lambda slot=slot: u_tm[:, slot * D:(slot + 1) * D])

            def f_ap(v, S_, slot=slot, r=r_ap, mr=mr_ap):
                return v.tensor_scalar(out=u_tm[:, slot * D:(slot + 1) * D],
                                       in0=xf_t[:, slot * D:(slot + 1) * D],
                                       scalar1=r(), scalar2=mr(),
                                       op0=ALU.mult, op1=ALU.add)
            p.op("vector", f_ap, n=1)
            n_ua = p.n("sv")
            xq_free[slot] = ("sv", n_ua)

            tr_waits = [("sv", n_ua), AUXW]
            gprev = qtps_free[slot] or tps_free[slot]
            if gprev:
                tr_waits.append(("sv", gprev))
            if q_ev[0] or q_ev[1]:
                tr_waits.append((g["mq"], p.n(g["mq"])))

            def f_tr(t, S_, slot=slot):
                last = None
                for j in range(n_ic):
                    last = t.transpose(
                        sb_ap(TPS[slot], j * 128, [[1, 128]]),
                        u_tm[:, slot * D + j * 128: slot * D + (j + 1) * 128],
                        idb_sb.ap())
                return last
            p.op("tensor", f_tr, waits=tr_waits, inc=g["tr"])

            def f_uev(v, S_, slot=slot, t4=t4):
                return v.tensor_copy(
                    out=sb_ap(u0qs, t4 * 128, [[512, n_ic], [1, 128]]),
                    in_=sb_ap(TPS[slot], 0, [[128, n_ic], [1, 128]]))
            p.op("vector", f_uev, waits=((g["tr"], p.n(g["tr"])),), n=1)
            qtps_free[slot] = p.n("sv")
        uq_done = p.n("sv")

        for oc in range(n_ic):
            pb = p.n(g["mq"]) % 2
            mm_waits = [("sv", uq_done), (g["wld"], WLDN)]
            if q_ev[pb]:
                mm_waits.append(("sv", q_ev[pb]))

            def f_mq(t, S_, oc=oc, pb=pb):
                ps = SS[pb]
                t.matmul(ps[:, 0:512], bq_sb[0:1, oc * 128:(oc + 1) * 128],
                         on_sb[0:1, 0:512], start=True, stop=False)
                last = None
                for ic in range(n_ic):
                    last = t.matmul(
                        ps[:, 0:512],
                        wq_sb[:, oc * D + ic * 128: oc * D + (ic + 1) * 128],
                        u0qs[:, ic * 512:(ic + 1) * 512],
                        start=False, stop=(ic == n_ic - 1))
                return last
            p.op("tensor", f_mq, waits=mm_waits, inc=g["mq"])

            def f_qev(v, S_, oc=oc, qc=qc, pb=pb):
                return v.tensor_scalar(
                    out=q_fm[:, oc * TQ + qc * 512: oc * TQ + (qc + 1) * 512],
                    in0=SS[pb][:, 0:512], scalar1=SCALE0, scalar2=None,
                    op0=ALU.mult)
            p.op("vector", f_qev, waits=((g["mq"], p.n(g["mq"])),), n=1)
            q_ev[pb] = p.n("sv")
    q_done = p.n("sv")
    mq_done = p.n(g["mq"])

    TPK = TPS[0].bitcast(F32)
    TPV = TPS[1].bitcast(F32)
    kev_free = [0]
    vev_free = [0]
    k_ready = [0]
    v_ready = [0]
    ao_written = 0

    def f_vones(v, S_):
        return v.memset(sb_ap(v_sb, 64, [[65, 2 * nkt], [1, 1]]), 1.0)
    p.op("vector", f_vones, n=1)

    def emit_k(hp, kb):
        mm_waits = [("sv", kev_free[0] if kev_free[0] else q_done),
                    ("sv", lat_done), (g["wld"], WLDN)]

        def f_mk(t, S_, hp=hp, kb=kb):
            t.matmul(TPK[:, 0:512], bk_sb[0:1, hp * 128:(hp + 1) * 128],
                     on_sb[0:1, 0:512], start=True, stop=False)
            last = None
            for lc in range(2):
                last = t.matmul(
                    TPK[:, 0:512],
                    wk_sb[:, hp * 256 + lc * 128: hp * 256 + (lc + 1) * 128],
                    latent[:, lc * S + kb * 512: lc * S + (kb + 1) * 512],
                    start=False, stop=(lc == 1))
            return last
        p.op("tensor", f_mk, waits=mm_waits, inc=g["mk"])

        def f_kev(v, S_, kb=kb):
            return v.tensor_copy(out=k_sb[:, kb * 512:(kb + 1) * 512],
                                 in_=TPK[:, 0:512])
        p.op("vector", f_kev, waits=((g["mk"], p.n(g["mk"])),), n=1)
        kev_free[0] = p.n("sv")
        k_ready[0] = p.n("sv")

    def emit_v(hp, kg):
        mm_waits = [("sv", vev_free[0] if vev_free[0] else q_done),
                    ("sv", lat_done), (g["wld"], WLDN)]

        def f_mv(t, S_, hp=hp, kg=kg):
            last = None
            for k4 in range(4):
                kt = kg * 4 + k4
                co = k4 * 128
                t.matmul(TPV[:, co:co + 128], on_sb[0:1, 0:128],
                         bv_sb[0:1, hp * 128:(hp + 1) * 128],
                         start=True, stop=False)
                for lc in range(2):
                    last = t.matmul(
                        TPV[:, co:co + 128],
                        latent[:, lc * S + kt * 128: lc * S + (kt + 1) * 128],
                        wv_sb[:, lc * D + hp * 128: lc * D + (hp + 1) * 128],
                        start=False, stop=(lc == 1))
            return last
        p.op("tensor", f_mv, waits=mm_waits, inc=g["mv"])

        def f_vev(v, S_, kg=kg):
            return v.tensor_copy(
                out=sb_ap(v_sb, kg * 4 * 130, [[130, 4], [65, 2], [1, 64]]),
                in_=bass.AP(TPV, 0, [[512, 128], [128, 4], [64, 2], [1, 64]]))
        p.op("vector", f_vev, waits=((g["mv"], p.n(g["mv"])),), n=1)
        vev_free[0] = p.n("sv")
        v_ready[0] = p.n("sv")

    def emit_norm(hp, qc):
        nonlocal rb_done, ao_written
        n_mav = p.n(g["mav"])

        def f_an(v, S_):
            v.tensor_copy(out=aon_sb[0:64, :], in_=P[0][0:64, 0:512])
            v.tensor_copy(out=aon_sb[64:128, :], in_=P[1][0:64, 0:512])
            v.tensor_copy(out=rr_sb[0:1, 0:512], in_=P[0][64:65, 0:512])
            return v.tensor_copy(out=rr_sb[0:1, 512:1024],
                                 in_=P[1][64:65, 0:512])
        p.op("vector", f_an, waits=((g["mav"], n_mav),), n=4)
        n_an = p.n("sv")

        def f_bc(t, S_):
            t.matmul(P[0][0:64, 0:512], on_sb[0:1, 0:64],
                     rr_sb[0:1, 0:512], start=True, stop=True)
            return t.matmul(P[1][0:64, 0:512], on_sb[0:1, 0:64],
                            rr_sb[0:1, 512:1024], start=True, stop=True)
        p.op("tensor", f_bc, waits=(("sv", n_an),), inc=g["mbc"])

        def f_rb(v, S_):
            v.tensor_copy(out=rb_sb[0:64, :], in_=P[0][0:64, 0:512])
            return v.tensor_copy(out=rb_sb[64:128, :], in_=P[1][0:64, 0:512])
        p.op("vector", f_rb, waits=((g["mbc"], p.n(g["mbc"])),), n=2)
        rb_done = p.n("sv")

        def f_rbr(v, S_):
            return v.reciprocal(out=rbr_sb.ap(), in_=rb_sb.ap())
        p.op("vector", f_rbr, n=1)

        def f_ao(v, S_, hp=hp, qc=qc):
            v.tensor_tensor(
                out=g_ap(ao_fm, 0, 64, hp * TQ + qc * 512, [[1, 512]]),
                in0=aon_sb[0:64, :], in1=rbr_sb[0:64, :], op=ALU.mult)
            return v.tensor_tensor(
                out=g_ap(ao_fm, 64, 64, hp * TQ + qc * 512, [[1, 512]]),
                in0=aon_sb[64:128, :], in1=rbr_sb[64:128, :], op=ALU.mult)
        ao_waits = []
        if hp == 0 and qc == 0:
            ao_waits = [(g["mq"], mq_done), (g["mlat"], mlat_done)]
        p.op("vector", f_ao, waits=ao_waits, n=2)
        ao_written = p.n("sv")

    for j in range(nkb):
        emit_k(0, j)
        emit_v(0, j)

    av_read = [0] * 8
    prev_gen_exp = [0, 0]
    rb_done = 0
    esl = [0]
    pending = [None]
    for hp in range(npair):
        pk_ready, pv_ready = k_ready[0], v_ready[0]
        for qc in range(nqc):
            interleave = (qc == nqc - 1) and (hp + 1 < npair)
            kts = list(range(nkt - 1, -1, -1)) if interleave \
                else list(range(nkt))
            exp_done = [0] * nkt
            exp_slot = [0] * nkt
            n_av = [0]

            def emit_av(j, qc=qc, pv_ready=pv_ready, n_av=n_av,
                        exp_done=exp_done, exp_slot=exp_slot):
                sl = exp_slot[j]
                av_waits = [("ss", exp_done[j])]
                if n_av[0] == 0:
                    av_waits.append(("sv", rb_done if rb_done else ao_written))
                    if qc == 0:
                        av_waits.append(("sv", pv_ready))
                first = (n_av[0] == 0)
                lastf = (n_av[0] == nkt - 1)

                def f_av(t, S_, j=j, sl=sl, first=first, lastf=lastf):
                    t.matmul(P[0][0:65, 0:512],
                             v_sb[:, j * 130: j * 130 + 65],
                             pexp[:, sl * 1024: sl * 1024 + 512],
                             start=first, stop=lastf)
                    return t.matmul(
                        P[1][0:65, 0:512],
                        v_sb[:, j * 130 + 65: j * 130 + 130],
                        pexp[:, sl * 1024 + 512: sl * 1024 + 1024],
                        start=first, stop=lastf)
                p.op("tensor", f_av, waits=av_waits, inc=g["mav"])
                av_read[sl] = p.n(g["mav"])
                n_av[0] += 1

            for i, kt in enumerate(kts):
                gen = kt % 2
                sg = SS[gen]
                sc_waits = [("sv", pk_ready)]
                if prev_gen_exp[gen]:
                    sc_waits.append(("ss", prev_gen_exp[gen]))

                def f_sc(t, S_, hp=hp, qc=qc, kt=kt, sg=sg):
                    t.matmul(sg[:, 0:512],
                             k_sb[0:64, kt * 128:(kt + 1) * 128],
                             g_ap(q_fm, 0, 64, hp * TQ + qc * 512, [[1, 512]]),
                             start=True, stop=True)
                    return t.matmul(
                        sg[:, 512:1024],
                        k_sb[64:128, kt * 128:(kt + 1) * 128],
                        g_ap(q_fm, 64, 64, hp * TQ + qc * 512, [[1, 512]]),
                        start=True, stop=True)
                p.op("tensor", f_sc, waits=sc_waits, inc=g["msc"])
                n_msc = p.n(g["msc"])

                if interleave and kt % 4 == 0:
                    emit_k(hp + 1, kt // 4)

                if i == 1 and pending[0]:
                    pending[0]()
                    pending[0] = None

                if i >= 4:
                    j = kts[i - 4]
                    emit_av(j)
                    if interleave and j % 4 == 0:
                        emit_v(hp + 1, j // 4)

                psl = esl[0] % 8
                esl[0] += 1
                exp_slot[kt] = psl
                exp_waits = [(g["msc"], n_msc)]
                if av_read[psl]:
                    exp_waits.append((g["mav"], av_read[psl]))

                def f_exp(s, S_, sg=sg, psl=psl):
                    return s.activation(out=pexp[:, psl * 1024:(psl + 1) * 1024],
                                        in_=sg.ap(), func=AF.Exp)
                p.op("scalar", f_exp, waits=exp_waits, n=1, no_pre=True)
                exp_done[kt] = p.n("ss")
                prev_gen_exp[gen] = p.n("ss")

            def flush_tail(hp=hp, qc=qc, kts=kts, interleave=interleave,
                           emit_av=emit_av):
                for i in range(nkt - 4, nkt):
                    j = kts[i]
                    emit_av(j)
                    if interleave and j % 4 == 0:
                        emit_v(hp + 1, j // 4)
                emit_norm(hp, qc)
            pending[0] = flush_tail
    pending[0]()
    pending[0] = None
    flash_done_sv = p.n("sv")

    if wo_late:
        n_mk_all, n_mv_all = p.n(g["mk"]), p.n(g["mv"])

        def f_wo(e, S_):
            return e.dma_start(out=wo_sb.ap(),
                               in_=wo.rearrange("a p c -> p a c"))
        p.op("gpsimd", f_wo, waits=((g["mk"], n_mk_all), (g["mv"], n_mv_all)),
             inc=g["wld"], k=16, n=0)
    WLDN2 = p.n(g["wld"])

    yev_free = [0, 0]
    for qt in range(TQ // 128):
        slot = qt % 2
        for oh in range(2):
            pb = oh
            mm_waits = [("sv", flash_done_sv), (g["wld"], WLDN2)]
            if yev_free[pb]:
                mm_waits.append(("sv", yev_free[pb]))

            def f_my(t, S_, qt=qt, oh=oh, pb=pb):
                ps = SS[pb]
                t.matmul(ps[:, 0:512], on_sb[0:1, 0:128],
                         bo_sb[0:1, oh * 512:(oh + 1) * 512],
                         start=True, stop=False)
                last = None
                for ic in range(n_ic):
                    last = t.matmul(
                        ps[:, 0:512],
                        ao_fm[:, ic * TQ + qt * 128: ic * TQ + (qt + 1) * 128],
                        wo_sb[:, ic * D + oh * 512: ic * D + (oh + 1) * 512],
                        start=False, stop=(ic == n_ic - 1))
                return last
            p.op("tensor", f_my, waits=mm_waits, inc=g["my"])

            def f_yev(v, S_, oh=oh, pb=pb):
                return v.tensor_copy(out=y_tm[:, oh * 512:(oh + 1) * 512],
                                     in_=SS[pb][:, 0:512])
            p.op("vector", f_yev, waits=((g["my"], p.n(g["my"])),), n=1)
            yev_free[pb] = p.n("sv")
        n_yev = p.n("sv")

        xqs = g[f"xq{slot}"]

        def f_xr(e, S_, qt=qt, slot=slot):
            return e.dma_start(out=xf_t[:, slot * D:(slot + 1) * D],
                               in_=xq[qt * 128:(qt + 1) * 128, :])
        w = (xq_free[slot],) if xq_free[slot] else ()
        p.op("sync", f_xr, waits=w, inc=xqs, k=16)
        nx = p.n(xqs)

        r_ap, mr_ap = emit_ln_stats(p, lambda: y_tm.ap(), st, 8,
                                    (("sv", n_yev),), (("sv", n_yev),),
                                    lambda: u_tm[:, 0:D])

        so = g[f"so{slot}"]

        def f_res(v, S_, slot=slot, r=r_ap, mr=mr_ap):
            v.tensor_scalar(out=xo_t[:, slot * D:(slot + 1) * D],
                            in0=y_tm.ap(), scalar1=r(), scalar2=mr(),
                            op0=ALU.mult, op1=ALU.add)
            return v.tensor_tensor(out=xo_t[:, slot * D:(slot + 1) * D],
                                   in0=xo_t[:, slot * D:(slot + 1) * D],
                                   in1=xf_t[:, slot * D:(slot + 1) * D],
                                   op=ALU.add)
        res_waits = [(xqs, nx)]
        if p.n(so) > 0:
            res_waits.append((so, p.n(so)))
        p.op("vector", f_res, waits=res_waits, n=2)
        xq_free[slot] = ("sv", p.n("sv"))

        def f_sto(e, S_, qt=qt, slot=slot):
            return e.dma_start(out=xout[qt * 128:(qt + 1) * 128, :],
                               in_=xo_t[:, slot * D:(slot + 1) * D])
        p.op("sync", f_sto, waits=(("sv", p.n("sv")),), inc=so, k=16)

    p.op("sync", lambda e, S_: e.nop(),
         waits=((g["so0"], p.n(g["so0"])), (g["so1"], p.n(g["so1"]))))


def build_l0(S=8192, TQ=2048):
    nc = bass.Bass("TRN2", target_bir_lowering=False)
    n_ic = D // 128
    cio = {
        "xfull": nc.dram_tensor("xfull", [S, D], F32, kind="ExternalInput").ap(),
        "xq": nc.dram_tensor("xq", [TQ, D], F32, kind="ExternalInput").ap(),
        "wq": nc.dram_tensor("wq", [n_ic, 128, D], F32, kind="ExternalInput").ap(),
        "wl": nc.dram_tensor("wl", [2, 128, D], F32, kind="ExternalInput").ap(),
        "wk": nc.dram_tensor("wk", [n_ic, 128, 256], F32, kind="ExternalInput").ap(),
        "wv": nc.dram_tensor("wv", [2, 128, D], F32, kind="ExternalInput").ap(),
        "wo": nc.dram_tensor("wo", [n_ic, 128, D], F32, kind="ExternalInput").ap(),
        "bq": nc.dram_tensor("bq", [1, D], F32, kind="ExternalInput").ap(),
        "bl": nc.dram_tensor("bl", [1, 256], F32, kind="ExternalInput").ap(),
        "bk": nc.dram_tensor("bk", [1, D], F32, kind="ExternalInput").ap(),
        "bv": nc.dram_tensor("bv", [1, D], F32, kind="ExternalInput").ap(),
        "bo": nc.dram_tensor("bo", [1, D], F32, kind="ExternalInput").ap(),
        "xout": nc.dram_tensor("xout", [TQ, D], F32, kind="ExternalOutput").ap(),
    }
    id_bf = nc.inline_tensor(np.eye(128, dtype=ml_dtypes.bfloat16), name="id_bf")
    ones_bf = nc.inline_tensor(np.ones((1, 512), dtype=ml_dtypes.bfloat16),
                               name="ones_bf")
    idb_sb = nc.alloc_sbuf_tensor("idb_sb", [128, 128], BF16)
    on_sb = nc.alloc_sbuf_tensor("on_sb", [1, 512], BF16)
    cio["idb"], cio["onb"] = idb_sb, on_sb
    cio["P2"] = [nc.alloc_psum_tensor(f"P{i}", [128, 512], F32) for i in range(2)]
    cio["SS"] = [nc.alloc_psum_tensor(f"S{i}", [128, 1024], F32) for i in range(2)]
    cio["TPS"] = [nc.alloc_psum_tensor(f"Tp{i}", [128, 1024], BF16) for i in range(2)]
    ovl = n_ic * D + 2 * D + n_ic * 512 + 2 * D
    extra = 0 if n_ic * TQ >= ovl else ovl + 256
    wo_cols = 0 if 2 * S >= n_ic * D else n_ic * D
    ar = Arena(nc, 2 * S + n_ic * TQ + S + (S // 128) * 130 + n_ic * TQ +
               2 * D + 2 * 512 + 1024 + 512 + n_ic * 256 + 2 * D + wo_cols +
               5 * D + 256 + 8192 + extra,
               7408)
    p = Prog(nc)
    p.sem("common_aux")

    def f_id1(e, S_):
        return e.dma_start(out=idb_sb.ap(), in_=id_bf.ap())
    p.op("sync", f_id1, inc="common_aux", k=16)

    def f_on(e, S_):
        return e.dma_start(out=on_sb.ap(), in_=ones_bf.ap())
    p.op("sync", f_on, inc="common_aux", k=16)
    cio["aux_n"] = p.n("common_aux")
    emit_l0(p, nc, ar, cio, "L_", S=S, TQ=TQ)
    with ExitStack() as es:
        p.emit(es)
    return nc


def l0_host_inputs(Wq, bq, Wl, bl, Wk, bk, Wv, bv, Wo, bo):
    f = np.float32
    return {
        "wq": swz_lhsT(np.ascontiguousarray(np.asarray(Wq).T)),
        "wl": swz_lhsT(np.ascontiguousarray(np.asarray(Wl).T)),
        "wk": swz_lhsT(np.ascontiguousarray(np.asarray(Wk).T)),
        "wv": np.ascontiguousarray(np.asarray(Wv).reshape(2, 128, D), f),
        "wo": np.ascontiguousarray(np.asarray(Wo).reshape(8, 128, D), f),
        "bq": np.ascontiguousarray(np.asarray(bq).reshape(1, D), f),
        "bl": np.ascontiguousarray(np.asarray(bl).reshape(1, 256), f),
        "bk": np.ascontiguousarray(np.asarray(bk).reshape(1, D), f),
        "bv": np.ascontiguousarray(np.asarray(bv).reshape(1, D), f),
        "bo": np.ascontiguousarray(np.asarray(bo).reshape(1, D), f),
    }





def build_fused(S=8192, TQ=2048, TC=512):
    n_ic = D // 128
    n_oc = DH // 128
    nc = bass.Bass("TRN2", target_bir_lowering=False)

    def ext(nm, shape):
        return nc.dram_tensor(nm, shape, F32, kind="ExternalInput").ap()

    l0 = {
        "xfull": ext("xfull", [S, D]), "xq": ext("xq", [TQ, D]),
        "wq": ext("l0_wq", [n_ic, 128, D]), "wl": ext("l0_wl", [2, 128, D]),
        "wk": ext("l0_wk", [n_ic, 128, 256]), "wv": ext("l0_wv", [2, 128, D]),
        "wo": ext("l0_wo", [n_ic, 128, D]),
        "bq": ext("l0_bq", [1, D]), "bl": ext("l0_bl", [1, 256]),
        "bk": ext("l0_bk", [1, D]), "bv": ext("l0_bv", [1, D]),
        "bo": ext("l0_bo", [1, D]),
    }
    ffn_shapes = {
        "w1s": [n_oc, 128, n_ic * 128], "w2s": [n_oc, 128, n_ic * 128],
        "h1s": [n_oc, 128, n_oc * 128], "h2s": [n_oc, 128, n_oc * 128],
        "w3s": [n_ic, 128, n_oc * 128],
        "sc12": [128, 2 * n_oc], "sch": [128, 2 * n_oc], "sc3": [128, n_ic],
    }
    f0 = {k: ext("f0_" + k, sh) for k, sh in ffn_shapes.items()}
    f1 = {k: ext("f1_" + k, sh) for k, sh in ffn_shapes.items()}
    l1 = {
        "kvw": ext("l1_kvw", [2, 128, n_ic * 128]),
        "gam": ext("l1_gam", [128, n_ic]),
        "qw": ext("l1_qw", [128, n_ic * 128]),
        "wout": ext("l1_wout", [128, D]),
    }
    xout = nc.dram_tensor("xout", [TQ, D], F32, kind="ExternalOutput").ap()

    x1 = nc.dram_tensor("x1", [TQ, D], F32, kind="Internal").ap()
    x2 = nc.dram_tensor("x2", [TQ, D], F32, kind="Internal").ap()
    x3 = nc.dram_tensor("x3", [TQ, D], F32, kind="Internal").ap()
    cc_in = nc.dram_tensor("cc_in", [17, NH1 * DPHI], F32, kind="Internal")
    cc_out = nc.dram_tensor("cc_out", [17, NH1 * DPHI], F32, kind="Internal")

    id_bf = nc.inline_tensor(np.eye(128, dtype=ml_dtypes.bfloat16), name="id_bf")
    id_f32 = nc.inline_tensor(np.eye(128, dtype=np.float32), name="id_f32")
    ones_bf = nc.inline_tensor(np.ones((1, 512), dtype=ml_dtypes.bfloat16),
                               name="ones_bf")
    idb_sb = nc.alloc_sbuf_tensor("idb_sb", [128, 128], BF16)
    idf_sb = nc.alloc_sbuf_tensor("idf_sb", [128, 128], F32)
    on_sb = nc.alloc_sbuf_tensor("on_sb", [1, 512], BF16)

    P2 = [nc.alloc_psum_tensor(f"P{i}", [128, 512], F32) for i in range(2)]
    SS = [nc.alloc_psum_tensor(f"S{i}", [128, 1024], F32) for i in range(2)]
    TPS = [nc.alloc_psum_tensor(f"Tp{i}", [128, 1024], BF16) for i in range(2)]
    P4 = [P2[0], P2[1], View(SS[0], 0, 512), View(SS[1], 0, 512)]
    P6 = [P2[0], P2[1], View(SS[0], 0, 512), View(SS[0], 512, 512),
          View(SS[1], 0, 512), View(SS[1], 512, 512)]

    ar = Arena(nc, 90496, 7408)
    p = Prog(nc)
    p.sem("common_aux", "ccs")

    for dst, srcap in [(idb_sb, id_bf.ap()), (idf_sb, id_f32.ap()),
                       (on_sb, ones_bf.ap())]:
        def f_c(e, S_, dst=dst, srcap=srcap):
            return e.dma_start(out=dst.ap(), in_=srcap)
        p.op("sync", f_c, inc="common_aux", k=16)
    aux_n = p.n("common_aux")
    common = {"idb": idb_sb, "idf": idf_sb, "onb": on_sb, "aux_n": aux_n}

    cio = dict(l0)
    cio.update(common)
    cio["xout"] = x1
    cio["P2"], cio["SS"], cio["TPS"] = P2, SS, TPS
    emit_l0(p, nc, ar, cio, "L_", S=S, TQ=TQ)
    p.barrier()
    ar.reset()

    cio = dict(f0)
    cio.update(common)
    cio["xin"], cio["xout"] = x1, x2
    cio["P"], cio["TPS"] = P6, TPS
    emit_ffn(p, nc, ar, cio, "F_", T=TQ, TC=TC)
    p.barrier()
    ar.reset()

    cio = {"xin": x2, "kvw": l1["kvw"], "gam": l1["gam"], "kvp": cc_in.ap()}
    cio.update(common)
    cio["P"], cio["TPS"] = P4, TPS
    emit_l1a(p, nc, ar, cio, "A_", T=TQ)

    def f_cc(e, S_):
        return e.collective_compute(
            "AllReduce", ALU.add,
            replica_groups=[[0, 1, 2, 3], [4, 5, 6, 7]],
            ins=[cc_in.ap().opt()], outs=[cc_out.ap().opt()])
    p.op("gpsimd", f_cc, waits=(("A_sto", p.n("A_sto")),), inc="ccs", k=1, n=0)
    p.barrier()
    ar.reset()

    cio = {"xin": x2, "qw": l1["qw"], "gam": l1["gam"], "wout": l1["wout"],
           "xout": x3, "cc_out": cc_out.ap(), "ccs_n": 1}
    cio.update(common)
    cio["P"], cio["TPS"] = P4, TPS
    cio["PH"] = TPS
    emit_l1b(p, nc, ar, cio, "B_", T=TQ)
    p.barrier()
    ar.reset()

    cio = dict(f1)
    cio.update(common)
    cio["xin"], cio["xout"] = x3, xout
    cio["P"], cio["TPS"] = P6, TPS
    emit_ffn(p, nc, ar, cio, "G_", T=TQ, TC=TC)

    with ExitStack() as es:
        p.emit(es)
    return nc


def kernel_fused(inputs):
    x0 = np.ascontiguousarray(np.asarray(inputs["L0_attn_x"], dtype=np.float32))
    B, S_, D_ = x0.shape
    TQ = (B * S_) // NCORES
    cores = list(range(NCORES))
    per_batch = NCORES // B

    h0 = l0_host_inputs(inputs["L0_attn_Wq"], inputs["L0_attn_bq"],
                        inputs["L0_attn_Wl"], inputs["L0_attn_bl"],
                        inputs["L0_attn_Wk"], inputs["L0_attn_bk"],
                        inputs["L0_attn_Wv"], inputs["L0_attn_bv"],
                        inputs["L0_attn_Wo"], inputs["L0_attn_bo"])
    hf0 = ffn_host_inputs(inputs["L0_ffn_w1_w"], inputs["L0_ffn_w1_s"],
                          inputs["L0_ffn_w2_w"], inputs["L0_ffn_w2_s"],
                          inputs["L0_ffn_h1_w"], inputs["L0_ffn_h1_s"],
                          inputs["L0_ffn_h2_w"], inputs["L0_ffn_h2_s"],
                          inputs["L0_ffn_w3_w"], inputs["L0_ffn_w3_s"])
    hf1 = ffn_host_inputs(inputs["L1_ffn_w1_w"], inputs["L1_ffn_w1_s"],
                          inputs["L1_ffn_w2_w"], inputs["L1_ffn_w2_s"],
                          inputs["L1_ffn_h1_w"], inputs["L1_ffn_h1_s"],
                          inputs["L1_ffn_h2_w"], inputs["L1_ffn_h2_s"],
                          inputs["L1_ffn_w3_w"], inputs["L1_ffn_w3_s"])
    h1 = l1_host_inputs(inputs["L1_attn_gamma"], inputs["L1_attn_w_q"],
                        inputs["L1_attn_w_kv"], inputs["L1_attn_w_out"])

    shared = {}
    for k in ["wq", "wl", "wk", "wv", "wo", "bq", "bl", "bk", "bv", "bo"]:
        shared["l0_" + k] = h0[k]
    for k, v in hf0.items():
        shared["f0_" + k] = v
    for k, v in hf1.items():
        shared["f1_" + k] = v
    shared["l1_kvw"] = h1["kvw"]
    shared["l1_gam"] = h1["gam"]
    shared["l1_qw"] = h1["qw"]
    shared["l1_wout"] = h1["wout"]

    nc = build_fused(S=S_, TQ=TQ, TC=512)
    in_maps = []
    for c in cores:
        b, qi = c // per_batch, c % per_batch
        m = dict(shared)
        m["xfull"] = x0[b]
        m["xq"] = np.ascontiguousarray(x0[b][qi * TQ:(qi + 1) * TQ])
        in_maps.append(m)
    res = run_bass_kernel_spmd(nc, in_maps, core_ids=cores)
    return np.concatenate([res.results[c]["xout"] for c in cores],
                          axis=0).reshape(B, S_, D_)



def kernel(**inputs):
    return kernel_fused(inputs)


def kernel_multilaunch(L0_attn_x, L0_attn_Wq, L0_attn_bq, L0_attn_Wl, L0_attn_bl,
           L0_attn_Wk, L0_attn_bk, L0_attn_Wv, L0_attn_bv, L0_attn_Wo,
           L0_attn_bo, L0_ffn_w1_w, L0_ffn_w1_s, L0_ffn_w2_w, L0_ffn_w2_s,
           L0_ffn_h1_w, L0_ffn_h1_s, L0_ffn_h2_w, L0_ffn_h2_s, L0_ffn_w3_w,
           L0_ffn_w3_s, L1_attn_gamma, L1_attn_w_q, L1_attn_w_kv,
           L1_attn_w_out, L1_ffn_w1_w, L1_ffn_w1_s, L1_ffn_w2_w, L1_ffn_w2_s,
           L1_ffn_h1_w, L1_ffn_h1_s, L1_ffn_h2_w, L1_ffn_h2_s, L1_ffn_w3_w,
           L1_ffn_w3_s):
    x0 = np.ascontiguousarray(np.asarray(L0_attn_x, dtype=np.float32))
    B, S_, D_ = x0.shape
    TQ = (B * S_) // NCORES
    cores = list(range(NCORES))
    per_batch = NCORES // B

    h0 = l0_host_inputs(L0_attn_Wq, L0_attn_bq, L0_attn_Wl, L0_attn_bl,
                        L0_attn_Wk, L0_attn_bk, L0_attn_Wv, L0_attn_bv,
                        L0_attn_Wo, L0_attn_bo)
    nc0 = build_l0(S=S_, TQ=TQ)
    in_maps = []
    for c in cores:
        b, qi = c // per_batch, c % per_batch
        m = dict(h0)
        m["xfull"] = x0[b]
        m["xq"] = np.ascontiguousarray(x0[b][qi * TQ:(qi + 1) * TQ])
        in_maps.append(m)
    res = run_bass_kernel_spmd(nc0, in_maps, core_ids=cores)
    x1 = [res.results[c]["xout"] for c in cores]

    hf0 = ffn_host_inputs(L0_ffn_w1_w, L0_ffn_w1_s, L0_ffn_w2_w, L0_ffn_w2_s,
                          L0_ffn_h1_w, L0_ffn_h1_s, L0_ffn_h2_w, L0_ffn_h2_s,
                          L0_ffn_w3_w, L0_ffn_w3_s)
    ncf = build_ffn(T=TQ, TC=512)
    in_maps = [dict(hf0, xin=x1[c]) for c in cores]
    res = run_bass_kernel_spmd(ncf, in_maps, core_ids=cores)
    x2 = [res.results[c]["xout"] for c in cores]

    h1 = l1_host_inputs(L1_attn_gamma, L1_attn_w_q, L1_attn_w_kv, L1_attn_w_out)
    nca = build_l1a(T=TQ)
    in_maps = [{"xin": x2[c], "kvw": h1["kvw"], "gam": h1["gam"]} for c in cores]
    res = run_bass_kernel_spmd(nca, in_maps, core_ids=cores)
    kvp = [res.results[c]["kvp"] for c in cores]

    ncb = build_l1b(T=TQ)
    in_maps = []
    for c in cores:
        b = c // per_batch
        kvp4 = np.ascontiguousarray(
            np.stack(kvp[b * per_batch:(b + 1) * per_batch], axis=0))
        in_maps.append({"xin": x2[c], "qw": h1["qw"], "gam": h1["gam"],
                        "wout": h1["wout"], "kvp4": kvp4})
    res = run_bass_kernel_spmd(ncb, in_maps, core_ids=cores)
    x3 = [res.results[c]["xout"] for c in cores]

    hf1 = ffn_host_inputs(L1_ffn_w1_w, L1_ffn_w1_s, L1_ffn_w2_w, L1_ffn_w2_s,
                          L1_ffn_h1_w, L1_ffn_h1_s, L1_ffn_h2_w, L1_ffn_h2_s,
                          L1_ffn_w3_w, L1_ffn_w3_s)
    in_maps = [dict(hf1, xin=x3[c]) for c in cores]
    res = run_bass_kernel_spmd(ncf, in_maps, core_ids=cores)
    x4 = [res.results[c]["xout"] for c in cores]

    return np.concatenate(x4, axis=0).reshape(B, S_, D_)


# version 19
# speedup vs baseline: 1.0196x; 1.0196x over previous
import numpy as np
import ml_dtypes
from contextlib import ExitStack

import concourse.bass as bass
import concourse.mybir as mybir
from concourse.bass_utils import run_bass_kernel_spmd

F32 = mybir.dt.float32
BF16 = mybir.dt.bfloat16
AF = mybir.ActivationFunctionType
ALU = mybir.AluOpType
AX = mybir.AxisListType

D = 1024
DH = 4096
EPS_LN = 1e-5
NCORES = 8
ENGS = ("sync", "gpsimd", "vector", "scalar", "tensor")
SELF = {"vector": "sv", "scalar": "ss", "gpsimd": "sp"}

COMPUTE = {
    "tensor_copy", "tensor_tensor", "tensor_scalar", "tensor_single_scalar",
    "scalar_tensor_tensor", "tensor_reduce", "reduce_sum", "reduce_max",
    "reciprocal", "activation", "memset", "memzero", "iota", "select",
    "bn_stats", "bn_aggr", "tensor_add", "tensor_sub", "tensor_mul",
    "tensor_max", "tensor_scalar_add", "tensor_scalar_sub",
    "tensor_scalar_mul", "tensor_scalar_max", "tensor_scalar_min",
    "square", "sqrt", "mul", "add", "copy", "sign",
    "tensor_relu", "tensor_tensor_scan", "affine_select", "stream_shuffle",
    "reciprocal_approx_fast", "reciprocal_approx_accurate",
}


class _Proxy:

    def __init__(self, eng, self_sem):
        self._e = eng
        self._sem = self_sem
        self.n = 0
        self._last = None

    def _pre(self):
        if self._last is not None:
            self._e.wait_ge(*self._last)

    def wait_ge(self, sem, val):
        return self._e.wait_ge(sem, val)

    def nop(self):
        return self._e.nop()

    def dma_start(self, *a, **k):
        self._pre()
        return self._e.dma_start(*a, **k)

    def dma_start_transpose(self, *a, **k):
        self._pre()
        return self._e.dma_start_transpose(*a, **k)

    def collective_compute(self, *a, **k):
        self._pre()
        return self._e.collective_compute(*a, **k)

    def __getattr__(self, name):
        attr = getattr(self._e, name)
        if name in COMPUTE and callable(attr) and self._sem is not None:
            def wrapped(*a, **k):
                self._pre()
                instr = attr(*a, **k)
                self.n += 1
                instr.then_inc(self._sem, 1)
                self._last = (self._sem, self.n)
                return instr
            return wrapped
        return attr


class Prog:

    def __init__(self, nc):
        self.nc = nc
        self.ops = {e: [] for e in ENGS}
        self.sem_names = ["sv", "ss", "sp"]
        self.cnt = {"sv": 0, "ss": 0, "sp": 0}

    def sem(self, *names):
        for n in names:
            self.sem_names.append(n)
            self.cnt[n] = 0

    def op(self, eng, fn, waits=(), inc=None, k=1, n=None, no_pre=False):
        if eng in SELF:
            assert n is not None, "declare instruction count"
            self.cnt[SELF[eng]] += n
        if inc is not None:
            self.cnt[inc] += k
        self.ops[eng].append((fn, tuple(waits), inc, k, no_pre))

    def n(self, name):
        return self.cnt[name]

    def barrier(self):
        snap = [(s, v) for s, v in self.cnt.items() if v > 0]
        for eng in ENGS:
            self.ops[eng].append((lambda e, S: None, tuple(snap), None, 1, False))

    def emit(self, es):
        nc = self.nc
        sems = {nm: es.enter_context(nc.semaphore(nm)) for nm in self.sem_names}
        block = es.enter_context(nc.Block())

        def body(eng_obj, eng_name):
            proxy = _Proxy(eng_obj, sems.get(SELF.get(eng_name)))
            for fn, waits, inc, k, no_pre in self.ops[eng_name]:
                for s, v in waits:
                    if v > 0:
                        eng_obj.wait_ge(sems[s], v)
                if no_pre:
                    proxy._last = None
                instr = fn(proxy, sems)
                if inc is not None:
                    assert instr is not None
                    instr.then_inc(sems[inc], k)
            if eng_name in SELF:
                assert proxy.n == self.cnt[SELF[eng_name]], \
                    f"{eng_name}: emitted {proxy.n} != scheduled {self.cnt[SELF[eng_name]]}"

        block.sync(lambda e: body(e, "sync"))
        block.gpsimd(lambda e: body(e, "gpsimd"))
        block.vector(lambda e: body(e, "vector"))
        block.scalar(lambda e: body(e, "scalar"))
        block.tensor(lambda e: body(e, "tensor"))
        return sems


def sb_ap(t, offset, dims):
    if isinstance(t, View):
        offset = offset + t.base
        t = t.t
    free = t.shape[1]
    return bass.AP(t, offset, [[free, 128]] + [list(d) for d in dims])


def g_ap(t, part_base, part_count, col_off, dims):
    if isinstance(t, View):
        base, tt = t.base, t.t
    else:
        base, tt = 0, t
    W = tt.shape[1]
    return bass.AP(tt, part_base * W + base + col_off,
                   [[W, part_count]] + [list(d) for d in dims])


class Arena:

    def __init__(self, nc, bf16_cols, f32_cols):
        self.t = {}
        self.t[BF16] = nc.alloc_sbuf_tensor("arena_bf", [128, bf16_cols], BF16)
        self.t[F32] = nc.alloc_sbuf_tensor("arena_f32", [128, f32_cols], F32)
        self.off = {BF16: 0, F32: 0}
        self.peak = {BF16: 0, F32: 0}

    def sb(self, name, cols, dt):
        o = (self.off[dt] + 15) // 16 * 16
        v = View(self.t[dt], o, cols)
        self.off[dt] = o + cols
        self.peak[dt] = max(self.peak[dt], self.off[dt])
        assert self.off[dt] <= self.t[dt].shape[1], \
            f"arena {dt} overflow at {name}: {self.off[dt]} > {self.t[dt].shape[1]}"
        return v

    def reset(self):
        self.off = {BF16: 0, F32: 0}


class View:

    def __init__(self, t, base, width):
        self.t = t
        self.base = base
        self.shape = (t.shape[0], width)

    def __getitem__(self, idx):
        ps, cs = idx
        a = self.base + (cs.start if cs.start is not None else 0)
        b = self.base + (cs.stop if cs.stop is not None else self.shape[1])
        return self.t[ps, a:b]

    def ap(self):
        return self.t[:, self.base:self.base + self.shape[1]]



def emit_ln_stats(p, src_ap_fn, st, cols, waits_v, waits_s, sq_out_fn):
    c0, c1, c2, c3, c4, c5, c6 = range(cols, cols + 7)

    def f_red(v, S):
        return v.reduce_sum(out=st[:, c0:c0 + 1], in_=src_ap_fn(), axis=AX.X)
    p.op("vector", f_red, waits=waits_v, n=1)

    def f_sq(s, S):
        return s.activation(out=sq_out_fn(), in_=src_ap_fn(), func=AF.Square,
                            accum_out=st[:, c1:c1 + 1])
    p.op("scalar", f_sq, waits=tuple(waits_s) + (("sv", p.n("sv")),), n=1)
    n_sq = p.n("ss")

    def f_fin(v, S):
        v.tensor_scalar(out=st[:, c2:c2 + 2], in0=st[:, c0:c0 + 2],
                        scalar1=1.0 / D, scalar2=None, op0=ALU.mult)
        v.tensor_tensor(out=st[:, c4:c4 + 1], in0=st[:, c2:c2 + 1],
                        in1=st[:, c2:c2 + 1], op=ALU.mult)
        v.tensor_tensor(out=st[:, c4:c4 + 1], in0=st[:, c3:c3 + 1],
                        in1=st[:, c4:c4 + 1], op=ALU.subtract)
        return v.tensor_scalar(out=st[:, c4:c4 + 1], in0=st[:, c4:c4 + 1],
                               scalar1=EPS_LN, scalar2=None, op0=ALU.add)
    p.op("vector", f_fin, waits=(("ss", n_sq),), n=4)
    n_var = p.n("sv")

    def f_sqrt(s, S):
        return s.activation(out=st[:, c5:c5 + 1], in_=st[:, c4:c4 + 1],
                            func=AF.Sqrt)
    p.op("scalar", f_sqrt, waits=(("sv", n_var),), n=1)
    n_rst = p.n("ss")

    def f_recip(v, S):
        v.reciprocal(out=st[:, c5:c5 + 1], in_=st[:, c5:c5 + 1])
        return v.scalar_tensor_tensor(out=st[:, c6:c6 + 1], in0=st[:, c2:c2 + 1],
                                      scalar=-1.0, in1=st[:, c5:c5 + 1],
                                      op0=ALU.mult, op1=ALU.mult)
    p.op("vector", f_recip, waits=(("ss", n_rst),), n=2)
    return (lambda: st[:, c5:c5 + 1]), (lambda: st[:, c6:c6 + 1])



def emit_ffn(p, nc, ar, cio, pf, T=2048, TC=512):
    nH = T // 2
    ntc = nH // TC
    ntile = nH // 128
    n_oc = DH // 128
    n_ic = D // 128
    n_oc3 = D // 128
    sub = TC // 128

    xin = cio["xin"]
    w1s, w2s, h1s, h2s, w3s = cio["w1s"], cio["w2s"], cio["h1s"], cio["h2s"], cio["w3s"]
    sc12, sch, sc3 = cio["sc12"], cio["sch"], cio["sc3"]
    xout = cio["xout"]
    idb_sb, idf_sb = cio["idb"], cio["idf"]
    P, TPS = cio["P"], cio["TPS"]

    xin_t = ar.sb("xin_t", 2 * D, F32)
    u_tm = ar.sb("u_tm", 2 * D, BF16)
    u = ar.sb("u", n_ic * nH, BF16)
    big_a = ar.sb("big_a", n_oc * nH, BF16)
    big_b = ar.sb("big_b", max(n_oc * nH, n_oc3 * n_oc * 128), BF16)
    wab = ar.sb("wab", 2 * 2 * n_ic * 128, BF16)
    wh = ar.sb("wh", 2 * n_oc * 128, BF16)
    hsil = ar.sb("hsil", 2 * TC, BF16)
    TCE = min(TC, 256)
    sub_e = TCE // 128
    ntc_e = nH // TCE
    ysc = ar.sb("ysc", n_oc3 * TCE, F32)
    y_tm = ar.sb("y_tm", D, F32)
    xo_t = ar.sb("xo_t", 2 * D, F32)
    st = ar.sb("st", 16, F32)
    sc12_sb = ar.sb("sc12_sb", 2 * n_oc, F32)
    sch_sb = ar.sb("sch_sb", 2 * n_oc, F32)
    sc3_sb = ar.sb("sc3_sb", n_oc3, F32)

    names = ["aux", "xa0", "xa1", "wa0", "wa1", "wh0", "wh1", "w3l",
             "tr", "mma", "mmb", "mmc", "mme", "tre", "so0", "so1"]
    g = {nm: pf + nm for nm in names}
    p.sem(*[g[nm] for nm in names])

    class _G:
        def __getitem__(self, k):
            return g[k]
    aux, xa0, xa1 = g["aux"], g["xa0"], g["xa1"]

    for dst, srcap in [(sc12_sb, sc12), (sch_sb, sch), (sc3_sb, sc3)]:
        def f_aux(e, S, dst=dst, srcap=srcap):
            return e.dma_start(out=dst.ap(), in_=srcap)
        p.op("sync", f_aux, inc=g["aux"], k=16)
    AUXN = ("common_aux", cio["aux_n"])
    SCLN = (g["aux"], p.n(g["aux"]))

    xin_free = [None, None]
    tps_free = [0, 0]
    silu_done = [0, 0]
    h_done = [0, 0]
    gelu_done = [0, 0, 0, 0]
    ye_done = [0, 0]
    yev_free = 0
    wa_free = [None, None]
    wh_free = [None, None]
    mme_total_prev = 0

    for half in range(2):
        tok0 = half * nH

        for i in range(ntile):
            slot = i % 2
            tok = tok0 + i * 128
            xa = g[f"xa{slot}"]
            gi = half * ntile + i

            def f_ld(e, S, tok=tok, slot=slot):
                return e.dma_start(out=xin_t[:, slot * D:(slot + 1) * D],
                                   in_=xin[tok:tok + 128, :])
            w = (xin_free[slot],) if xin_free[slot] else ()
            p.op("sync", f_ld, waits=w, inc=xa, k=16)
            nx = p.n(xa)

            src = (lambda slot=slot: xin_t[:, slot * D:(slot + 1) * D])
            sqdump = (lambda slot=slot: u_tm[:, slot * D:(slot + 1) * D])
            sq_waits = [(xa, nx)]
            if tps_free[slot]:
                sq_waits.append((g["tr"], p.n(g["tr"])))
            r_ap, mr_ap = emit_ln_stats(p, src, st, 0, ((xa, nx),),
                                        sq_waits, sqdump)

            def f_apply(v, S, slot=slot, r=r_ap, mr=mr_ap):
                return v.tensor_scalar(out=u_tm[:, slot * D:(slot + 1) * D],
                                       in0=xin_t[:, slot * D:(slot + 1) * D],
                                       scalar1=r(), scalar2=mr(),
                                       op0=ALU.mult, op1=ALU.add)
            p.op("vector", f_apply, n=1)
            n_ua = p.n("sv")
            xin_free[slot] = ("sv", n_ua)

            tr_waits = [("sv", n_ua), AUXN, SCLN]
            if tps_free[slot]:
                tr_waits.append(("sv", tps_free[slot]))

            def f_tr(t, S, slot=slot):
                last = None
                for j in range(n_ic):
                    last = t.transpose(
                        sb_ap(TPS[slot], j * 128, [[1, 128]]),
                        u_tm[:, slot * D + j * 128: slot * D + (j + 1) * 128],
                        idb_sb.ap())
                return last
            p.op("tensor", f_tr, waits=tr_waits, inc=g["tr"])

            uev_waits = [(g["tr"], p.n(g["tr"]))]
            if half == 1 and i == 0:
                uev_waits.append((g["mmb"], n_oc * ntc))

            def f_uev(v, S, slot=slot, i=i):
                return v.tensor_copy(
                    out=sb_ap(u, i * 128, [[nH, n_ic], [1, 128]]),
                    in_=sb_ap(TPS[slot], 0, [[128, n_ic], [1, 128]]))
            p.op("vector", f_uev, waits=uev_waits, n=1)
            tps_free[slot] = p.n("sv")
        uev_done = p.n("sv")

        for oc in range(n_oc):
            gslot = (half * n_oc + oc) % 2
            wl_waits = [wa_free[gslot]] if wa_free[gslot] else []

            def f_wab1(e, S, oc=oc, slot=gslot):
                c = slot * 2 * n_ic * 128
                return e.dma_start(out=wab[:, c:c + n_ic * 128], in_=w1s[oc])

            def f_wab2(e, S, oc=oc, slot=gslot):
                c = slot * 2 * n_ic * 128
                return e.dma_start(out=wab[:, c + n_ic * 128:c + 2 * n_ic * 128],
                                   in_=w2s[oc])
            wa = g[f"wa{gslot}"]
            p.op("gpsimd", f_wab1, waits=wl_waits, inc=wa, k=16, n=0)
            p.op("gpsimd", f_wab2, inc=wa, k=16, n=0)
            wl_wait = (wa, p.n(wa))

            for tc in range(ntc):
                idx = (half * n_oc + oc) * ntc + tc
                mm_waits = [wl_wait, ("sv", uev_done)]
                if silu_done[idx % 2]:
                    mm_waits.append(("ss", silu_done[idx % 2]))
                if h_done[idx % 2]:
                    mm_waits.append(("sv", h_done[idx % 2]))
                pa, pb = P[idx % 2], P[2 + idx % 2]

                def f_mma(t, S, oc=oc, tc=tc, slot=gslot, pa=pa):
                    c = slot * 2 * n_ic * 128
                    last = None
                    for ic in range(n_ic):
                        last = t.matmul(pa[:, 0:TC],
                                        wab[:, c + ic * 128:c + (ic + 1) * 128],
                                        u[:, ic * nH + tc * TC: ic * nH + (tc + 1) * TC],
                                        start=(ic == 0), stop=(ic == n_ic - 1))
                    return last
                p.op("tensor", f_mma, waits=mm_waits, inc=g["mma"])

                def f_mmb(t, S, oc=oc, tc=tc, slot=gslot, pb=pb):
                    c = slot * 2 * n_ic * 128 + n_ic * 128
                    last = None
                    for ic in range(n_ic):
                        last = t.matmul(pb[:, 0:TC],
                                        wab[:, c + ic * 128:c + (ic + 1) * 128],
                                        u[:, ic * nH + tc * TC: ic * nH + (tc + 1) * TC],
                                        start=(ic == 0), stop=(ic == n_ic - 1))
                    return last
                p.op("tensor", f_mmb, inc=g["mmb"])
                if tc == ntc - 1:
                    wa_free[gslot] = (g["mmb"], p.n(g["mmb"]))

                sil_waits = [(g["mma"], p.n(g["mma"])), SCLN]
                if h_done[idx % 2]:
                    sil_waits.append(("sv", h_done[idx % 2]))

                def f_sil(s, S, oc=oc, idx=idx, pa=pa):
                    return s.activation(out=hsil[:, (idx % 2) * TC:(idx % 2 + 1) * TC],
                                        in_=pa[:, 0:TC], func=AF.Silu,
                                        scale=sc12_sb[:, oc:oc + 1])
                p.op("scalar", f_sil, waits=sil_waits, n=1)
                silu_done[idx % 2] = p.n("ss")

                hw_waits = [(g["mmb"], p.n(g["mmb"])), ("ss", silu_done[idx % 2])]
                if half == 1 and oc == 0 and tc == 0:
                    hw_waits.append((g["mme"], mme_total_prev))

                def f_h(v, S, oc=oc, tc=tc, idx=idx, pb=pb):
                    return v.scalar_tensor_tensor(
                        out=big_a[:, oc * nH + tc * TC: oc * nH + (tc + 1) * TC],
                        in0=pb[:, 0:TC], scalar=sc12_sb[:, n_oc + oc:n_oc + oc + 1],
                        in1=hsil[:, (idx % 2) * TC:(idx % 2 + 1) * TC],
                        op0=ALU.mult, op1=ALU.mult)
                p.op("vector", f_h, waits=hw_waits, n=1)
                h_done[idx % 2] = p.n("sv")
        h_all = p.n("sv")

        for stage in range(2):
            wsrc = h1s if stage == 0 else h2s
            rbuf = big_a if stage == 0 else big_b
            obuf = big_b if stage == 0 else big_a
            scol0 = 0 if stage == 0 else n_oc
            ready = ("sv", h_all) if stage == 0 else ("ss", p.n("ss"))
            for oc in range(n_oc):
                gslot = ((half * 2 + stage) * n_oc + oc) % 2
                wl_waits = [wh_free[gslot]] if wh_free[gslot] else []

                def f_wh(e, S, oc=oc, slot=gslot, wsrc=wsrc):
                    c = slot * n_oc * 128
                    return e.dma_start(out=wh[:, c:c + n_oc * 128], in_=wsrc[oc])
                whs = g[f"wh{gslot}"]
                p.op("gpsimd", f_wh, waits=wl_waits, inc=whs, k=16, n=0)
                wl_wait = (whs, p.n(whs))

                for tc in range(ntc):
                    pslot = p.n(g["mmc"]) % 2
                    mm_waits = [wl_wait, ready]
                    if gelu_done[pslot]:
                        mm_waits.append(("ss", gelu_done[pslot]))
                    pc = P[pslot]

                    def f_mmc(t, S, oc=oc, tc=tc, slot=gslot, pc=pc, rbuf=rbuf):
                        c = slot * n_oc * 128
                        last = None
                        for ic in range(n_oc):
                            last = t.matmul(
                                pc[:, 0:TC], wh[:, c + ic * 128:c + (ic + 1) * 128],
                                rbuf[:, ic * nH + tc * TC: ic * nH + (tc + 1) * TC],
                                start=(ic == 0), stop=(ic == n_oc - 1))
                        return last
                    p.op("tensor", f_mmc, waits=mm_waits, inc=g["mmc"])
                    if tc == ntc - 1:
                        wh_free[gslot] = (g["mmc"], p.n(g["mmc"]))

                    g_waits = [(g["mmc"], p.n(g["mmc"]))]
                    if half == 1 and stage == 0 and oc == 0 and tc == 0:
                        g_waits.append((g["mme"], mme_total_prev))

                    def f_g(s, S, oc=oc, tc=tc, pc=pc, obuf=obuf, scol=scol0 + oc):
                        return s.activation(
                            out=obuf[:, oc * nH + tc * TC: oc * nH + (tc + 1) * TC],
                            in_=pc[:, 0:TC], func=AF.Gelu,
                            scale=sch_sb[:, scol:scol + 1])
                    p.op("scalar", f_g, waits=g_waits, n=1)
                    gelu_done[pslot] = p.n("ss")
        d_mm_done = p.n(g["mmc"])
        g2_done = p.n("ss")

        for oc3 in range(n_oc3):
            def f_w3(e, S, oc=oc3):
                return e.dma_start(out=big_b[:, oc * n_oc * 128:(oc + 1) * n_oc * 128],
                                   in_=w3s[oc])
            p.op("gpsimd", f_w3,
                 waits=(((g["mmc"], d_mm_done),) if oc3 == 0 else ()),
                 inc=g["w3l"], k=16, n=0)
        w3_wait = (g["w3l"], p.n(g["w3l"]))

        for tc in range(ntc_e):
            tre_before = p.n(g["tre"])
            for oc in range(n_oc3):
                eslot = p.n(g["mme"]) % 2
                mm_waits = [w3_wait, ("ss", g2_done)]
                if ye_done[eslot]:
                    mm_waits.append(("sv", ye_done[eslot]))
                pe = P[eslot]

                def f_mme(t, S, oc=oc, tc=tc, pe=pe):
                    last = None
                    for ic in range(n_oc):
                        last = t.matmul(
                            pe[:, 0:TCE], big_b[:, oc * n_oc * 128 + ic * 128:
                                                oc * n_oc * 128 + (ic + 1) * 128],
                            big_a[:, ic * nH + tc * TCE: ic * nH + (tc + 1) * TCE],
                            start=(ic == 0), stop=(ic == n_oc - 1))
                    return last
                p.op("tensor", f_mme, waits=mm_waits, inc=g["mme"])

                ye_waits = [(g["mme"], p.n(g["mme"]))]
                if tre_before > 0:
                    ye_waits.append((g["tre"], tre_before))

                def f_ye(v, S, oc=oc, pe=pe):
                    return v.tensor_scalar(out=ysc[:, oc * TCE:(oc + 1) * TCE],
                                           in0=pe[:, 0:TCE], scalar1=sc3_sb[:, oc:oc + 1],
                                           scalar2=None, op0=ALU.mult)
                p.op("vector", f_ye, waits=ye_waits, n=1)
                ye_done[eslot] = p.n("sv")
            ye_all = p.n("sv")

            for t4 in range(sub_e):
                i_glob = half * ntile + tc * sub_e + t4
                tok = tok0 + (tc * sub_e + t4) * 128

                tre_waits = [("sv", ye_all)]
                if yev_free:
                    tre_waits.append(("sv", yev_free))

                def f_tre(t, S, t4=t4):
                    last = None
                    for oc in range(n_oc3):
                        last = t.transpose(
                            sb_ap(P[4 + oc // 4], (oc % 4) * 128, [[1, 128]]),
                            ysc[:, oc * TCE + t4 * 128: oc * TCE + (t4 + 1) * 128],
                            idf_sb.ap())
                    return last
                p.op("tensor", f_tre, waits=tre_waits, inc=g["tre"])

                def f_yev(v, S):
                    v.tensor_copy(out=y_tm[:, 0:512], in_=P[4].ap())
                    return v.tensor_copy(out=y_tm[:, 512:1024], in_=P[5].ap())
                p.op("vector", f_yev, waits=((g["tre"], p.n(g["tre"])),), n=2)
                yev_free = p.n("sv")
                n_yev = p.n("sv")

                slot = i_glob % 2
                xa = g[f"xa{slot}"]

                def f_xr(e, S, tok=tok, slot=slot):
                    return e.dma_start(out=xin_t[:, slot * D:(slot + 1) * D],
                                       in_=xin[tok:tok + 128, :])
                w = (xin_free[slot],) if xin_free[slot] else ()
                p.op("sync", f_xr, waits=w, inc=xa, k=16)
                nx = p.n(xa)

                r_ap, mr_ap = emit_ln_stats(p, lambda: y_tm.ap(), st, 8,
                                            (("sv", n_yev),), (("sv", n_yev),),
                                            lambda: u_tm[:, 0:D])

                so = g[f"so{slot}"]

                def f_res(v, S, slot=slot, r=r_ap, mr=mr_ap):
                    v.tensor_scalar(out=xo_t[:, slot * D:(slot + 1) * D],
                                    in0=y_tm.ap(), scalar1=r(), scalar2=mr(),
                                    op0=ALU.mult, op1=ALU.add)
                    return v.tensor_tensor(out=xo_t[:, slot * D:(slot + 1) * D],
                                           in0=xo_t[:, slot * D:(slot + 1) * D],
                                           in1=xin_t[:, slot * D:(slot + 1) * D],
                                           op=ALU.add)
                res_waits = [(xa, nx)]
                if p.n(so) > 0:
                    res_waits.append((so, p.n(so)))
                p.op("vector", f_res, waits=res_waits, n=2)
                xin_free[slot] = ("sv", p.n("sv"))

                def f_sto(e, S, tok=tok, slot=slot):
                    return e.dma_start(out=xout[tok:tok + 128, :],
                                       in_=xo_t[:, slot * D:(slot + 1) * D])
                p.op("sync", f_sto, waits=(("sv", p.n("sv")),), inc=so, k=16)
        mme_total_prev = p.n(g["mme"])

    p.op("sync", lambda e, S: e.nop(),
         waits=((g["so0"], p.n(g["so0"])), (g["so1"], p.n(g["so1"]))))





def build_ffn(T=2048, TC=512):
    nc = bass.Bass("TRN2", target_bir_lowering=False)
    n_oc, n_ic, n_oc3 = DH // 128, D // 128, D // 128
    cio = {
        "xin": nc.dram_tensor("xin", [T, D], F32, kind="ExternalInput").ap(),
        "w1s": nc.dram_tensor("w1s", [n_oc, 128, n_ic * 128], F32, kind="ExternalInput").ap(),
        "w2s": nc.dram_tensor("w2s", [n_oc, 128, n_ic * 128], F32, kind="ExternalInput").ap(),
        "h1s": nc.dram_tensor("h1s", [n_oc, 128, n_oc * 128], F32, kind="ExternalInput").ap(),
        "h2s": nc.dram_tensor("h2s", [n_oc, 128, n_oc * 128], F32, kind="ExternalInput").ap(),
        "w3s": nc.dram_tensor("w3s", [n_oc3, 128, n_oc * 128], F32, kind="ExternalInput").ap(),
        "sc12": nc.dram_tensor("sc12", [128, 2 * n_oc], F32, kind="ExternalInput").ap(),
        "sch": nc.dram_tensor("sch", [128, 2 * n_oc], F32, kind="ExternalInput").ap(),
        "sc3": nc.dram_tensor("sc3", [128, n_oc3], F32, kind="ExternalInput").ap(),
        "xout": nc.dram_tensor("xout", [T, D], F32, kind="ExternalOutput").ap(),
    }
    id_bf = nc.inline_tensor(np.eye(128, dtype=ml_dtypes.bfloat16), name="id_bf")
    id_f32 = nc.inline_tensor(np.eye(128, dtype=np.float32), name="id_f32")
    idb_sb = nc.alloc_sbuf_tensor("idb_sb", [128, 128], BF16)
    idf_sb = nc.alloc_sbuf_tensor("idf_sb", [128, 128], F32)
    nH = T // 2
    bfc = (2 * D) + (n_ic * nH) + 2 * max(DH // 128 * nH, 8 * 32 * 128) + \
          (2 * 2 * (D // 128) * 128) + (2 * 32 * 128) + (2 * TC) + 64
    f32c = 2 * D + 8 * min(TC, 256) + D + 2 * D + 16 + 64 + 64 + 8 + 64
    ar = Arena(nc, bfc, f32c)
    Pb = [nc.alloc_psum_tensor(f"P{i}", [128, 512], F32) for i in range(6)]
    TPSb = [nc.alloc_psum_tensor(f"Tp{i}", [128, 1024], BF16) for i in range(2)]
    cio["P"], cio["TPS"] = Pb, TPSb
    cio["idb"], cio["idf"] = idb_sb, idf_sb
    p = Prog(nc)
    p.sem("common_aux")

    def f_id1(e, S):
        return e.dma_start(out=idb_sb.ap(), in_=id_bf.ap())
    p.op("sync", f_id1, inc="common_aux", k=16)

    def f_id2(e, S):
        return e.dma_start(out=idf_sb.ap(), in_=id_f32.ap())
    p.op("sync", f_id2, inc="common_aux", k=16)
    cio["aux_n"] = p.n("common_aux")
    emit_ffn(p, nc, ar, cio, "F_", T=T, TC=TC)
    with ExitStack() as es:
        p.emit(es)
    return nc

def swz_lhsT(w):
    O, I = w.shape
    a = np.ascontiguousarray(
        w.reshape(O // 128, 128, I // 128, 128).transpose(0, 3, 2, 1))
    return a.reshape(O // 128, 128, I)


def ffn_host_inputs(w1_w, w1_s, w2_w, w2_s, h1_w, h1_s, h2_w, h2_s, w3_w, w3_s):
    d = {
        "w1s": swz_lhsT(np.asarray(w1_w)),
        "w2s": swz_lhsT(np.asarray(w2_w)),
        "h1s": swz_lhsT(np.asarray(h1_w)),
        "h2s": swz_lhsT(np.asarray(h2_w)),
        "w3s": swz_lhsT(np.asarray(w3_w)),
        "sc12": np.ascontiguousarray(
            np.concatenate([np.asarray(w1_s).reshape(32, 128).T,
                            np.asarray(w2_s).reshape(32, 128).T], axis=1)),
        "sch": np.ascontiguousarray(
            np.concatenate([np.asarray(h1_s).reshape(32, 128).T,
                            np.asarray(h2_s).reshape(32, 128).T], axis=1)),
        "sc3": np.ascontiguousarray(np.asarray(w3_s).reshape(8, 128).T),
    }
    return {k: np.ascontiguousarray(v, dtype=np.float32) for k, v in d.items()}





NH1 = 8
DH1 = 16
DPHI = 1 + DH1 + DH1 * DH1
EPS_L1 = 1e-5


def _emit_rms_tile(p, xin, xin_t, st, slot, tok, xa, xin_free, z_tm,
                   sq_extra=()):

    def f_ld(e, S, tok=tok, slot=slot):
        return e.dma_start(out=xin_t[:, slot * D:(slot + 1) * D],
                           in_=xin[tok:tok + 128, :])
    w = (xin_free[slot],) if xin_free[slot] else ()
    p.op("sync", f_ld, waits=w, inc=xa, k=16)
    nx = p.n(xa)

    def f_sq(s, S, slot=slot):
        return s.activation(out=z_tm[:, slot * D:(slot + 1) * D],
                            in_=xin_t[:, slot * D:(slot + 1) * D],
                            func=AF.Square, accum_out=st[:, 0:1])
    p.op("scalar", f_sq,
         waits=((xa, nx), ("sv", p.n("sv"))) + tuple(sq_extra), n=1)

    def f_nrm(s, S):
        return s.activation(out=st[:, 1:2], in_=st[:, 0:1], func=AF.Sqrt)
    p.op("scalar", f_nrm, waits=(("sv", p.n("sv")),), n=1)
    n_rst = p.n("ss")

    def f_r(v, S):
        v.tensor_scalar(out=st[:, 1:2], in0=st[:, 1:2], scalar1=1e-12,
                        scalar2=None, op0=ALU.max)
        v.reciprocal(out=st[:, 1:2], in_=st[:, 1:2])
        return v.tensor_scalar(out=st[:, 2:3], in0=st[:, 1:2],
                               scalar1=float(D) ** 0.5, scalar2=None,
                               op0=ALU.mult)
    p.op("vector", f_r, waits=(("ss", n_rst),), n=3)

    def f_z(v, S, slot=slot):
        return v.tensor_scalar(out=z_tm[:, slot * D:(slot + 1) * D],
                               in0=xin_t[:, slot * D:(slot + 1) * D],
                               scalar1=st[:, 2:3], scalar2=None, op0=ALU.mult)
    p.op("vector", f_z, n=1)
    xin_free[slot] = ("sv", p.n("sv"))


def _emit_phi(p, src_sb, src_col0, khalf, phi, waits_v):

    def f_phi(v, S):
        src = lambda dims, off=0: g_ap(src_sb, 0, 128, src_col0 + off, dims)
        v.tensor_scalar(out=khalf.ap(),
                        in0=src([[1, NH1 * DH1]]),
                        scalar1=0.5 ** 0.5, scalar2=None, op0=ALU.mult)
        v.memset(sb_ap(phi, 0, [[DPHI, NH1], [1, 1]]), 1.0)
        v.tensor_copy(out=sb_ap(phi, 1, [[DPHI, NH1], [1, DH1]]),
                      in_=src([[DH1, NH1], [1, DH1]]))
        return v.tensor_tensor(
            out=sb_ap(phi, 1 + DH1, [[DPHI, NH1], [DH1, DH1], [1, DH1]]),
            in0=src([[DH1, NH1], [0, DH1], [1, DH1]]),
            in1=sb_ap(khalf, 0, [[DH1, NH1], [1, DH1], [0, DH1]]),
            op=ALU.mult)
    p.op("vector", f_phi, waits=waits_v, n=4)


def emit_l1a(p, nc, ar, cio, pf, T=2048):
    ntile = T // 128
    n_ic = D // 128

    xin = cio["xin"]
    kvw = cio["kvw"]
    gam = cio["gam"]
    kvp = cio["kvp"]
    idb_sb = cio["idb"]
    Pg, TPSg = cio["P"], cio["TPS"]
    P = Pg[0:4]
    TPS = TPSg

    xin_t = ar.sb("xin_t", 2 * D, F32)
    z_tm = ar.sb("z_tm", 2 * D, BF16)
    z_fm = ar.sb("z_fm", n_ic * T, BF16)
    kvw_sb = ar.sb("kvw_sb", 2 * n_ic * 128, BF16)
    gam_sb = ar.sb("gam_sb", n_ic, F32)
    kv_fm = ar.sb("kv_fm", 2 * T, BF16)
    ktm = ar.sb("ktm", 256, BF16)
    khalf = ar.sb("khalf", 128, BF16)
    phi = ar.sb("phi", NH1 * DPHI, BF16)
    vaug = ar.sb("vaug", NH1 * 17, BF16)
    kvacc = ar.sb("kvacc", NH1 * DPHI, F32)
    st = ar.sb("st", 8, F32)

    names = ["aux", "gld", "xa0", "xa1", "tr", "mkv", "mmk", "sto"]
    g = {nm: pf + nm for nm in names}
    p.sem(*[g[nm] for nm in names])
    AUXC = ("common_aux", cio["aux_n"])

    def f_gam(e, S):
        return e.dma_start(out=gam_sb.ap(), in_=gam)
    p.op("sync", f_gam, inc=g["aux"], k=16)

    def f_wl(e, S):
        return e.dma_start(out=kvw_sb.ap(),
                           in_=kvw.rearrange("a p c -> p a c"))
    p.op("gpsimd", f_wl, inc=g["gld"], k=16, n=0)

    def f_gs(v, S):
        last = None
        for oc in range(2):
            for ic in range(n_ic):
                c = oc * n_ic * 128 + ic * 128
                last = v.tensor_scalar(out=kvw_sb[:, c:c + 128],
                                       in0=kvw_sb[:, c:c + 128],
                                       scalar1=gam_sb[:, ic:ic + 1],
                                       scalar2=None, op0=ALU.mult)
        return last
    p.op("vector", f_gs, waits=((g["gld"], 16), (g["aux"], 16), AUXC), n=2 * n_ic)
    gs_done = p.n("sv")

    xin_free = [None, None]
    tps_free = [0, 0]

    for i in range(ntile):
        slot = i % 2
        xa = g[f"xa{slot}"]
        sq_extra = [(g["tr"], p.n(g["tr"]))] if tps_free[slot] else []
        _emit_rms_tile(p, xin, xin_t, st, slot, i * 128, xa, xin_free, z_tm,
                       sq_extra)
        n_z = p.n("sv")

        tr_waits = [("sv", n_z), (g["aux"], 16), AUXC]
        if tps_free[slot]:
            tr_waits.append(("sv", tps_free[slot]))

        def f_tr(t, S, slot=slot):
            last = None
            for j in range(n_ic):
                last = t.transpose(
                    sb_ap(TPS[slot], j * 128, [[1, 128]]),
                    z_tm[:, slot * D + j * 128: slot * D + (j + 1) * 128],
                    idb_sb.ap())
            return last
        p.op("tensor", f_tr, waits=tr_waits, inc=g["tr"])

        def f_zev(v, S, slot=slot, i=i):
            return v.tensor_copy(
                out=sb_ap(z_fm, i * 128, [[T, n_ic], [1, 128]]),
                in_=sb_ap(TPS[slot], 0, [[128, n_ic], [1, 128]]))
        p.op("vector", f_zev, waits=((g["tr"], p.n(g["tr"])),), n=1)
        tps_free[slot] = p.n("sv")
    z_done = p.n("sv")

    ev_free = [0, 0]
    for oc in range(2):
        for tc in range(T // 512):
            idx = oc * (T // 512) + tc
            ps = P[idx % 2]
            mm_waits = [("sv", z_done)]
            if ev_free[idx % 2]:
                mm_waits.append(("sv", ev_free[idx % 2]))

            def f_mm(t, S, oc=oc, tc=tc, ps=ps):
                last = None
                for ic in range(n_ic):
                    last = t.matmul(ps.ap(),
                                    kvw_sb[:, oc * n_ic * 128 + ic * 128:
                                           oc * n_ic * 128 + (ic + 1) * 128],
                                    z_fm[:, ic * T + tc * 512: ic * T + (tc + 1) * 512],
                                    start=(ic == 0), stop=(ic == n_ic - 1))
                return last
            p.op("tensor", f_mm, waits=mm_waits, inc=g["mkv"])

            def f_ev(v, S, oc=oc, tc=tc, ps=ps):
                return v.tensor_copy(out=kv_fm[:, oc * T + tc * 512:
                                               oc * T + (tc + 1) * 512],
                                     in_=ps.ap())
            p.op("vector", f_ev, waits=((g["mkv"], p.n(g["mkv"])),), n=1)
            ev_free[idx % 2] = p.n("sv")
    kv_done = p.n("sv")

    def f_z0(v, S):
        return v.memset(g_ap(kvacc, 0, 17, 0, [[1, NH1 * DPHI]]), 0.0)
    p.op("vector", f_z0, n=1)

    pk_free = [0, 0]
    for i in range(ntile):
        slot = i % 2
        tr_waits = [("sv", kv_done), (g["tr"], p.n(g["tr"]))]
        if tps_free[slot]:
            tr_waits.append(("sv", tps_free[slot]))

        def f_trk(t, S, slot=slot, i=i):
            t.transpose(sb_ap(TPS[slot], 0, [[1, 128]]),
                        kv_fm[:, i * 128:(i + 1) * 128], idb_sb.ap())
            return t.transpose(sb_ap(TPS[slot], 128, [[1, 128]]),
                               kv_fm[:, T + i * 128: T + (i + 1) * 128],
                               idb_sb.ap())
        p.op("tensor", f_trk, waits=tr_waits, inc=g["tr"])

        def f_kev(v, S, slot=slot):
            return v.tensor_copy(out=ktm.ap(), in_=TPS[slot].ap()[:, 0:256])
        p.op("vector", f_kev, waits=((g["tr"], p.n(g["tr"])),), n=1)
        tps_free[slot] = p.n("sv")

        _emit_phi(p, ktm, 0, khalf, phi, ())

        def f_va(v, S):
            v.tensor_copy(out=sb_ap(vaug, 0, [[17, NH1], [1, DH1]]),
                          in_=sb_ap(ktm, 128, [[DH1, NH1], [1, DH1]]))
            return v.memset(sb_ap(vaug, DH1, [[17, NH1], [1, 1]]), 1.0)
        p.op("vector", f_va, n=2)
        n_feat = p.n("sv")

        for h in range(NH1):
            pslot = 2 + (p.n(g["mmk"]) % 2)
            mm_waits = [("sv", n_feat)]
            if pk_free[pslot - 2]:
                mm_waits.append(("sv", pk_free[pslot - 2]))

            def f_mmk(t, S, h=h, pslot=pslot):
                return t.matmul(P[pslot][0:17, 0:DPHI],
                                vaug[:, h * 17:(h + 1) * 17],
                                phi[:, h * DPHI:(h + 1) * DPHI],
                                start=True, stop=True)
            p.op("tensor", f_mmk, waits=mm_waits, inc=g["mmk"])

            def f_acc(v, S, h=h, pslot=pslot):
                return v.tensor_tensor(
                    out=g_ap(kvacc, 0, 17, h * DPHI, [[1, DPHI]]),
                    in0=P[pslot][0:17, 0:DPHI],
                    in1=g_ap(kvacc, 0, 17, h * DPHI, [[1, DPHI]]),
                    op=ALU.add)
            p.op("vector", f_acc, waits=((g["mmk"], p.n(g["mmk"])),), n=1)
            pk_free[pslot - 2] = p.n("sv")

    def f_out(e, S):
        return e.dma_start(out=kvp,
                           in_=g_ap(kvacc, 0, 17, 0, [[1, NH1 * DPHI]]))
    p.op("sync", f_out, waits=(("sv", p.n("sv")),), inc=g["sto"], k=16)
    p.op("sync", lambda e, S: e.nop(), waits=((g["sto"], p.n(g["sto"])),))


def build_l1a(T=2048):
    nc = bass.Bass("TRN2", target_bir_lowering=False)
    n_ic = D // 128
    cio = {
        "xin": nc.dram_tensor("xin", [T, D], F32, kind="ExternalInput").ap(),
        "kvw": nc.dram_tensor("kvw", [2, 128, n_ic * 128], F32, kind="ExternalInput").ap(),
        "gam": nc.dram_tensor("gam", [128, n_ic], F32, kind="ExternalInput").ap(),
        "kvp": nc.dram_tensor("kvp", [17, NH1 * DPHI], F32, kind="ExternalOutput").ap(),
    }
    id_bf = nc.inline_tensor(np.eye(128, dtype=ml_dtypes.bfloat16), name="id_bf")
    idb_sb = nc.alloc_sbuf_tensor("idb_sb", [128, 128], BF16)
    cio["idb"] = idb_sb
    cio["P"] = [nc.alloc_psum_tensor(f"P{i}", [128, 512], F32) for i in range(4)]
    cio["TPS"] = [nc.alloc_psum_tensor(f"Tp{i}", [128, 1024], BF16) for i in range(2)]
    ar = Arena(nc, n_ic * T + 2 * T + 16000, 3 * D + 4096)
    p = Prog(nc)
    p.sem("common_aux")

    def f_id1(e, S):
        return e.dma_start(out=idb_sb.ap(), in_=id_bf.ap())
    p.op("sync", f_id1, inc="common_aux", k=16)
    cio["aux_n"] = p.n("common_aux")
    emit_l1a(p, nc, ar, cio, "A_", T=T)
    with ExitStack() as es:
        p.emit(es)
    return nc




def emit_l1b(p, nc, ar, cio, pf, T=2048):
    ntile = T // 128
    n_ic = D // 128
    DI = NH1 * DH1

    xin = cio["xin"]
    qw = cio["qw"]
    gam = cio["gam"]
    wout = cio["wout"]
    xout = cio["xout"]
    idb_sb = cio["idb"]
    Pg, TPS = cio["P"], cio["TPS"]
    P = Pg[0:4]
    PH = cio["PH"]

    xin_t = ar.sb("xin_t", 2 * D, F32)
    z_tm = ar.sb("z_tm", 2 * D, BF16)
    z_fm = ar.sb("z_fm", n_ic * T, BF16)
    qw_sb = ar.sb("qw_sb", n_ic * 128, BF16)
    gam_sb = ar.sb("gam_sb", n_ic, F32)
    wo_sb = ar.sb("wo_sb", D, BF16)
    q_fm = ar.sb("q_fm", T, BF16)
    qtm = ar.sb("qtm", 128, BF16)
    khalf = ar.sb("khalf", 128, BF16)
    phi = ar.sb("phi", NH1 * 384, BF16)
    phf = ar.sb("phf", 2 * 384, BF16)
    kvg = ar.sb("kvg", NH1 * DPHI, F32)
    kvtmp = None if "cc_out" in cio else ar.sb("kvtmp", NH1 * DPHI, F32)
    kvgb = ar.sb("kvgb", NH1 * DPHI, BF16)
    kvT = ar.sb("kvT", NH1 * 3 * 17, BF16)
    comb = ar.sb("comb", 128, BF16)
    combf = ar.sb("combf", 128, BF16)
    y_tm = ar.sb("y_tm", D, F32)
    xo_t = ar.sb("xo_t", 2 * D, F32)
    sqd = ar.sb("sqd", D, BF16)
    st = ar.sb("st", 16, F32)

    names = ["aux", "gld", "kl0", "kl1", "xa0", "xa1",
             "tr", "trf", "mq", "mmn", "mmy", "so0", "so1"]
    g = {nm: pf + nm for nm in names}
    p.sem(*[g[nm] for nm in names])
    AUXC = ("common_aux", cio["aux_n"])

    def f_gam(e, S):
        return e.dma_start(out=gam_sb.ap(), in_=gam)
    p.op("sync", f_gam, inc=g["aux"], k=16)

    def f_wl(e, S):
        return e.dma_start(out=qw_sb.ap(), in_=qw)
    p.op("gpsimd", f_wl, inc=g["gld"], k=16, n=0)

    def f_wo(e, S):
        return e.dma_start(out=wo_sb.ap(), in_=wout)
    p.op("gpsimd", f_wo, inc=g["gld"], k=16, n=0)

    if "cc_out" in cio:
        def f_k0(e, S):
            return e.dma_start(out=g_ap(kvg, 0, 17, 0, [[1, NH1 * DPHI]]),
                               in_=cio["cc_out"])
        p.op("sync", f_k0, waits=(("ccs", cio["ccs_n"]),), inc=g["kl0"], k=16)

        def f_cast(v, S):
            return v.tensor_copy(
                out=g_ap(kvgb, 0, 17, 0, [[1, NH1 * DPHI]]),
                in_=g_ap(kvg, 0, 17, 0, [[1, NH1 * DPHI]]))
        p.op("vector", f_cast, waits=((g["kl0"], 16),), n=1)
        cast_done = p.n("sv")
    else:
        kvp4 = cio["kvp4"]

        def f_k0(e, S):
            return e.dma_start(out=g_ap(kvg, 0, 17, 0, [[1, NH1 * DPHI]]),
                               in_=kvp4[0])
        p.op("sync", f_k0, inc=g["kl0"], k=16)
        add_done = 0
        for j in range(1, 4):
            def f_kj(e, S, j=j):
                return e.dma_start(
                    out=g_ap(kvtmp, 0, 17, 0, [[1, NH1 * DPHI]]),
                    in_=kvp4[j])
            w = [("sv", add_done)] if add_done else []
            p.op("sync", f_kj, waits=w, inc=g["kl1"], k=16)

            def f_add(v, S):
                return v.tensor_tensor(
                    out=g_ap(kvg, 0, 17, 0, [[1, NH1 * DPHI]]),
                    in0=g_ap(kvtmp, 0, 17, 0, [[1, NH1 * DPHI]]),
                    in1=g_ap(kvg, 0, 17, 0, [[1, NH1 * DPHI]]),
                    op=ALU.add)
            p.op("vector", f_add,
                 waits=((g["kl0"], 16), (g["kl1"], p.n(g["kl1"]))), n=1)
            add_done = p.n("sv")

        def f_cast(v, S):
            return v.tensor_copy(
                out=g_ap(kvgb, 0, 17, 0, [[1, NH1 * DPHI]]),
                in_=g_ap(kvg, 0, 17, 0, [[1, NH1 * DPHI]]))
        p.op("vector", f_cast, n=1)
        cast_done = p.n("sv")

    def f_kvz(v, S_):
        return v.memset(kvT.ap(), 0.0)
    p.op("vector", f_kvz, n=1)
    for h in range(NH1):
        def f_trk(t, S, h=h):
            last = None
            for c in range(3):
                dsz = min(128, DPHI - c * 128)
                last = t.transpose(
                    g_ap(PH[h % 2], 0, dsz, c * 32, [[1, 17]]),
                    g_ap(kvgb, 0, 17, h * DPHI + c * 128, [[1, dsz]]),
                    idb_sb[0:17, 0:17])
            return last
        trk_waits = [("sv", cast_done), (g["aux"], 16), AUXC]
        if h >= 2:
            trk_waits.append(("sv", kvt_ev[h % 2]))
        p.op("tensor", f_trk, waits=trk_waits, inc=g["tr"])

        def f_tev(v, S, h=h):
            last = None
            for c in range(3):
                dsz = min(128, DPHI - c * 128)
                last = v.tensor_copy(
                    out=g_ap(kvT, 0, dsz, h * 51 + c * 17, [[1, 17]]),
                    in_=g_ap(PH[h % 2], 0, dsz, c * 32, [[1, 17]]))
            return last
        p.op("vector", f_tev, waits=((g["tr"], p.n(g["tr"])),), n=3)
        if h == 0:
            kvt_ev = [0, 0]
        kvt_ev[h % 2] = p.n("sv")
    kvt_done = p.n("sv")
    tps_init = [kvt_ev[0], kvt_ev[1]]

    def f_gs(v, S):
        last = None
        for ic in range(n_ic):
            last = v.tensor_scalar(out=qw_sb[:, ic * 128:(ic + 1) * 128],
                                   in0=qw_sb[:, ic * 128:(ic + 1) * 128],
                                   scalar1=gam_sb[:, ic:ic + 1],
                                   scalar2=float(DH1) ** -0.5, op0=ALU.mult,
                                   op1=ALU.mult)
        return last
    p.op("vector", f_gs, waits=((g["gld"], p.n(g["gld"])), (g["aux"], 16), AUXC), n=n_ic)
    gs_done = p.n("sv")

    xin_free = [None, None]
    tps_free = [tps_init[0], tps_init[1]]

    for i in range(ntile):
        slot = i % 2
        xa = g[f"xa{slot}"]
        sq_extra = [(g["tr"], p.n(g["tr"]))] if tps_free[slot] else []
        _emit_rms_tile(p, xin, xin_t, st, slot, i * 128, xa, xin_free, z_tm,
                       sq_extra)
        n_z = p.n("sv")
        tr_waits = [("sv", n_z), (g["aux"], 16), AUXC]
        if tps_free[slot]:
            tr_waits.append(("sv", tps_free[slot]))

        def f_tr(t, S, slot=slot):
            last = None
            for j in range(n_ic):
                last = t.transpose(
                    sb_ap(TPS[slot], j * 128, [[1, 128]]),
                    z_tm[:, slot * D + j * 128: slot * D + (j + 1) * 128],
                    idb_sb.ap())
            return last
        p.op("tensor", f_tr, waits=tr_waits, inc=g["tr"])

        def f_zev(v, S, slot=slot, i=i):
            return v.tensor_copy(
                out=sb_ap(z_fm, i * 128, [[T, n_ic], [1, 128]]),
                in_=sb_ap(TPS[slot], 0, [[128, n_ic], [1, 128]]))
        p.op("vector", f_zev, waits=((g["tr"], p.n(g["tr"])),), n=1)
        tps_free[slot] = p.n("sv")
    z_done = p.n("sv")

    ev_free = [0, 0]
    for tc in range(T // 512):
        ps = P[tc % 2]
        mm_waits = [("sv", z_done), ("sv", gs_done)]
        if ev_free[tc % 2]:
            mm_waits.append(("sv", ev_free[tc % 2]))

        def f_mm(t, S, tc=tc, ps=ps):
            last = None
            for ic in range(n_ic):
                last = t.matmul(ps.ap(),
                                qw_sb[:, ic * 128:(ic + 1) * 128],
                                z_fm[:, ic * T + tc * 512: ic * T + (tc + 1) * 512],
                                start=(ic == 0), stop=(ic == n_ic - 1))
            return last
        p.op("tensor", f_mm, waits=mm_waits, inc=g["mq"])

        def f_ev(v, S, tc=tc, ps=ps):
            return v.tensor_copy(out=q_fm[:, tc * 512:(tc + 1) * 512], in_=ps.ap())
        p.op("vector", f_ev, waits=((g["mq"], p.n(g["mq"])),), n=1)
        ev_free[tc % 2] = p.n("sv")
    q_done = p.n("sv")

    phf_free = [0, 0]
    pn_free = [0, 0]
    py_free = 0
    yev_free = 0
    prev_cf = 0
    for i in range(ntile):
        slot = i % 2
        tok = i * 128

        tr_waits = [("sv", q_done), (g["tr"], p.n(g["tr"]))]
        if tps_free[slot]:
            tr_waits.append(("sv", tps_free[slot]))
        if prev_cf:
            tr_waits.append(("sv", prev_cf))

        def f_trq(t, S, slot=slot, i=i):
            return t.transpose(sb_ap(TPS[slot], 0, [[1, 128]]),
                               q_fm[:, i * 128:(i + 1) * 128], idb_sb.ap())
        p.op("tensor", f_trq, waits=tr_waits, inc=g["tr"])

        def f_qev(v, S, slot=slot):
            return v.tensor_copy(out=qtm.ap(), in_=TPS[slot].ap()[:, 0:128])
        p.op("vector", f_qev, waits=((g["tr"], p.n(g["tr"])),), n=1)
        tps_free[slot] = p.n("sv")

        _emit_phi(p, qtm, 0, khalf, phi, ())
        n_phi = p.n("sv")

        for h in range(NH1):
            trf_waits = [("sv", n_phi)]
            if phf_free[h % 2]:
                trf_waits.append(("sv", phf_free[h % 2]))

            def f_trf(t, S, h=h):
                last = None
                for c in range(3):
                    dsz = min(128, DPHI - c * 128)
                    last = t.transpose(
                        g_ap(PH[h % 2], 0, dsz, c * 128, [[1, 128]]),
                        phi[:, h * DPHI + c * 128: h * DPHI + c * 128 + dsz],
                        idb_sb.ap())
                return last
            p.op("tensor", f_trf, waits=trf_waits, inc=g["trf"])

            def f_fev(v, S, h=h):
                last = None
                for c in range(3):
                    dsz = min(128, DPHI - c * 128)
                    last = v.tensor_copy(
                        out=g_ap(phf, 0, dsz, c * 128, [[1, 128]]),
                        in_=g_ap(PH[h % 2], 0, dsz, c * 128, [[1, 128]]))
                return last
            p.op("vector", f_fev, waits=((g["trf"], p.n(g["trf"])),), n=3)
            phf_free[h % 2] = p.n("sv")

            pn = P[2 + (h % 2)]
            mmn_waits = [("sv", phf_free[h % 2]), ("sv", kvt_done)]
            if pn_free[h % 2]:
                mmn_waits.append(("sv", pn_free[h % 2]))

            def f_mmn(t, S, h=h, pn=pn):
                last = None
                for c in range(3):
                    dsz = min(128, DPHI - c * 128)
                    last = t.matmul(
                        pn[:, 0:17],
                        phf[0:dsz, c * 128:c * 128 + 128],
                        kvT[0:dsz, (h * 3 + c) * 17:(h * 3 + c + 1) * 17],
                        start=(c == 0), stop=(c == 2))
                return last
            p.op("tensor", f_mmn, waits=mmn_waits, inc=g["mmn"])

            def f_nrm(v, S, h=h, pn=pn):
                v.tensor_scalar(out=st[:, 4:5], in0=pn[:, 16:17],
                                scalar1=EPS_L1, scalar2=None, op0=ALU.max)
                v.reciprocal(out=st[:, 4:5], in_=st[:, 4:5])
                return v.tensor_scalar(out=comb[:, h * DH1:(h + 1) * DH1],
                                       in0=pn[:, 0:DH1], scalar1=st[:, 4:5],
                                       scalar2=None, op0=ALU.mult)
            p.op("vector", f_nrm, waits=((g["mmn"], p.n(g["mmn"])),), n=3)
            pn_free[h % 2] = p.n("sv")
        comb_done = p.n("sv")

        def f_trc(t, S, slot=slot):
            return t.transpose(sb_ap(TPS[slot], 256, [[1, 128]]),
                               comb.ap(), idb_sb.ap())
        p.op("tensor", f_trc, waits=(("sv", comb_done),), inc=g["tr"])

        def f_cev(v, S, slot=slot):
            return v.tensor_copy(out=combf.ap(), in_=TPS[slot].ap()[:, 256:384])
        p.op("vector", f_cev, waits=((g["tr"], p.n(g["tr"])),), n=1)
        cf_done = p.n("sv")
        prev_cf = cf_done

        mmy_waits = [("sv", cf_done), (g["gld"], p.n(g["gld"]))]
        if py_free:
            mmy_waits.append(("sv", py_free))

        def f_mmy(t, S):
            t.matmul(P[2][:, 0:512], combf.ap(), wo_sb[:, 0:512],
                     start=True, stop=True)
            return t.matmul(P[3][:, 0:512], combf.ap(), wo_sb[:, 512:1024],
                            start=True, stop=True)
        p.op("tensor", f_mmy, waits=mmy_waits, inc=g["mmy"])

        def f_yev(v, S):
            v.tensor_copy(out=y_tm[:, 0:512], in_=P[2].ap())
            return v.tensor_copy(out=y_tm[:, 512:1024], in_=P[3].ap())
        p.op("vector", f_yev, waits=((g["mmy"], p.n(g["mmy"])),), n=2)
        py_free = p.n("sv")
        pn_free = [p.n("sv"), p.n("sv")]
        n_yev = p.n("sv")

        xa = g[f"xa{slot}"]

        def f_xr(e, S, tok=tok, slot=slot):
            return e.dma_start(out=xin_t[:, slot * D:(slot + 1) * D],
                               in_=xin[tok:tok + 128, :])
        w = (xin_free[slot],) if xin_free[slot] else ()
        p.op("sync", f_xr, waits=w, inc=xa, k=16)
        nx = p.n(xa)

        r_ap, mr_ap = emit_ln_stats(p, lambda: y_tm.ap(), st, 8,
                                    (("sv", n_yev),), (("sv", n_yev),),
                                    lambda: sqd.ap())

        so = g[f"so{slot}"]

        def f_res(v, S, slot=slot, r=r_ap, mr=mr_ap):
            v.tensor_scalar(out=xo_t[:, slot * D:(slot + 1) * D],
                            in0=y_tm.ap(), scalar1=r(), scalar2=mr(),
                            op0=ALU.mult, op1=ALU.add)
            return v.tensor_tensor(out=xo_t[:, slot * D:(slot + 1) * D],
                                   in0=xo_t[:, slot * D:(slot + 1) * D],
                                   in1=xin_t[:, slot * D:(slot + 1) * D],
                                   op=ALU.add)
        res_waits = [(xa, nx)]
        if p.n(so) > 0:
            res_waits.append((so, p.n(so)))
        p.op("vector", f_res, waits=res_waits, n=2)
        xin_free[slot] = ("sv", p.n("sv"))

        def f_sto(e, S, tok=tok, slot=slot):
            return e.dma_start(out=xout[tok:tok + 128, :],
                               in_=xo_t[:, slot * D:(slot + 1) * D])
        p.op("sync", f_sto, waits=(("sv", p.n("sv")),), inc=so, k=16)

    p.op("sync", lambda e, S: e.nop(),
         waits=((g["so0"], p.n(g["so0"])), (g["so1"], p.n(g["so1"]))))


def build_l1b(T=2048):
    nc = bass.Bass("TRN2", target_bir_lowering=False)
    n_ic = D // 128
    cio = {
        "xin": nc.dram_tensor("xin", [T, D], F32, kind="ExternalInput").ap(),
        "qw": nc.dram_tensor("qw", [128, n_ic * 128], F32, kind="ExternalInput").ap(),
        "gam": nc.dram_tensor("gam", [128, n_ic], F32, kind="ExternalInput").ap(),
        "wout": nc.dram_tensor("wout", [128, D], F32, kind="ExternalInput").ap(),
        "kvp4": nc.dram_tensor("kvp4", [4, 17, NH1 * DPHI], F32, kind="ExternalInput").ap(),
        "xout": nc.dram_tensor("xout", [T, D], F32, kind="ExternalOutput").ap(),
    }
    id_bf = nc.inline_tensor(np.eye(128, dtype=ml_dtypes.bfloat16), name="id_bf")
    idb_sb = nc.alloc_sbuf_tensor("idb_sb", [128, 128], BF16)
    cio["idb"] = idb_sb
    cio["P"] = [nc.alloc_psum_tensor(f"P{i}", [128, 512], F32) for i in range(4)]
    cio["TPS"] = [nc.alloc_psum_tensor(f"Tp{i}", [128, 1024], BF16) for i in range(2)]
    cio["PH"] = [nc.alloc_psum_tensor(f"Ph{i}", [128, 512], BF16) for i in range(2)]
    ar = Arena(nc, n_ic * T + T + 16000, 4 * D + 3 * NH1 * DPHI + 4096)
    p = Prog(nc)
    p.sem("common_aux")

    def f_id1(e, S):
        return e.dma_start(out=idb_sb.ap(), in_=id_bf.ap())
    p.op("sync", f_id1, inc="common_aux", k=16)
    cio["aux_n"] = p.n("common_aux")
    emit_l1b(p, nc, ar, cio, "B_", T=T)
    with ExitStack() as es:
        p.emit(es)
    return nc


def l1_host_inputs(gamma, w_q, w_kv, w_out):
    w_kv = np.asarray(w_kv)
    w_q = np.asarray(w_q)
    return {
        "kvw": swz_lhsT(np.ascontiguousarray(w_kv.T)),
        "qw": swz_lhsT(np.ascontiguousarray(w_q.T))[0],
        "gam": np.ascontiguousarray(np.asarray(gamma).reshape(8, 128).T,
                                    dtype=np.float32),
        "wout": np.ascontiguousarray(np.asarray(w_out), dtype=np.float32),
    }





NH0 = 16
DH0 = 64
SCALE0 = DH0 ** -0.5


def emit_l0(p, nc, ar, cio, pf, S=8192, TQ=2048):
    n_ic = D // 128
    nkt = S // 128
    nkb = S // 512
    nqc = TQ // 512
    npair = NH0 // 2

    xfull, xq = cio["xfull"], cio["xq"]
    wq, wl, wk, wv, wo = cio["wq"], cio["wl"], cio["wk"], cio["wv"], cio["wo"]
    bq, bl, bk, bv, bo = cio["bq"], cio["bl"], cio["bk"], cio["bv"], cio["bo"]
    xout = cio["xout"]
    idb_sb, on_sb = cio["idb"], cio["onb"]
    P, SS, TPS = cio["P2"], cio["SS"], cio["TPS"]

    xf_t = ar.sb("xf_t", 2 * D, F32)
    u_tm = ar.sb("u_tm", 4 * D, BF16)
    latent = ar.sb("latent", 2 * S, BF16)
    q_fm = ar.sb("q_fm", n_ic * TQ, BF16)
    k_sb = ar.sb("k_sb", S, BF16)
    v_sb = ar.sb("v_sb", nkt * 130, BF16)
    ao_fm = ar.sb("ao_fm", n_ic * TQ, BF16)
    if n_ic * TQ >= n_ic * D + 2 * D + n_ic * 512 + 2 * D:
        _b = ao_fm.base
        wq_ov = View(ao_fm.t, _b, n_ic * D); _b += n_ic * D
        wl_ov = View(ao_fm.t, _b, 2 * D); _b += 2 * D
        u0qs = View(ao_fm.t, _b, n_ic * 512); _b += n_ic * 512
        u0s = View(ao_fm.t, _b, 2 * D)
    else:
        wq_ov = ar.sb("wq_sb", n_ic * D, BF16)
        wl_ov = ar.sb("wl_sb", 2 * D, BF16)
        u0qs = ar.sb("u0qs", n_ic * 512, BF16)
        u0s = ar.sb("u0s", 2 * D, BF16)
    pexp = ar.sb("pexp", 8 * 1024, BF16)
    rr_sb = ar.sb("rr_sb", 1024, BF16)
    rrf_sb = ar.sb("rrf_sb", 1024, F32)
    rb_sb = ar.sb("rb_sb", 512, F32)
    rbr_sb = ar.sb("rbr_sb", 512, F32)
    aon_sb = ar.sb("aon_sb", 512, BF16)
    wq_sb = wq_ov
    wl_sb = wl_ov
    wk_sb = ar.sb("wk_sb", n_ic * 256, BF16)
    wv_sb = ar.sb("wv_sb", 2 * D, BF16)
    wo_late = 2 * S >= n_ic * D
    if wo_late:
        wo_sb = View(latent.t, latent.base, n_ic * D)
    else:
        wo_sb = ar.sb("wo_sb", n_ic * D, BF16)
    bq_sb = ar.sb("bq_sb", D, BF16)
    bl_sb = ar.sb("bl_sb", 256, BF16)
    bk_sb = ar.sb("bk_sb", D, BF16)
    bv_sb = ar.sb("bv_sb", D, BF16)
    bo_sb = ar.sb("bo_sb", D, BF16)
    y_tm = ar.sb("y_tm", D, F32)
    xo_t = ar.sb("xo_t", 2 * D, F32)
    st = ar.sb("st", 16, F32)

    names = ["aux", "wld", "xf0", "xf1", "xf2", "xf3", "xq0", "xq1",
             "tr", "mlat", "mq", "mk", "mv", "msc", "mav", "mbc", "my",
             "so0", "so1"]
    g = {nm: pf + nm for nm in names}
    p.sem(*[g[nm] for nm in names])
    AUXW = ("common_aux", cio["aux_n"])

    w_list = [(wq_sb, wq.rearrange("a p c -> p a c")),
              (wl_sb, wl.rearrange("a p c -> p a c")),
              (wk_sb, wk.rearrange("a p c -> p a c")),
              (wv_sb, wv.rearrange("a p c -> p a c")),
              (bq_sb, bq), (bl_sb, bl), (bk_sb, bk),
              (bv_sb, bv), (bo_sb, bo)]
    if not wo_late:
        w_list.append((wo_sb, wo.rearrange("a p c -> p a c")))
    for dst, srcap in w_list:
        def f_w(e, S_, dst=dst, srcap=srcap):
            o = dst.ap()
            if srcap.shape[0] == 1:
                o = dst[0:1, :]
            return e.dma_start(out=o, in_=srcap)
        p.op("gpsimd", f_w, inc=g["wld"], k=16, n=0)
    WLDN = p.n(g["wld"])

    xf4 = View(xf_t.t, xf_t.base, 4 * D)
    xf_free = [None, None, None, None]
    tr_done = [0, 0, 0, 0]
    tps_free = [0, 0]
    lat_ev = [0, 0]
    for kt in range(nkt):
        slot = kt % 2
        ls = kt % 4
        xf = g[f"xf{ls}"]

        def f_ld(e, S_, kt=kt, ls=ls):
            return e.dma_start(out=xf4[:, ls * D:(ls + 1) * D],
                               in_=xfull[kt * 128:(kt + 1) * 128, :])
        w = (xf_free[ls],) if xf_free[ls] else ()
        p.op("sync", f_ld, waits=w, inc=xf, k=16)
        nx = p.n(xf)

        src = (lambda ls=ls: xf4[:, ls * D:(ls + 1) * D])
        sq_waits = [(xf, nx)]
        if tr_done[ls]:
            sq_waits.append((g["tr"], tr_done[ls]))
        r_ap, mr_ap = emit_ln_stats(
            p, src, st, 0, ((xf, nx),), sq_waits,
            lambda ls=ls: u_tm[:, ls * D:(ls + 1) * D])

        def f_ap(v, S_, ls=ls, r=r_ap, mr=mr_ap):
            return v.tensor_scalar(out=u_tm[:, ls * D:(ls + 1) * D],
                                   in0=xf4[:, ls * D:(ls + 1) * D],
                                   scalar1=r(), scalar2=mr(),
                                   op0=ALU.mult, op1=ALU.add)
        p.op("vector", f_ap, n=1)
        n_ua = p.n("sv")
        xf_free[ls] = ("sv", n_ua)

        tr_waits = [("sv", n_ua), AUXW]
        if tps_free[slot]:
            tr_waits.append(("sv", tps_free[slot]))

        def f_tr(t, S_, slot=slot, ls=ls):
            last = None
            for j in range(n_ic):
                last = t.transpose(
                    sb_ap(TPS[slot], j * 128, [[1, 128]]),
                    u_tm[:, ls * D + j * 128: ls * D + (j + 1) * 128],
                    idb_sb.ap())
            return last
        p.op("tensor", f_tr, waits=tr_waits, inc=g["tr"])
        tr_done[ls] = p.n(g["tr"])

        def f_uev(v, S_, slot=slot):
            return v.tensor_copy(out=u0s[:, slot * D:(slot + 1) * D],
                                 in_=TPS[slot].ap()[:, 0:D])
        p.op("vector", f_uev, waits=((g["tr"], p.n(g["tr"])),), n=1)
        tps_free[slot] = p.n("sv")
        n_uev = p.n("sv")

        sg = SS[kt % 2]
        mm_waits = [("sv", n_uev), (g["wld"], WLDN)]
        if lat_ev[kt % 2]:
            mm_waits.append(("sv", lat_ev[kt % 2]))

        def f_mlat(t, S_, slot=slot, sg=sg):
            last = None
            for lc in range(2):
                co = lc * 512
                t.matmul(sg[:, co:co + 128], bl_sb[0:1, lc * 128:(lc + 1) * 128],
                         on_sb[0:1, 0:128], start=True, stop=False)
                for ic in range(n_ic):
                    last = t.matmul(
                        sg[:, co:co + 128],
                        wl_sb[:, lc * D + ic * 128: lc * D + (ic + 1) * 128],
                        u0s[:, slot * D + ic * 128: slot * D + (ic + 1) * 128],
                        start=False, stop=(ic == n_ic - 1))
            return last
        p.op("tensor", f_mlat, waits=mm_waits, inc=g["mlat"])

        def f_lev(v, S_, kt=kt, sg=sg):
            v.tensor_copy(out=latent[:, kt * 128:(kt + 1) * 128],
                          in_=sg[:, 0:128])
            return v.tensor_copy(out=latent[:, S + kt * 128: S + (kt + 1) * 128],
                                 in_=sg[:, 512:640])
        p.op("vector", f_lev, waits=((g["mlat"], p.n(g["mlat"])),), n=2)
        lat_ev[kt % 2] = p.n("sv")
    lat_done = p.n("sv")
    mlat_done = p.n(g["mlat"])

    xq_free = [None, None]
    qtps_free = [0, 0]
    q_ev = [0, 0]
    for qc in range(nqc):
        uq_start = p.n("sv")
        for t4 in range(4):
            i = qc * 4 + t4
            slot = i % 2
            xqs = g[f"xq{slot}"]

            def f_ld(e, S_, i=i, slot=slot):
                return e.dma_start(out=xf_t[:, slot * D:(slot + 1) * D],
                                   in_=xq[i * 128:(i + 1) * 128, :])
            w = [xq_free[slot]] if xq_free[slot] else []
            if i < 2:
                w.append(xf_free[slot])
            p.op("sync", f_ld, waits=w, inc=xqs, k=16)
            nx = p.n(xqs)

            src = (lambda slot=slot: xf_t[:, slot * D:(slot + 1) * D])
            sq_waits = [(xqs, nx)]
            if qtps_free[slot] or tps_free[slot]:
                sq_waits.append((g["tr"], p.n(g["tr"])))
            r_ap, mr_ap = emit_ln_stats(
                p, src, st, 0, ((xqs, nx),), sq_waits,
                lambda slot=slot: u_tm[:, slot * D:(slot + 1) * D])

            def f_ap(v, S_, slot=slot, r=r_ap, mr=mr_ap):
                return v.tensor_scalar(out=u_tm[:, slot * D:(slot + 1) * D],
                                       in0=xf_t[:, slot * D:(slot + 1) * D],
                                       scalar1=r(), scalar2=mr(),
                                       op0=ALU.mult, op1=ALU.add)
            p.op("vector", f_ap, n=1)
            n_ua = p.n("sv")
            xq_free[slot] = ("sv", n_ua)

            tr_waits = [("sv", n_ua), AUXW]
            gprev = qtps_free[slot] or tps_free[slot]
            if gprev:
                tr_waits.append(("sv", gprev))
            if q_ev[0] or q_ev[1]:
                tr_waits.append((g["mq"], p.n(g["mq"])))

            def f_tr(t, S_, slot=slot):
                last = None
                for j in range(n_ic):
                    last = t.transpose(
                        sb_ap(TPS[slot], j * 128, [[1, 128]]),
                        u_tm[:, slot * D + j * 128: slot * D + (j + 1) * 128],
                        idb_sb.ap())
                return last
            p.op("tensor", f_tr, waits=tr_waits, inc=g["tr"])

            def f_uev(v, S_, slot=slot, t4=t4):
                return v.tensor_copy(
                    out=sb_ap(u0qs, t4 * 128, [[512, n_ic], [1, 128]]),
                    in_=sb_ap(TPS[slot], 0, [[128, n_ic], [1, 128]]))
            p.op("vector", f_uev, waits=((g["tr"], p.n(g["tr"])),), n=1)
            qtps_free[slot] = p.n("sv")
        uq_done = p.n("sv")

        for oc in range(n_ic):
            pb = p.n(g["mq"]) % 2
            mm_waits = [("sv", uq_done), (g["wld"], WLDN)]
            if q_ev[pb]:
                mm_waits.append(("sv", q_ev[pb]))

            def f_mq(t, S_, oc=oc, pb=pb):
                ps = SS[pb]
                t.matmul(ps[:, 0:512], bq_sb[0:1, oc * 128:(oc + 1) * 128],
                         on_sb[0:1, 0:512], start=True, stop=False)
                last = None
                for ic in range(n_ic):
                    last = t.matmul(
                        ps[:, 0:512],
                        wq_sb[:, oc * D + ic * 128: oc * D + (ic + 1) * 128],
                        u0qs[:, ic * 512:(ic + 1) * 512],
                        start=False, stop=(ic == n_ic - 1))
                return last
            p.op("tensor", f_mq, waits=mm_waits, inc=g["mq"])

            def f_qev(v, S_, oc=oc, qc=qc, pb=pb):
                return v.tensor_scalar(
                    out=q_fm[:, oc * TQ + qc * 512: oc * TQ + (qc + 1) * 512],
                    in0=SS[pb][:, 0:512], scalar1=SCALE0, scalar2=None,
                    op0=ALU.mult)
            p.op("vector", f_qev, waits=((g["mq"], p.n(g["mq"])),), n=1)
            q_ev[pb] = p.n("sv")
    q_done = p.n("sv")
    mq_done = p.n(g["mq"])

    TPK = TPS[0].bitcast(F32)
    TPV = TPS[1].bitcast(F32)
    kev_free = [0]
    vev_free = [0]
    k_ready = [0]
    v_ready = [0]
    ao_written = 0

    def f_vones(v, S_):
        return v.memset(sb_ap(v_sb, 64, [[65, 2 * nkt], [1, 1]]), 1.0)
    p.op("vector", f_vones, n=1)

    def emit_k(hp, kb):
        mm_waits = [("sv", kev_free[0] if kev_free[0] else q_done),
                    ("sv", lat_done), (g["wld"], WLDN)]

        def f_mk(t, S_, hp=hp, kb=kb):
            t.matmul(TPK[:, 0:512], bk_sb[0:1, hp * 128:(hp + 1) * 128],
                     on_sb[0:1, 0:512], start=True, stop=False)
            last = None
            for lc in range(2):
                last = t.matmul(
                    TPK[:, 0:512],
                    wk_sb[:, hp * 256 + lc * 128: hp * 256 + (lc + 1) * 128],
                    latent[:, lc * S + kb * 512: lc * S + (kb + 1) * 512],
                    start=False, stop=(lc == 1))
            return last
        p.op("tensor", f_mk, waits=mm_waits, inc=g["mk"])

        def f_kev(v, S_, kb=kb):
            return v.tensor_copy(out=k_sb[:, kb * 512:(kb + 1) * 512],
                                 in_=TPK[:, 0:512])
        p.op("vector", f_kev, waits=((g["mk"], p.n(g["mk"])),), n=1)
        kev_free[0] = p.n("sv")
        k_ready[0] = p.n("sv")

    def emit_v(hp, kg):
        mm_waits = [("sv", vev_free[0] if vev_free[0] else q_done),
                    ("sv", lat_done), (g["wld"], WLDN)]

        def f_mv(t, S_, hp=hp, kg=kg):
            last = None
            for k4 in range(4):
                kt = kg * 4 + k4
                co = k4 * 128
                t.matmul(TPV[:, co:co + 128], on_sb[0:1, 0:128],
                         bv_sb[0:1, hp * 128:(hp + 1) * 128],
                         start=True, stop=False)
                for lc in range(2):
                    last = t.matmul(
                        TPV[:, co:co + 128],
                        latent[:, lc * S + kt * 128: lc * S + (kt + 1) * 128],
                        wv_sb[:, lc * D + hp * 128: lc * D + (hp + 1) * 128],
                        start=False, stop=(lc == 1))
            return last
        p.op("tensor", f_mv, waits=mm_waits, inc=g["mv"])

        def f_vev(v, S_, kg=kg):
            return v.tensor_copy(
                out=sb_ap(v_sb, kg * 4 * 130, [[130, 4], [65, 2], [1, 64]]),
                in_=bass.AP(TPV, 0, [[512, 128], [128, 4], [64, 2], [1, 64]]))
        p.op("vector", f_vev, waits=((g["mv"], p.n(g["mv"])),), n=1)
        vev_free[0] = p.n("sv")
        v_ready[0] = p.n("sv")

    def emit_norm(hp, qc):
        nonlocal rb_done, ao_written
        n_mav = p.n(g["mav"])

        def f_an(v, S_):
            v.tensor_copy(out=aon_sb[0:64, :], in_=P[0][0:64, 0:512])
            v.tensor_copy(out=aon_sb[64:128, :], in_=P[1][0:64, 0:512])
            v.tensor_copy(out=rr_sb[0:1, 0:512], in_=P[0][64:65, 0:512])
            return v.tensor_copy(out=rr_sb[0:1, 512:1024],
                                 in_=P[1][64:65, 0:512])
        p.op("vector", f_an, waits=((g["mav"], n_mav),), n=4)
        n_an = p.n("sv")

        def f_bc(t, S_):
            t.matmul(P[0][0:64, 0:512], on_sb[0:1, 0:64],
                     rr_sb[0:1, 0:512], start=True, stop=True)
            return t.matmul(P[1][0:64, 0:512], on_sb[0:1, 0:64],
                            rr_sb[0:1, 512:1024], start=True, stop=True)
        p.op("tensor", f_bc, waits=(("sv", n_an),), inc=g["mbc"])

        def f_rb(v, S_):
            v.tensor_copy(out=rb_sb[0:64, :], in_=P[0][0:64, 0:512])
            return v.tensor_copy(out=rb_sb[64:128, :], in_=P[1][0:64, 0:512])
        p.op("vector", f_rb, waits=((g["mbc"], p.n(g["mbc"])),), n=2)
        rb_done = p.n("sv")

        def f_rbr(v, S_):
            return v.reciprocal(out=rbr_sb.ap(), in_=rb_sb.ap())
        p.op("vector", f_rbr, n=1)

        def f_ao(v, S_, hp=hp, qc=qc):
            v.tensor_tensor(
                out=g_ap(ao_fm, 0, 64, hp * TQ + qc * 512, [[1, 512]]),
                in0=aon_sb[0:64, :], in1=rbr_sb[0:64, :], op=ALU.mult)
            return v.tensor_tensor(
                out=g_ap(ao_fm, 64, 64, hp * TQ + qc * 512, [[1, 512]]),
                in0=aon_sb[64:128, :], in1=rbr_sb[64:128, :], op=ALU.mult)
        ao_waits = []
        if hp == 0 and qc == 0:
            ao_waits = [(g["mq"], mq_done), (g["mlat"], mlat_done)]
        p.op("vector", f_ao, waits=ao_waits, n=2)
        ao_written = p.n("sv")

    for j in range(nkb):
        emit_k(0, j)
        emit_v(0, j)

    av_read = [0] * 8
    prev_gen_exp = [0, 0]
    rb_done = 0
    esl = [0]
    pending = [None]
    for hp in range(npair):
        pk_ready, pv_ready = k_ready[0], v_ready[0]
        for qc in range(nqc):
            interleave = (qc == nqc - 1) and (hp + 1 < npair)
            kts = list(range(nkt - 1, -1, -1)) if interleave \
                else list(range(nkt))
            exp_done = [0] * nkt
            exp_slot = [0] * nkt
            n_av = [0]

            def emit_av(j, qc=qc, pv_ready=pv_ready, n_av=n_av,
                        exp_done=exp_done, exp_slot=exp_slot):
                sl = exp_slot[j]
                av_waits = [("ss", exp_done[j])]
                if n_av[0] == 0:
                    av_waits.append(("sv", rb_done if rb_done else ao_written))
                    if qc == 0:
                        av_waits.append(("sv", pv_ready))
                first = (n_av[0] == 0)
                lastf = (n_av[0] == nkt - 1)

                def f_av(t, S_, j=j, sl=sl, first=first, lastf=lastf):
                    t.matmul(P[0][0:65, 0:512],
                             v_sb[:, j * 130: j * 130 + 65],
                             pexp[:, sl * 1024: sl * 1024 + 512],
                             start=first, stop=lastf)
                    return t.matmul(
                        P[1][0:65, 0:512],
                        v_sb[:, j * 130 + 65: j * 130 + 130],
                        pexp[:, sl * 1024 + 512: sl * 1024 + 1024],
                        start=first, stop=lastf)
                p.op("tensor", f_av, waits=av_waits, inc=g["mav"])
                av_read[sl] = p.n(g["mav"])
                n_av[0] += 1

            for i, kt in enumerate(kts):
                gen = kt % 2
                sg = SS[gen]
                sc_waits = [("sv", pk_ready)]
                if prev_gen_exp[gen]:
                    sc_waits.append(("ss", prev_gen_exp[gen]))

                def f_sc(t, S_, hp=hp, qc=qc, kt=kt, sg=sg):
                    t.matmul(sg[:, 0:512],
                             k_sb[0:64, kt * 128:(kt + 1) * 128],
                             g_ap(q_fm, 0, 64, hp * TQ + qc * 512, [[1, 512]]),
                             start=True, stop=True)
                    return t.matmul(
                        sg[:, 512:1024],
                        k_sb[64:128, kt * 128:(kt + 1) * 128],
                        g_ap(q_fm, 64, 64, hp * TQ + qc * 512, [[1, 512]]),
                        start=True, stop=True)
                p.op("tensor", f_sc, waits=sc_waits, inc=g["msc"])
                n_msc = p.n(g["msc"])

                if interleave and kt % 4 == 0:
                    emit_k(hp + 1, kt // 4)

                if i == 1 and pending[0]:
                    pending[0]()
                    pending[0] = None

                if i >= 4:
                    j = kts[i - 4]
                    emit_av(j)
                    if interleave and j % 4 == 0:
                        emit_v(hp + 1, j // 4)

                psl = esl[0] % 8
                esl[0] += 1
                exp_slot[kt] = psl
                exp_waits = [(g["msc"], n_msc)]
                if av_read[psl]:
                    exp_waits.append((g["mav"], av_read[psl]))

                def f_exp(s, S_, sg=sg, psl=psl):
                    return s.activation(out=pexp[:, psl * 1024:(psl + 1) * 1024],
                                        in_=sg.ap(), func=AF.Exp)
                p.op("scalar", f_exp, waits=exp_waits, n=1, no_pre=True)
                exp_done[kt] = p.n("ss")
                prev_gen_exp[gen] = p.n("ss")

            def flush_tail(hp=hp, qc=qc, kts=kts, interleave=interleave,
                           emit_av=emit_av):
                for i in range(nkt - 4, nkt):
                    j = kts[i]
                    emit_av(j)
                    if interleave and j % 4 == 0:
                        emit_v(hp + 1, j // 4)
                emit_norm(hp, qc)
            pending[0] = flush_tail
    pending[0]()
    pending[0] = None
    flash_done_sv = p.n("sv")

    if wo_late:
        n_mk_all, n_mv_all = p.n(g["mk"]), p.n(g["mv"])

        def f_wo(e, S_):
            return e.dma_start(out=wo_sb.ap(),
                               in_=wo.rearrange("a p c -> p a c"))
        p.op("gpsimd", f_wo, waits=((g["mk"], n_mk_all), (g["mv"], n_mv_all)),
             inc=g["wld"], k=16, n=0)
    WLDN2 = p.n(g["wld"])

    yev_free = [0, 0]
    for qt in range(TQ // 128):
        slot = qt % 2
        for oh in range(2):
            pb = oh
            mm_waits = [("sv", flash_done_sv), (g["wld"], WLDN2)]
            if yev_free[pb]:
                mm_waits.append(("sv", yev_free[pb]))

            def f_my(t, S_, qt=qt, oh=oh, pb=pb):
                ps = SS[pb]
                t.matmul(ps[:, 0:512], on_sb[0:1, 0:128],
                         bo_sb[0:1, oh * 512:(oh + 1) * 512],
                         start=True, stop=False)
                last = None
                for ic in range(n_ic):
                    last = t.matmul(
                        ps[:, 0:512],
                        ao_fm[:, ic * TQ + qt * 128: ic * TQ + (qt + 1) * 128],
                        wo_sb[:, ic * D + oh * 512: ic * D + (oh + 1) * 512],
                        start=False, stop=(ic == n_ic - 1))
                return last
            p.op("tensor", f_my, waits=mm_waits, inc=g["my"])

            def f_yev(v, S_, oh=oh, pb=pb):
                return v.tensor_copy(out=y_tm[:, oh * 512:(oh + 1) * 512],
                                     in_=SS[pb][:, 0:512])
            p.op("vector", f_yev, waits=((g["my"], p.n(g["my"])),), n=1)
            yev_free[pb] = p.n("sv")
        n_yev = p.n("sv")

        xqs = g[f"xq{slot}"]

        def f_xr(e, S_, qt=qt, slot=slot):
            return e.dma_start(out=xf_t[:, slot * D:(slot + 1) * D],
                               in_=xq[qt * 128:(qt + 1) * 128, :])
        w = (xq_free[slot],) if xq_free[slot] else ()
        p.op("sync", f_xr, waits=w, inc=xqs, k=16)
        nx = p.n(xqs)

        r_ap, mr_ap = emit_ln_stats(p, lambda: y_tm.ap(), st, 8,
                                    (("sv", n_yev),), (("sv", n_yev),),
                                    lambda: u_tm[:, 0:D])

        so = g[f"so{slot}"]

        def f_res(v, S_, slot=slot, r=r_ap, mr=mr_ap):
            v.tensor_scalar(out=xo_t[:, slot * D:(slot + 1) * D],
                            in0=y_tm.ap(), scalar1=r(), scalar2=mr(),
                            op0=ALU.mult, op1=ALU.add)
            return v.tensor_tensor(out=xo_t[:, slot * D:(slot + 1) * D],
                                   in0=xo_t[:, slot * D:(slot + 1) * D],
                                   in1=xf_t[:, slot * D:(slot + 1) * D],
                                   op=ALU.add)
        res_waits = [(xqs, nx)]
        if p.n(so) > 0:
            res_waits.append((so, p.n(so)))
        p.op("vector", f_res, waits=res_waits, n=2)
        xq_free[slot] = ("sv", p.n("sv"))

        def f_sto(e, S_, qt=qt, slot=slot):
            return e.dma_start(out=xout[qt * 128:(qt + 1) * 128, :],
                               in_=xo_t[:, slot * D:(slot + 1) * D])
        p.op("sync", f_sto, waits=(("sv", p.n("sv")),), inc=so, k=16)

    p.op("sync", lambda e, S_: e.nop(),
         waits=((g["so0"], p.n(g["so0"])), (g["so1"], p.n(g["so1"]))))


def build_l0(S=8192, TQ=2048):
    nc = bass.Bass("TRN2", target_bir_lowering=False)
    n_ic = D // 128
    cio = {
        "xfull": nc.dram_tensor("xfull", [S, D], F32, kind="ExternalInput").ap(),
        "xq": nc.dram_tensor("xq", [TQ, D], F32, kind="ExternalInput").ap(),
        "wq": nc.dram_tensor("wq", [n_ic, 128, D], F32, kind="ExternalInput").ap(),
        "wl": nc.dram_tensor("wl", [2, 128, D], F32, kind="ExternalInput").ap(),
        "wk": nc.dram_tensor("wk", [n_ic, 128, 256], F32, kind="ExternalInput").ap(),
        "wv": nc.dram_tensor("wv", [2, 128, D], F32, kind="ExternalInput").ap(),
        "wo": nc.dram_tensor("wo", [n_ic, 128, D], F32, kind="ExternalInput").ap(),
        "bq": nc.dram_tensor("bq", [1, D], F32, kind="ExternalInput").ap(),
        "bl": nc.dram_tensor("bl", [1, 256], F32, kind="ExternalInput").ap(),
        "bk": nc.dram_tensor("bk", [1, D], F32, kind="ExternalInput").ap(),
        "bv": nc.dram_tensor("bv", [1, D], F32, kind="ExternalInput").ap(),
        "bo": nc.dram_tensor("bo", [1, D], F32, kind="ExternalInput").ap(),
        "xout": nc.dram_tensor("xout", [TQ, D], F32, kind="ExternalOutput").ap(),
    }
    id_bf = nc.inline_tensor(np.eye(128, dtype=ml_dtypes.bfloat16), name="id_bf")
    ones_bf = nc.inline_tensor(np.ones((1, 512), dtype=ml_dtypes.bfloat16),
                               name="ones_bf")
    idb_sb = nc.alloc_sbuf_tensor("idb_sb", [128, 128], BF16)
    on_sb = nc.alloc_sbuf_tensor("on_sb", [1, 512], BF16)
    cio["idb"], cio["onb"] = idb_sb, on_sb
    cio["P2"] = [nc.alloc_psum_tensor(f"P{i}", [128, 512], F32) for i in range(2)]
    cio["SS"] = [nc.alloc_psum_tensor(f"S{i}", [128, 1024], F32) for i in range(2)]
    cio["TPS"] = [nc.alloc_psum_tensor(f"Tp{i}", [128, 1024], BF16) for i in range(2)]
    ovl = n_ic * D + 2 * D + n_ic * 512 + 2 * D
    extra = 0 if n_ic * TQ >= ovl else ovl + 256
    wo_cols = 0 if 2 * S >= n_ic * D else n_ic * D
    ar = Arena(nc, 2 * S + n_ic * TQ + S + (S // 128) * 130 + n_ic * TQ +
               4 * D + 2 * 512 + 1024 + 512 + n_ic * 256 + 2 * D + wo_cols +
               5 * D + 256 + 8192 + extra,
               7408)
    p = Prog(nc)
    p.sem("common_aux")

    def f_id1(e, S_):
        return e.dma_start(out=idb_sb.ap(), in_=id_bf.ap())
    p.op("sync", f_id1, inc="common_aux", k=16)

    def f_on(e, S_):
        return e.dma_start(out=on_sb.ap(), in_=ones_bf.ap())
    p.op("sync", f_on, inc="common_aux", k=16)
    cio["aux_n"] = p.n("common_aux")
    emit_l0(p, nc, ar, cio, "L_", S=S, TQ=TQ)
    with ExitStack() as es:
        p.emit(es)
    return nc


def l0_host_inputs(Wq, bq, Wl, bl, Wk, bk, Wv, bv, Wo, bo):
    f = np.float32
    return {
        "wq": swz_lhsT(np.ascontiguousarray(np.asarray(Wq).T)),
        "wl": swz_lhsT(np.ascontiguousarray(np.asarray(Wl).T)),
        "wk": swz_lhsT(np.ascontiguousarray(np.asarray(Wk).T)),
        "wv": np.ascontiguousarray(np.asarray(Wv).reshape(2, 128, D), f),
        "wo": np.ascontiguousarray(np.asarray(Wo).reshape(8, 128, D), f),
        "bq": np.ascontiguousarray(np.asarray(bq).reshape(1, D), f),
        "bl": np.ascontiguousarray(np.asarray(bl).reshape(1, 256), f),
        "bk": np.ascontiguousarray(np.asarray(bk).reshape(1, D), f),
        "bv": np.ascontiguousarray(np.asarray(bv).reshape(1, D), f),
        "bo": np.ascontiguousarray(np.asarray(bo).reshape(1, D), f),
    }





def build_fused(S=8192, TQ=2048, TC=512):
    n_ic = D // 128
    n_oc = DH // 128
    nc = bass.Bass("TRN2", target_bir_lowering=False)

    def ext(nm, shape):
        return nc.dram_tensor(nm, shape, F32, kind="ExternalInput").ap()

    l0 = {
        "xfull": ext("xfull", [S, D]), "xq": ext("xq", [TQ, D]),
        "wq": ext("l0_wq", [n_ic, 128, D]), "wl": ext("l0_wl", [2, 128, D]),
        "wk": ext("l0_wk", [n_ic, 128, 256]), "wv": ext("l0_wv", [2, 128, D]),
        "wo": ext("l0_wo", [n_ic, 128, D]),
        "bq": ext("l0_bq", [1, D]), "bl": ext("l0_bl", [1, 256]),
        "bk": ext("l0_bk", [1, D]), "bv": ext("l0_bv", [1, D]),
        "bo": ext("l0_bo", [1, D]),
    }
    ffn_shapes = {
        "w1s": [n_oc, 128, n_ic * 128], "w2s": [n_oc, 128, n_ic * 128],
        "h1s": [n_oc, 128, n_oc * 128], "h2s": [n_oc, 128, n_oc * 128],
        "w3s": [n_ic, 128, n_oc * 128],
        "sc12": [128, 2 * n_oc], "sch": [128, 2 * n_oc], "sc3": [128, n_ic],
    }
    f0 = {k: ext("f0_" + k, sh) for k, sh in ffn_shapes.items()}
    f1 = {k: ext("f1_" + k, sh) for k, sh in ffn_shapes.items()}
    l1 = {
        "kvw": ext("l1_kvw", [2, 128, n_ic * 128]),
        "gam": ext("l1_gam", [128, n_ic]),
        "qw": ext("l1_qw", [128, n_ic * 128]),
        "wout": ext("l1_wout", [128, D]),
    }
    xout = nc.dram_tensor("xout", [TQ, D], F32, kind="ExternalOutput").ap()

    x1 = nc.dram_tensor("x1", [TQ, D], F32, kind="Internal").ap()
    x2 = nc.dram_tensor("x2", [TQ, D], F32, kind="Internal").ap()
    x3 = nc.dram_tensor("x3", [TQ, D], F32, kind="Internal").ap()
    cc_in = nc.dram_tensor("cc_in", [17, NH1 * DPHI], F32, kind="Internal")
    cc_out = nc.dram_tensor("cc_out", [17, NH1 * DPHI], F32, kind="Internal")

    id_bf = nc.inline_tensor(np.eye(128, dtype=ml_dtypes.bfloat16), name="id_bf")
    id_f32 = nc.inline_tensor(np.eye(128, dtype=np.float32), name="id_f32")
    ones_bf = nc.inline_tensor(np.ones((1, 512), dtype=ml_dtypes.bfloat16),
                               name="ones_bf")
    idb_sb = nc.alloc_sbuf_tensor("idb_sb", [128, 128], BF16)
    idf_sb = nc.alloc_sbuf_tensor("idf_sb", [128, 128], F32)
    on_sb = nc.alloc_sbuf_tensor("on_sb", [1, 512], BF16)

    P2 = [nc.alloc_psum_tensor(f"P{i}", [128, 512], F32) for i in range(2)]
    SS = [nc.alloc_psum_tensor(f"S{i}", [128, 1024], F32) for i in range(2)]
    TPS = [nc.alloc_psum_tensor(f"Tp{i}", [128, 1024], BF16) for i in range(2)]
    P4 = [P2[0], P2[1], View(SS[0], 0, 512), View(SS[1], 0, 512)]
    P6 = [P2[0], P2[1], View(SS[0], 0, 512), View(SS[0], 512, 512),
          View(SS[1], 0, 512), View(SS[1], 512, 512)]

    ar = Arena(nc, 90496, 7408)
    p = Prog(nc)
    p.sem("common_aux", "ccs")

    for dst, srcap in [(idb_sb, id_bf.ap()), (idf_sb, id_f32.ap()),
                       (on_sb, ones_bf.ap())]:
        def f_c(e, S_, dst=dst, srcap=srcap):
            return e.dma_start(out=dst.ap(), in_=srcap)
        p.op("sync", f_c, inc="common_aux", k=16)
    aux_n = p.n("common_aux")
    common = {"idb": idb_sb, "idf": idf_sb, "onb": on_sb, "aux_n": aux_n}

    cio = dict(l0)
    cio.update(common)
    cio["xout"] = x1
    cio["P2"], cio["SS"], cio["TPS"] = P2, SS, TPS
    emit_l0(p, nc, ar, cio, "L_", S=S, TQ=TQ)
    p.barrier()
    ar.reset()

    cio = dict(f0)
    cio.update(common)
    cio["xin"], cio["xout"] = x1, x2
    cio["P"], cio["TPS"] = P6, TPS
    emit_ffn(p, nc, ar, cio, "F_", T=TQ, TC=TC)
    p.barrier()
    ar.reset()

    cio = {"xin": x2, "kvw": l1["kvw"], "gam": l1["gam"], "kvp": cc_in.ap()}
    cio.update(common)
    cio["P"], cio["TPS"] = P4, TPS
    emit_l1a(p, nc, ar, cio, "A_", T=TQ)

    def f_cc(e, S_):
        return e.collective_compute(
            "AllReduce", ALU.add,
            replica_groups=[[0, 1, 2, 3], [4, 5, 6, 7]],
            ins=[cc_in.ap().opt()], outs=[cc_out.ap().opt()])
    p.op("gpsimd", f_cc, waits=(("A_sto", p.n("A_sto")),), inc="ccs", k=1, n=0)
    p.barrier()
    ar.reset()

    cio = {"xin": x2, "qw": l1["qw"], "gam": l1["gam"], "wout": l1["wout"],
           "xout": x3, "cc_out": cc_out.ap(), "ccs_n": 1}
    cio.update(common)
    cio["P"], cio["TPS"] = P4, TPS
    cio["PH"] = TPS
    emit_l1b(p, nc, ar, cio, "B_", T=TQ)
    p.barrier()
    ar.reset()

    cio = dict(f1)
    cio.update(common)
    cio["xin"], cio["xout"] = x3, xout
    cio["P"], cio["TPS"] = P6, TPS
    emit_ffn(p, nc, ar, cio, "G_", T=TQ, TC=TC)

    with ExitStack() as es:
        p.emit(es)
    return nc


def kernel_fused(inputs):
    x0 = np.ascontiguousarray(np.asarray(inputs["L0_attn_x"], dtype=np.float32))
    B, S_, D_ = x0.shape
    TQ = (B * S_) // NCORES
    cores = list(range(NCORES))
    per_batch = NCORES // B

    h0 = l0_host_inputs(inputs["L0_attn_Wq"], inputs["L0_attn_bq"],
                        inputs["L0_attn_Wl"], inputs["L0_attn_bl"],
                        inputs["L0_attn_Wk"], inputs["L0_attn_bk"],
                        inputs["L0_attn_Wv"], inputs["L0_attn_bv"],
                        inputs["L0_attn_Wo"], inputs["L0_attn_bo"])
    hf0 = ffn_host_inputs(inputs["L0_ffn_w1_w"], inputs["L0_ffn_w1_s"],
                          inputs["L0_ffn_w2_w"], inputs["L0_ffn_w2_s"],
                          inputs["L0_ffn_h1_w"], inputs["L0_ffn_h1_s"],
                          inputs["L0_ffn_h2_w"], inputs["L0_ffn_h2_s"],
                          inputs["L0_ffn_w3_w"], inputs["L0_ffn_w3_s"])
    hf1 = ffn_host_inputs(inputs["L1_ffn_w1_w"], inputs["L1_ffn_w1_s"],
                          inputs["L1_ffn_w2_w"], inputs["L1_ffn_w2_s"],
                          inputs["L1_ffn_h1_w"], inputs["L1_ffn_h1_s"],
                          inputs["L1_ffn_h2_w"], inputs["L1_ffn_h2_s"],
                          inputs["L1_ffn_w3_w"], inputs["L1_ffn_w3_s"])
    h1 = l1_host_inputs(inputs["L1_attn_gamma"], inputs["L1_attn_w_q"],
                        inputs["L1_attn_w_kv"], inputs["L1_attn_w_out"])

    shared = {}
    for k in ["wq", "wl", "wk", "wv", "wo", "bq", "bl", "bk", "bv", "bo"]:
        shared["l0_" + k] = h0[k]
    for k, v in hf0.items():
        shared["f0_" + k] = v
    for k, v in hf1.items():
        shared["f1_" + k] = v
    shared["l1_kvw"] = h1["kvw"]
    shared["l1_gam"] = h1["gam"]
    shared["l1_qw"] = h1["qw"]
    shared["l1_wout"] = h1["wout"]

    nc = build_fused(S=S_, TQ=TQ, TC=512)
    in_maps = []
    for c in cores:
        b, qi = c // per_batch, c % per_batch
        m = dict(shared)
        m["xfull"] = x0[b]
        m["xq"] = np.ascontiguousarray(x0[b][qi * TQ:(qi + 1) * TQ])
        in_maps.append(m)
    res = run_bass_kernel_spmd(nc, in_maps, core_ids=cores)
    return np.concatenate([res.results[c]["xout"] for c in cores],
                          axis=0).reshape(B, S_, D_)



def kernel(**inputs):
    return kernel_fused(inputs)


def kernel_multilaunch(L0_attn_x, L0_attn_Wq, L0_attn_bq, L0_attn_Wl, L0_attn_bl,
           L0_attn_Wk, L0_attn_bk, L0_attn_Wv, L0_attn_bv, L0_attn_Wo,
           L0_attn_bo, L0_ffn_w1_w, L0_ffn_w1_s, L0_ffn_w2_w, L0_ffn_w2_s,
           L0_ffn_h1_w, L0_ffn_h1_s, L0_ffn_h2_w, L0_ffn_h2_s, L0_ffn_w3_w,
           L0_ffn_w3_s, L1_attn_gamma, L1_attn_w_q, L1_attn_w_kv,
           L1_attn_w_out, L1_ffn_w1_w, L1_ffn_w1_s, L1_ffn_w2_w, L1_ffn_w2_s,
           L1_ffn_h1_w, L1_ffn_h1_s, L1_ffn_h2_w, L1_ffn_h2_s, L1_ffn_w3_w,
           L1_ffn_w3_s):
    x0 = np.ascontiguousarray(np.asarray(L0_attn_x, dtype=np.float32))
    B, S_, D_ = x0.shape
    TQ = (B * S_) // NCORES
    cores = list(range(NCORES))
    per_batch = NCORES // B

    h0 = l0_host_inputs(L0_attn_Wq, L0_attn_bq, L0_attn_Wl, L0_attn_bl,
                        L0_attn_Wk, L0_attn_bk, L0_attn_Wv, L0_attn_bv,
                        L0_attn_Wo, L0_attn_bo)
    nc0 = build_l0(S=S_, TQ=TQ)
    in_maps = []
    for c in cores:
        b, qi = c // per_batch, c % per_batch
        m = dict(h0)
        m["xfull"] = x0[b]
        m["xq"] = np.ascontiguousarray(x0[b][qi * TQ:(qi + 1) * TQ])
        in_maps.append(m)
    res = run_bass_kernel_spmd(nc0, in_maps, core_ids=cores)
    x1 = [res.results[c]["xout"] for c in cores]

    hf0 = ffn_host_inputs(L0_ffn_w1_w, L0_ffn_w1_s, L0_ffn_w2_w, L0_ffn_w2_s,
                          L0_ffn_h1_w, L0_ffn_h1_s, L0_ffn_h2_w, L0_ffn_h2_s,
                          L0_ffn_w3_w, L0_ffn_w3_s)
    ncf = build_ffn(T=TQ, TC=512)
    in_maps = [dict(hf0, xin=x1[c]) for c in cores]
    res = run_bass_kernel_spmd(ncf, in_maps, core_ids=cores)
    x2 = [res.results[c]["xout"] for c in cores]

    h1 = l1_host_inputs(L1_attn_gamma, L1_attn_w_q, L1_attn_w_kv, L1_attn_w_out)
    nca = build_l1a(T=TQ)
    in_maps = [{"xin": x2[c], "kvw": h1["kvw"], "gam": h1["gam"]} for c in cores]
    res = run_bass_kernel_spmd(nca, in_maps, core_ids=cores)
    kvp = [res.results[c]["kvp"] for c in cores]

    ncb = build_l1b(T=TQ)
    in_maps = []
    for c in cores:
        b = c // per_batch
        kvp4 = np.ascontiguousarray(
            np.stack(kvp[b * per_batch:(b + 1) * per_batch], axis=0))
        in_maps.append({"xin": x2[c], "qw": h1["qw"], "gam": h1["gam"],
                        "wout": h1["wout"], "kvp4": kvp4})
    res = run_bass_kernel_spmd(ncb, in_maps, core_ids=cores)
    x3 = [res.results[c]["xout"] for c in cores]

    hf1 = ffn_host_inputs(L1_ffn_w1_w, L1_ffn_w1_s, L1_ffn_w2_w, L1_ffn_w2_s,
                          L1_ffn_h1_w, L1_ffn_h1_s, L1_ffn_h2_w, L1_ffn_h2_s,
                          L1_ffn_w3_w, L1_ffn_w3_s)
    in_maps = [dict(hf1, xin=x3[c]) for c in cores]
    res = run_bass_kernel_spmd(ncf, in_maps, core_ids=cores)
    x4 = [res.results[c]["xout"] for c in cores]

    return np.concatenate(x4, axis=0).reshape(B, S_, D_)
